# version 04
# baseline (speedup 1.0000x reference)
import numpy as np
import concourse.bass as bass
import concourse.mybir as mybir
from concourse.bass_utils import run_bass_kernel_spmd

F32 = mybir.dt.float32
BF16 = mybir.dt.bfloat16
I32 = mybir.dt.int32
AF = mybir.ActivationFunctionType
ALU = mybir.AluOpType
AX = mybir.AxisListType

D = 1024
OC = 8
T = 2048
TT = 16
T4 = 4
NH = 16
HD = 64
I0 = 2816
IC0 = 22
I1 = 3072
IC1 = 24
LN_EPS = 1e-5
ATT_EPS = 1e-6
RG_PAIR = [[0, 1], [2, 3], [4, 5], [6, 7]]
RG_ALL = [[0, 1, 2, 3, 4, 5, 6, 7]]

_UPTO = 99

ENGS = ["tensor", "vector", "scalar", "gpsimd", "sync"]
SEMS = ["qS", "qC", "qG", "eT", "eV", "eS", "eG", "cc", "x1", "x2", "x3"]


class MK:
    def __init__(self, nc, block):
        self.nc = nc
        self.block = block
        self.emit = True
        self.sems = {}
        self.ctr = {}
        self.waited = {}
        for s in SEMS:
            self.sems[s] = nc.alloc_semaphore(s)
            self.ctr[s] = 0

    def sig(self, inst, name, by=1):
        inst.then_inc(self.sems[name], by)
        self.ctr[name] += by
        return self.ctr[name]

    def wait(self, eng, engname, name, val=None):
        if val is None:
            val = self.ctr[name]
        if val <= 0:
            return
        key = (engname, name)
        if self.waited.get(key, 0) >= val:
            return
        eng.wait_ge(self.sems[name], val)
        self.waited[key] = val

    def on(self, engname):
        def run(f):
            if not self.emit:
                return

            def wrapped(e):
                n0 = self.nc.n_instructions()
                f(e, engname)
                if self.nc.n_instructions() == n0:
                    e.nop()
            getattr(self.block, engname)(wrapped)
        return run

    def gate(self):
        if not self.emit:
            return
        snap = dict(self.ctr)
        for en in ENGS:
            @self.on(en)
            def _(e, en):
                for s in SEMS:
                    self.wait(e, en, s, snap[s])


def _build(upto=4):
    nc = bass.Bass("TRN2", target_bir_lowering=False, debug=False,
                   detect_race_conditions=False)

    xT = nc.dram_tensor("xT", [D, T], F32, kind="ExternalInput")
    out = nc.dram_tensor("out", [D, T], F32, kind="ExternalOutput")
    wqt0 = nc.dram_tensor("wqt0", [D, D], F32, kind="ExternalInput")
    wkt0 = nc.dram_tensor("wkt0", [D, D], F32, kind="ExternalInput")
    wvt0 = nc.dram_tensor("wvt0", [D, D], F32, kind="ExternalInput")
    wot0 = nc.dram_tensor("wot0", [D, D], F32, kind="ExternalInput")
    bq0 = nc.dram_tensor("bq0", [128, 8], F32, kind="ExternalInput")
    bk0r = nc.dram_tensor("bk0r", [1, D], F32, kind="ExternalInput")
    bv0r = nc.dram_tensor("bv0r", [1, D], F32, kind="ExternalInput")
    bo0 = nc.dram_tensor("bo0", [128, 8], F32, kind="ExternalInput")
    cx0 = nc.dram_tensor("cx0", [1, 1], F32, kind="ExternalInput")
    cd0 = nc.dram_tensor("cd0", [1, 1], F32, kind="ExternalInput")
    wg0 = nc.dram_tensor("wg0", [I0, D], F32, kind="ExternalInput")
    wu0 = nc.dram_tensor("wu0", [I0, D], F32, kind="ExternalInput")
    wd0 = nc.dram_tensor("wd0", [D, I0], F32, kind="ExternalInput")
    wqt1 = nc.dram_tensor("wqt1", [D, D], F32, kind="ExternalInput")
    wkt1 = nc.dram_tensor("wkt1", [D, D], F32, kind="ExternalInput")
    wvt1 = nc.dram_tensor("wvt1", [D, D], F32, kind="ExternalInput")
    wut1 = nc.dram_tensor("wut1", [D, D], F32, kind="ExternalInput")
    bu1 = nc.dram_tensor("bu1", [128, 8], F32, kind="ExternalInput")
    wg1 = nc.dram_tensor("wg1", [I1, D], F32, kind="ExternalInput")
    wup1 = nc.dram_tensor("wup1", [I1, D], F32, kind="ExternalInput")
    wd1 = nc.dram_tensor("wd1", [D, I1], F32, kind="ExternalInput")

    ctx_bin = nc.dram_tensor("ctx_bin", [128, 8 * 65], F32)
    ctx_bout = nc.dram_tensor("ctx_bout", [128, 8 * 65], F32)
    amax_in = [nc.dram_tensor(f"amax_in{i}", [1, 8], F32) for i in range(8)]
    amax_out = [nc.dram_tensor(f"amax_out{i}", [1, 8], F32) for i in range(8)]
    w8g_d = nc.dram_tensor("w8g_d", [I0, D], BF16)
    w8u_d = nc.dram_tensor("w8u_d", [I0, D], BF16)
    w8d_d = nc.dram_tensor("w8d_d", [D, I0], BF16)
    wq1g = nc.dram_tensor("wq1g", [I1, D], BF16)
    wq1u = nc.dram_tensor("wq1u", [I1, D], BF16)
    wq1d = nc.dram_tensor("wq1d", [D, I1], BF16)
    hpre_d = nc.dram_tensor("hpre_d", [IC0, 128, T], F32)
    kag_in = nc.dram_tensor("kag_in", [D, T], BF16)
    kag_out = nc.dram_tensor("kag_out", [2 * D, T], BF16)
    vag_in = nc.dram_tensor("vag_in", [NH, T, HD], BF16)
    vag_out = nc.dram_tensor("vag_out", [2, NH, T, HD], BF16)
    xspill = nc.dram_tensor("xspill", [D, T], F32)
    graw_d = nc.dram_tensor("graw_d", [IC1, 128, T], BF16)
    uraw_d = nc.dram_tensor("uraw_d", [IC1, 128, T], BF16)
    gs_d = nc.dram_tensor("gs_d", [IC1, 128, T], BF16)
    opre_d = nc.dram_tensor("opre_d", [IC1, 128, T], BF16)

    from contextlib import ExitStack
    es = ExitStack()
    sbp = lambda name, shape, dt: es.enter_context(nc.sbuf_tensor(name, shape, dt))

    ones_blk = sbp("ones_blk", [128, 128], BF16)
    identf = sbp("identf", [128, 128], F32)
    ones_r1 = sbp("ones_r1", [1, 128], BF16)
    ones_r1f = sbp("ones_r1f", [1, 128], F32)
    eps_r512 = sbp("eps_r512", [1, 512], BF16)
    epsln_col = sbp("epsln_col", [128, 1], F32)
    blockmask = sbp("blockmask", [128, 128], BF16)
    bq0_sb = sbp("bq0_sb", [128, 8], F32)
    bo0_sb = sbp("bo0_sb", [128, 8], F32)
    bu1_sb = sbp("bu1_sb", [128, 8], F32)
    bkv_sb = sbp("bkv_sb", [1, 2 * D], BF16)
    sig_cx = sbp("sig_cx", [1, 1], F32)
    sig_cd = sbp("sig_cd", [1, 1], F32)
    row16 = sbp("row16", [1, 64], F32)
    scal = sbp("scal", [128, 8], F32)
    wsg_sb = sbp("wsg_sb", [128, IC0], F32)
    wsu_sb = sbp("wsu_sb", [128, IC0], F32)
    wsd_sb = sbp("wsd_sb", [128, OC], F32)
    csg_sb = sbp("csg_sb", [128, IC0], F32)
    csu_sb = sbp("csu_sb", [128, IC0], F32)
    csd_sb = sbp("csd_sb", [128, OC], F32)
    maxc_sb = sbp("maxc_sb", [128, 36], F32)

    def _align(v, a=64):
        return (v + a - 1) // a * a

    MB = _align(nc.sbuf_base)
    A0 = MB
    B0 = A0 + 65536
    C0 = B0 + 32768
    D0 = C0 + 32768
    E0 = D0 + 16384
    ETOP = nc.sbuf_top
    assert ETOP - E0 >= 50 * 1024, f"E region too small: {ETOP - E0}"

    def at(off, name, shape, dt):
        h = nc.alloc_sbuf_tensor_at(name, shape, dt, offset=off)
        return h

    stream = at(A0, "stream", [128, OC, T], F32)
    w2 = at(A0, "w2", [128, 2, OC, D], BF16)
    kv_t = at(A0 + 32768, "kv_t", [128, 2, 2, NH * 65], BF16)
    scr2 = at(A0 + 41120, "scr2", [128, 2, D], BF16)
    xbf = at(B0, "xbf", [128, OC, T], BF16)
    qbf = at(C0, "qbf", [128, OC, T], BF16)
    wo_sb = at(D0, "wo_sb", [128, OC, D], BF16)
    ctx_sb = at(E0, "ctx_sb", [128, 8, 65], F32)
    att_o = at(E0 + 2112, "att_o", [128, OC, 512], BF16)
    mtile = at(E0 + 10304, "mtile", [128, 2, 128], BF16)
    ctx_bf = at(E0 + 10816, "ctx_bf", [128, 8, 64], BF16)
    scr = at(E0 + 11840, "scr", [128, 4, 512], F32)
    dinvs = at(E0 + 11840, "dinvs", [128, 2, 512], BF16)
    r_bc = at(E0 + 20032, "r_bc", [128, 2, 512], F32)
    ybf = at(E0 + 24128, "ybf", [128, OC, 512], BF16)
    y2bf = at(E0 + 32320, "y2bf", [128, OC, 512], BF16)
    xres = at(E0 + 40512, "xres", [128, OC, 512], F32)
    assert E0 + 40512 + 16384 <= ETOP, (E0, ETOP)
    p2a = at(E0 + 40512, "p2a", [128, 2, T], F32)
    p2b = at(E0, "p2b", [128, T], I32)
    p2bh = at(E0, "p2bh", [128, 2, T], BF16)
    xq = at(B0, "xq", [128, OC, T], BF16)
    hintC = at(C0, "hintC", [128, IC0, 512], BF16)
    wrawA = at(D0, "wrawA", [128, 2, D], F32)
    w8st = at(D0 + 8192, "w8st", [128, 2, D], BF16)
    wTgu = at(D0, "wTgu", [128, 2, OC, 512], BF16)
    wTd = at(D0, "wTd", [128, 2, 512], BF16)
    wrawD = at(D0, "wrawD", [128, I0], F32)
    p3att = at(A0, "p3att", [128, OC, T], BF16)
    p4i = at(E0 + 24128, "p4i", [128, 3072], I32)
    p4h = at(C0 + 8192, "p4h", [128, IC1, 512], BF16)
    p4st = at(C0, "p4st", [128, 2, T], BF16)
    p3k = at(A0 + 32768, "p3k", [128, OC, T], BF16)
    kh = at(A0 + 32768, "kh", [128, 2, T], BF16)
    vh = at(A0 + 40960, "vh", [128, 32, 65], BF16)
    pt = at(A0 + 49344, "pt", [128, 2, 1024], BF16)
    strip = at(A0 + 53440, "strip", [64, T], BF16)
    vstg = at(A0 + 57536, "vstg", [128, 2, D], BF16)
    p2af = at(E0 + 40512, "p2af", [128, 2 * T], F32)
    p2i2 = at(E0 + 24128, "p2i2", [128, I0], I32)
    p2bh4 = at(E0, "p2bh4", [128, 2 * T], BF16)

    with nc.allow_low_precision("bf16 matmul operands / intermediates; tolerance is gated by the harness"), \
         nc.Block() as block:
        mk = MK(nc, block)
        C = mk.ctr
        
        def emit_armax(idx, src_col, out_col):
            with nc.psum_tensor(f"amx_ps{idx}", [1, 128], F32) as aps:
                @mk.on("tensor")
                def _(e, en):
                    for s in ("eV", "eS"):
                        mk.wait(e, en, s)
                    mk.sig(e.transpose(aps[0:1, :], src_col, identf[:]), "eT")

                @mk.on("vector")
                def _(e, en):
                    mk.wait(e, en, "eT")
                    c = mk.sig(e.reduce_max(out=row16[:, idx * 8:idx * 8 + 1],
                                            in_=aps[0:1, :], axis=AX.X), "eV")
                    mk.wait(e, en, "eV", c)
                    mk.sig(e.tensor_copy(out=row16[:, idx * 8:idx * 8 + 8],
                                         in_=row16[:, idx * 8:idx * 8 + 1].broadcast_to((1, 8))), "eV")

                @mk.on("gpsimd")
                def _(e, en):
                    mk.wait(e, en, "eV")
                    mk.sig(e.dma_start(out=amax_in[idx].ap(), in_=row16[:, idx * 8:idx * 8 + 8]), "qG", 16)
                    mk.wait(e, en, "qG")
                    mk.sig(e.collective_compute(
                        "AllReduce", ALU.max, replica_groups=RG_ALL,
                        ins=[amax_in[idx].ap().opt()], outs=[amax_out[idx].ap().opt()]), "cc")
                    mk.wait(e, en, "cc")
                    mk.sig(e.dma_start(out=row16[:, idx * 8:idx * 8 + 8], in_=amax_out[idx].ap()), "qG", 16)
                mk.gate()

        def emit_bcast(val_ap, slot, ps_ap):
            @mk.on("tensor")
            def _(e, en):
                mk.wait(e, en, "eV")
                mk.wait(e, en, "qG")
                mk.sig(e.matmul(ps_ap, ones_r1f[:, 0:128], val_ap, start=True, stop=True), "eT")

            @mk.on("vector")
            def _(e, en):
                mk.wait(e, en, "eT")
                mk.sig(e.tensor_copy(out=scal[:, slot:slot + 1], in_=ps_ap), "eV")
            mk.gate()

        @mk.on("sync")
        def _(e, en):
            mk.sig(e.dma_start(out=bq0_sb[:], in_=bq0.ap()), "qS", 16)
            mk.sig(e.dma_start(out=bo0_sb[:], in_=bo0.ap()), "qS", 16)
            mk.sig(e.dma_start(out=bu1_sb[:], in_=bu1.ap()), "qS", 16)
            mk.sig(e.dma_start(out=sig_cx[:], in_=cx0.ap()), "qS", 16)
            mk.sig(e.dma_start(out=sig_cd[:], in_=cd0.ap()), "qS", 16)

        @mk.on("gpsimd")
        def _(e, en):
            mk.sig(e.dma_start(out=bkv_sb[:, 0:D], in_=bk0r.ap()), "qG", 16)
            mk.sig(e.dma_start(out=bkv_sb[:, D:2 * D], in_=bv0r.ap()), "qG", 16)
            for oc in range(OC):
                mk.sig(e.dma_start(out=xbf[:, oc, :],
                                   in_=xT.ap()[oc * 128:(oc + 1) * 128, :]), "qG", 16)
            for fc in range(OC):
                mk.sig(e.dma_start(out=w2[:, 0, fc, :],
                                   in_=wqt0.ap()[fc * 128:(fc + 1) * 128, :]), "qG", 16)
        P0_QG = C["qG"]

        @mk.on("vector")
        def _(e, en):
            e.memset(ones_blk[:], 1.0)
            e.memset(identf[:], 1.0)
            e.memset(ones_r1[:], 1.0)
            e.memset(ones_r1f[:], 1.0)
            e.memset(eps_r512[:], ATT_EPS)
            e.memset(epsln_col[:], LN_EPS)
            e.memset(blockmask[:], 0.0)
            e.memset(blockmask[0:64, 0:64], 1.0)
            mk.sig(e.memset(blockmask[64:128, 64:128], 1.0), "eV")

        @mk.on("gpsimd")
        def _(e, en):
            mk.wait(e, en, "eV")
            mk.sig(e.affine_select(identf[:], identf[:], [[1, 128]], ALU.is_equal, 0.0,
                                   base=0, channel_multiplier=-1), "eG")

        @mk.on("scalar")
        def _(e, en):
            mk.wait(e, en, "qS")
            e.activation(out=sig_cx[:], in_=sig_cx[:], func=AF.Sigmoid)
            mk.sig(e.activation(out=sig_cd[:], in_=sig_cd[:], func=AF.Sigmoid), "eS")

        mk.gate()

        mk.emit = (upto >= 1)
        with nc.psum_tensor("q_ps", [128, 2, 512], F32) as q_ps:
            NQT = OC * T4
            qT0, qV0, qS0 = C["eT"], C["eV"], C["eS"]

            @mk.on("tensor")
            def _(e, en):
                mk.wait(e, en, "qG", P0_QG)
                for i in range(NQT):
                    oc, t4 = divmod(i, T4)
                    r = i % 2
                    if i >= 2:
                        mk.wait(e, en, "eV", qV0 + (i - 1) * 2)
                    for fc in range(OC):
                        mm = e.matmul(q_ps[:, r, :],
                                      w2[:, 0, fc, oc * 128:(oc + 1) * 128],
                                      xbf[:, fc, t4 * 512:(t4 + 1) * 512],
                                      start=(fc == 0), stop=(fc == OC - 1))
                    mk.sig(mm, "eT")

            @mk.on("scalar")
            def _(e, en):
                for i in range(NQT):
                    oc, t4 = divmod(i, T4)
                    r = i % 2
                    mk.wait(e, en, "eT", qT0 + i + 1)
                    if i >= 1:
                        mk.wait(e, en, "eV", qV0 + i * 2)
                    mk.sig(e.activation(out=scr[:, 2, :], in_=q_ps[:, r, :], func=AF.Relu,
                                        bias=bq0_sb[:, oc:oc + 1]), "eS")
                    mk.wait(e, en, "eV", qV0 + i * 2 + 1)
                    mk.sig(e.activation(out=scr[:, 1, :], in_=scr[:, 0, :], func=AF.Exp), "eS")

            @mk.on("vector")
            def _(e, en):
                for i in range(NQT):
                    oc, t4 = divmod(i, T4)
                    r = i % 2
                    mk.wait(e, en, "eT", qT0 + i + 1)
                    mk.wait(e, en, "eS", qS0 + i * 2 + 1)
                    mk.sig(e.tensor_scalar(out=scr[:, 0, :], in0=q_ps[:, r, :],
                                           scalar1=bq0_sb[:, oc:oc + 1], scalar2=0.0,
                                           op0=ALU.add, op1=ALU.min), "eV")
                    mk.wait(e, en, "eS", qS0 + (i + 1) * 2)
                    mk.sig(e.tensor_add(out=qbf[:, oc, t4 * 512:(t4 + 1) * 512],
                                        in0=scr[:, 1, :], in1=scr[:, 2, :]), "eV")
            assert not mk.emit or (C["eV"] == qV0 + NQT * 2 and C["eS"] == qS0 + NQT * 2)
            mk.gate()

        mk.emit = (upto >= 2)
        @mk.on("gpsimd")
        def _(e, en):
            for j, w in enumerate([wkt0, wvt0]):
                for fc in range(OC):
                    mk.sig(e.dma_start(out=w2[:, j, fc, :],
                                       in_=w.ap()[fc * 128:(fc + 1) * 128, :]), "qG", 16)
            for fc in range(OC):
                mk.sig(e.dma_start(out=wo_sb[:, fc, :],
                                   in_=wot0.ap()[fc * 128:(fc + 1) * 128, :]), "qG", 16)
        P1_WKV = C["qG"]

        with nc.psum_tensor("kv_ps", [128, 2, D], F32) as kv_ps, \
             nc.psum_tensor("ctx_ps", [128, 2, 512], F32) as ctx_ps:

            @mk.on("vector")
            def _(e, en):
                e.memset(ctx_sb[:, :, :].rearrange("p a b -> p (a b)"), 0.0)
                for r in range(2):
                    mk.sig(e.memset(
                        kv_t[:, r, 1, :].rearrange("p (h e) -> p h e", e=65)[:, :, 64:65],
                        1.0), "eV")
            KV1_EV = C["eV"]
            kT0, kS0 = C["eT"], C["eS"]
            EVTT, ESTT, ETTT = 3, 2, 3

            @mk.on("tensor")
            def _(e, en):
                mk.wait(e, en, "eV", KV1_EV)
                mk.wait(e, en, "qG", P1_WKV)
                for tt in range(TT):
                    r = tt % 2
                    if tt >= 2:
                        mk.wait(e, en, "eV", KV1_EV + (tt - 1) * EVTT)
                    for j in (0, 1):
                        dst = kv_ps[:, j, :]
                        for fc in range(OC):
                            for o2 in range(2):
                                e.matmul(dst[:, o2 * 512:(o2 + 1) * 512],
                                         xbf[:, fc, tt * 128:(tt + 1) * 128],
                                         w2[:, j, fc, o2 * 512:(o2 + 1) * 512],
                                         start=(fc == 0), stop=False)
                        for o2 in range(2):
                            mm = e.matmul(dst[:, o2 * 512:(o2 + 1) * 512],
                                          ones_r1[:, 0:128],
                                          bkv_sb[:, j * D + o2 * 512:j * D + (o2 + 1) * 512],
                                          start=False, stop=True)
                        mk.sig(mm, "eT")
                    mk.wait(e, en, "eV", KV1_EV + (tt + 1) * EVTT)
                    if tt >= 1:
                        mk.wait(e, en, "x1", tt)
                    for h in range(NH):
                        hp, sub = divmod(h, 2)
                        bank = hp // 4
                        col = (hp % 4) * 65
                        mm = e.matmul(ctx_ps[sub * 64:(sub + 1) * 64, bank, col:col + 65],
                                      kv_t[:, r, 0, h * 64:(h + 1) * 64],
                                      kv_t[:, r, 1, h * 65:(h + 1) * 65],
                                      start=True, stop=True,
                                      tile_position=(0, 64 * sub))
                    mk.sig(mm, "eT")

            @mk.on("scalar")
            def _(e, en):
                for tt in range(TT):
                    mk.wait(e, en, "eT", kT0 + tt * ETTT + 2)
                    if tt >= 1:
                        mk.wait(e, en, "eV", KV1_EV + (tt - 1) * EVTT + 2)
                    mk.sig(e.activation(out=scr2[:, 1, :], in_=kv_ps[:, 0, :],
                                        func=AF.Relu), "eS")
                    mk.wait(e, en, "eV", KV1_EV + tt * EVTT + 1)
                    mk.sig(e.activation(out=scr2[:, 0, :], in_=scr2[:, 0, :],
                                        func=AF.Exp), "eS")

            @mk.on("vector")
            def _(e, en):
                for tt in range(TT):
                    r = tt % 2
                    mk.wait(e, en, "eT", kT0 + tt * ETTT + 2)
                    mk.wait(e, en, "eS", kS0 + tt * ESTT + 1)
                    mk.sig(e.tensor_scalar_min(out=scr2[:, 0, :], in0=kv_ps[:, 0, :],
                                               scalar1=0.0), "eV")
                    mk.wait(e, en, "eS", kS0 + (tt + 1) * ESTT)
                    mk.sig(e.tensor_add(out=kv_t[:, r, 0, 0:D],
                                        in0=scr2[:, 0, :], in1=scr2[:, 1, :]), "eV")
                    mk.sig(e.tensor_copy(
                        out=kv_t[:, r, 1, :].rearrange("p (h e) -> p h e", e=65)[:, :, 0:64],
                        in_=kv_ps[:, 1, :].rearrange("p (h e) -> p h e", e=64)), "eV")
                    if tt >= 1:
                        mk.wait(e, en, "eT", kT0 + tt * ETTT)
                        cv = ctx_sb[:, :, :].rearrange("p (g h) e -> p g (h e)", g=2)
                        c1 = mk.sig(e.tensor_add(
                            out=cv, in0=cv, in1=ctx_ps[:, :, 0:260]), "x1")
                        e.wait_ge(mk.sems["x1"], c1)
                mk.wait(e, en, "eT", kT0 + TT * ETTT)
                cv = ctx_sb[:, :, :].rearrange("p (g h) e -> p g (h e)", g=2)
                mk.sig(e.tensor_add(out=cv, in0=cv, in1=ctx_ps[:, :, 0:260]), "x1")
            assert not mk.emit or (C["eV"] == KV1_EV + TT * EVTT and C["x1"] == TT)

            mk.gate()

            mk.emit = (upto >= 3)
            @mk.on("gpsimd")
            def _(e, en):
                mk.wait(e, en, "eV")
                mk.sig(e.dma_start(out=ctx_bin.ap(),
                                   in_=ctx_sb[:, :, :].rearrange("p a b -> p (a b)")), "qG", 16)
                mk.wait(e, en, "qG")
                mk.sig(e.collective_compute(
                    "AllReduce", ALU.add, replica_groups=RG_PAIR,
                    ins=[ctx_bin.ap().opt()], outs=[ctx_bout.ap().opt()]), "cc")
                mk.wait(e, en, "cc")
                mk.sig(e.dma_start(out=ctx_sb[:, :, :].rearrange("p a b -> p (a b)"),
                                   in_=ctx_bout.ap()), "qG", 16)

            @mk.on("vector")
            def _(e, en):
                mk.wait(e, en, "qG")
                mk.sig(e.tensor_copy(out=ctx_bf[:, :, :], in_=ctx_sb[:, :, 0:64]), "eV")
            mk.gate()

        mk.emit = (upto >= 4)
        with nc.psum_tensor("num_ps", [128, 2, 512], F32) as num_ps, \
             nc.psum_tensor("den_ps", [128, 2, 512], F32) as den_ps, \
             nc.psum_tensor("wo_ps", [128, 2, 512], F32) as wo_ps, \
             nc.psum_tensor("st_ps", [128, 2, 512], F32) as st_ps:

            for t4 in range(T4):
                t4T0, t4V0 = C["eT"], C["eV"]

                @mk.on("sync")
                def _(e, en, t4=t4, t4V0=t4V0):
                    if t4 >= 1:
                        mk.wait(e, en, "eV", t4V0)
                    for oc in range(OC):
                        mk.sig(e.dma_start(out=xres[:, oc, :],
                                           in_=xT.ap()[oc * 128:(oc + 1) * 128,
                                                       t4 * 512:(t4 + 1) * 512]), "qS", 16)
                XRES_QS = C["qS"]

                @mk.on("vector")
                def _(e, en, t4=t4, t4T0=t4T0):
                    for hp in range(8):
                        rr = hp % 2
                        mk.wait(e, en, "eT", t4T0 + (hp - 1) * 2 if hp >= 2 else t4T0)
                        mk.sig(e.tensor_scalar_mul(out=mtile[:, rr, :], in0=blockmask[:],
                                                   scalar1=ctx_sb[:, hp, 64:65]), "eV")
                        mk.wait(e, en, "eT", t4T0 + (hp + 1) * 2)
                        mk.sig(e.reciprocal(out=dinvs[:, rr, :],
                                            in_=den_ps[:, rr, :]), "eV")
                        mk.sig(e.tensor_mul(out=att_o[:, hp, :],
                                            in0=num_ps[:, rr, :], in1=dinvs[:, rr, :]), "eV")

                @mk.on("tensor")
                def _(e, en, t4=t4, t4V0=t4V0):
                    for hp in range(8):
                        rr = hp % 2
                        mk.wait(e, en, "eV", t4V0 + hp * 3 + 1)
                        for sub in range(2):
                            mm = e.matmul(num_ps[sub * 64:(sub + 1) * 64, rr, :],
                                          ctx_bf[sub * 64:(sub + 1) * 64, hp, :],
                                          qbf[sub * 64:(sub + 1) * 64, hp,
                                              t4 * 512:(t4 + 1) * 512],
                                          start=True, stop=True,
                                          tile_position=(64 * sub, 64 * sub))
                        mk.sig(mm, "eT")
                        e.matmul(den_ps[:, rr, :], mtile[:, rr, :],
                                 qbf[:, hp, t4 * 512:(t4 + 1) * 512],
                                 start=True, stop=False)
                        mm = e.matmul(den_ps[:, rr, :], ones_r1[:, 0:128],
                                      eps_r512[:], start=False, stop=True)
                        mk.sig(mm, "eT")
                ATT_EV = t4V0 + 24

                woT0, woV0, woS0 = C["eT"], C["eV"], C["eS"]

                @mk.on("tensor")
                def _(e, en, t4=t4, woV0=woV0, woS0=woS0, ATT_EV=ATT_EV):
                    mk.wait(e, en, "eV", ATT_EV)
                    for oc in range(OC):
                        rr = oc % 2
                        if oc >= 2:
                            mk.wait(e, en, "eV", woV0 + (oc - 1) * 2)
                        for hp in range(8):
                            mm = e.matmul(wo_ps[:, rr, :],
                                          wo_sb[:, hp, oc * 128:(oc + 1) * 128],
                                          att_o[:, hp, :],
                                          start=(hp == 0), stop=(hp == 7))
                        mk.sig(mm, "eT")
                    for oc in range(OC):
                        mk.wait(e, en, "eV", woV0 + oc * 2 + 2)
                        mm = e.matmul(st_ps[:, 0, :], ones_blk[:], ybf[:, oc, :],
                                      start=(oc == 0), stop=(oc == OC - 1))
                    mk.sig(mm, "eT")
                    for oc in range(OC):
                        mk.wait(e, en, "eS", woS0 + oc + 1)
                        mm = e.matmul(st_ps[:, 1, :], ones_blk[:], y2bf[:, oc, :],
                                      start=(oc == 0), stop=(oc == OC - 1))
                    mk.sig(mm, "eT")

                @mk.on("vector")
                def _(e, en, t4=t4, woT0=woT0, XRES_QS=XRES_QS):
                    mk.wait(e, en, "qS", XRES_QS)
                    for oc in range(OC):
                        rr = oc % 2
                        mk.wait(e, en, "eT", woT0 + oc + 1)
                        mk.sig(e.scalar_tensor_tensor(
                            out=stream[:, oc, t4 * 512:(t4 + 1) * 512],
                            in0=wo_ps[:, rr, :], scalar=bo0_sb[:, oc:oc + 1],
                            in1=xres[:, oc, :],
                            op0=ALU.add, op1=ALU.add), "eV")
                        mk.sig(e.tensor_copy(out=ybf[:, oc, :],
                                             in_=stream[:, oc, t4 * 512:(t4 + 1) * 512]), "eV")

                @mk.on("scalar")
                def _(e, en, t4=t4, woV0=woV0):
                    for oc in range(OC):
                        mk.wait(e, en, "eV", woV0 + oc * 2 + 2)
                        mk.sig(e.activation(out=y2bf[:, oc, :], in_=ybf[:, oc, :],
                                            func=AF.Square), "eS")

                lnT = C["eT"]

                @mk.on("vector")
                def _(e, en, t4=t4, lnT=lnT):
                    mk.wait(e, en, "eT", lnT)
                    c1 = mk.sig(e.tensor_copy(out=scr[:, 0, :], in_=st_ps[:, 0, :]), "eV")
                    mk.wait(e, en, "eV", c1)
                    c2 = mk.sig(e.scalar_tensor_tensor(out=scr[:, 1, :], in0=scr[:, 0, :],
                                                       scalar=1.0 / (D * D), in1=scr[:, 0, :],
                                                       op0=ALU.mult, op1=ALU.mult), "eV")
                    mk.wait(e, en, "eV", c2)
                    mk.sig(e.scalar_tensor_tensor(out=scr[:, 2, :], in0=st_ps[:, 1, :],
                                                  scalar=1.0 / D, in1=scr[:, 1, :],
                                                  op0=ALU.mult, op1=ALU.subtract), "eV")
                VE_EV = C["eV"]

                @mk.on("scalar")
                def _(e, en, t4=t4, VE_EV=VE_EV):
                    mk.wait(e, en, "eV", VE_EV)
                    mk.sig(e.activation(out=r_bc[:, 0, :], in_=scr[:, 2, :], func=AF.Sqrt,
                                        bias=epsln_col[:, 0:1]), "eS")
                SQ_ES = C["eS"]

                @mk.on("vector")
                def _(e, en, t4=t4, SQ_ES=SQ_ES):
                    mk.wait(e, en, "eS", SQ_ES)
                    c1 = mk.sig(e.reciprocal(out=r_bc[:, 1, :], in_=r_bc[:, 0, :]), "eV")
                    mk.wait(e, en, "eV", c1)
                    c2 = mk.sig(e.scalar_tensor_tensor(out=scr[:, 3, :], in0=scr[:, 0, :],
                                                       scalar=1.0 / D, in1=r_bc[:, 1, :],
                                                       op0=ALU.mult, op1=ALU.mult), "eV")
                    mk.wait(e, en, "eV", c2)
                    for oc in range(OC):
                        sl = stream[:, oc, t4 * 512:(t4 + 1) * 512]
                        c3 = mk.sig(e.tensor_mul(out=scr[:, 1, :], in0=sl,
                                                 in1=r_bc[:, 1, :]), "eV")
                        mk.wait(e, en, "eV", c3)
                        c4 = mk.sig(e.tensor_sub(out=sl, in0=scr[:, 1, :],
                                                 in1=scr[:, 3, :]), "eV")
                        mk.wait(e, en, "eV", c4)
                        mk.sig(e.reduce_max(out=maxc_sb[:, t4 * OC + oc:t4 * OC + oc + 1],
                                            in_=sl, axis=AX.X,
                                            apply_absolute_value=True), "eV")
            mk.gate()

        mk.emit = (upto >= 5)

        @mk.on("vector")
        def _(e, en):
            c1 = mk.sig(e.reduce_max(out=maxc_sb[:, 32:33], in_=maxc_sb[:, 0:32],
                                     axis=AX.X), "eV")
            mk.wait(e, en, "eV", c1)
        emit_armax(0, maxc_sb[:, 32:33], maxc_sb[:, 33:34])

        mk.emit = (upto >= 6)

        with nc.psum_tensor("bc_ps", [128, 1], F32) as bc_ps:
            @mk.on("vector")
            def _(e, en):
                mk.wait(e, en, "qG")
                c = mk.sig(e.tensor_mul(out=row16[:, 56:57], in0=row16[:, 0:1], in1=sig_cx[:]), "eV")
                mk.wait(e, en, "eV", c)
                c = mk.sig(e.tensor_scalar_mul(out=row16[:, 57:58], in0=row16[:, 56:57],
                                               scalar1=1.0 / 127.0), "eV")
                mk.wait(e, en, "eV", c)
                c = mk.sig(e.reciprocal(out=row16[:, 58:59], in_=row16[:, 57:58]), "eV")
                mk.wait(e, en, "eV", c)
            emit_bcast(row16[:, 57:58], 0, bc_ps[:, 0:1])
            emit_bcast(row16[:, 58:59], 1, bc_ps[:, 0:1])

            def prep_weight(raw, dst, ws, nchunk, cols, halves=1):
                for ic in range(nchunk):
                    r = ic % 2

                    @mk.on("sync")
                    def _(e, en, ic=ic, r=r):
                        mk.wait(e, en, "qC")
                        for hf in range(halves):
                            mk.sig(e.dma_start(
                                out=wrawA[:, r, hf * cols:(hf + 1) * cols],
                                in_=raw.ap()[ic * 128:(ic + 1) * 128,
                                             hf * cols:(hf + 1) * cols]), "qS", 16)

                    @mk.on("vector")
                    def _(e, en, ic=ic, r=r):
                        mk.wait(e, en, "qS")
                        c = mk.sig(e.reduce_max(out=ws[:, ic:ic + 1],
                                                in_=wrawA[:, r, 0:halves * cols],
                                                axis=AX.X, apply_absolute_value=True), "eV")
                        mk.wait(e, en, "eV", c)
                        c = mk.sig(e.reciprocal(out=scal[:, 2:3], in_=ws[:, ic:ic + 1]), "eV")
                        mk.wait(e, en, "eV", c)
                        c = mk.sig(e.tensor_scalar_mul(out=scal[:, 3:4], in0=scal[:, 2:3],
                                                       scalar1=127.0), "eV")
                        mk.wait(e, en, "eV", c)
                        c = mk.sig(e.tensor_scalar_mul(out=wrawA[:, r, 0:halves * cols],
                                                       in0=wrawA[:, r, 0:halves * cols],
                                                       scalar1=scal[:, 3:4]), "eV")
                        mk.wait(e, en, "eV", c)

                    @mk.on("scalar")
                    def _(e, en, ic=ic, r=r):
                        mk.wait(e, en, "qS")
                        mk.sig(e.activation(out=p2a[:, 0, 0:halves * cols],
                                            in_=raw.ap() if False else wrawA[:, r, 0:halves * cols],
                                            func=AF.Sign), "eS")

                    @mk.on("vector")
                    def _(e, en, ic=ic, r=r):
                        mk.wait(e, en, "eS")
                        c = mk.sig(e.scalar_tensor_tensor(
                            out=p2b[:, 0:halves * cols].bitcast(I32) if False else p2b[:, 0:halves * cols],
                            in0=p2a[:, 0, 0:halves * cols], scalar=0.0,
                            in1=wrawA[:, r, 0:halves * cols],
                            op0=ALU.mult, op1=ALU.add), "eV")
                        mk.wait(e, en, "eV", c)
                        c = mk.sig(e.tensor_copy(out=w8st[:, r, 0:halves * cols],
                                                 in_=p2b[:, 0:halves * cols]), "eV")
                        mk.wait(e, en, "eV", c)

                    @mk.on("scalar")
                    def _(e, en, ic=ic, r=r):
                        mk.wait(e, en, "eV")
                        for hf in range(halves):
                            mk.sig(e.dma_start(
                                out=dst.ap()[ic * 128:(ic + 1) * 128,
                                             hf * cols:(hf + 1) * cols],
                                in_=w8st[:, r, hf * cols:(hf + 1) * cols]), "qC", 16)

            prep_weight(wg0, w8g_d, wsg_sb, IC0, D, 1)
            prep_weight(wu0, w8u_d, wsu_sb, IC0, D, 1)
            mk.gate()

            for oc in range(OC):
                @mk.on("sync")
                def _(e, en, oc=oc):
                    mk.wait(e, en, "qC")
                    mk.wait(e, en, "eV")
                    mk.sig(e.dma_start(out=wrawD[:, :],
                                       in_=wd0.ap()[oc * 128:(oc + 1) * 128, :]), "qS", 16)

                @mk.on("vector")
                def _(e, en, oc=oc):
                    mk.wait(e, en, "qS")
                    c = mk.sig(e.reduce_max(out=wsd_sb[:, oc:oc + 1], in_=wrawD[:, :],
                                            axis=AX.X, apply_absolute_value=True), "eV")
                    mk.wait(e, en, "eV", c)
                    c = mk.sig(e.reciprocal(out=scal[:, 2:3], in_=wsd_sb[:, oc:oc + 1]), "eV")
                    mk.wait(e, en, "eV", c)
                    c = mk.sig(e.tensor_scalar_mul(out=scal[:, 3:4], in0=scal[:, 2:3],
                                                   scalar1=127.0), "eV")
                    mk.wait(e, en, "eV", c)
                    c = mk.sig(e.tensor_scalar_mul(out=wrawD[:, :], in0=wrawD[:, :],
                                                   scalar1=scal[:, 3:4]), "eV")
                    mk.wait(e, en, "eV", c)

                @mk.on("scalar")
                def _(e, en, oc=oc):
                    mk.wait(e, en, "eV")
                    mk.sig(e.activation(out=p2af[:, 0:I0], in_=wrawD[:, :],
                                        func=AF.Sign), "eS")

                @mk.on("vector")
                def _(e, en, oc=oc):
                    mk.wait(e, en, "eS")
                    c = mk.sig(e.scalar_tensor_tensor(out=p2i2[:, :], in0=p2af[:, 0:I0],
                                                      scalar=0.0, in1=wrawD[:, :],
                                                      op0=ALU.mult, op1=ALU.add), "eV")
                    mk.wait(e, en, "eV", c)
                    c = mk.sig(e.tensor_copy(out=p2bh4[:, 0:I0], in_=p2i2[:, :]), "eV")
                    mk.wait(e, en, "eV", c)

                @mk.on("scalar")
                def _(e, en, oc=oc):
                    mk.wait(e, en, "eV")
                    mk.sig(e.dma_start(out=w8d_d.ap()[oc * 128:(oc + 1) * 128, :],
                                       in_=p2bh4[:, 0:I0]), "qC", 16)
            mk.gate()

            XQ_V0 = C["eV"]
            XQ_S0 = C["eS"]

            @mk.on("scalar")
            def _(e, en):
                for oc in range(OC):
                    if oc >= 1:
                        mk.wait(e, en, "eV", XQ_V0 + (oc - 1) * 3 + 2)
                    mk.sig(e.activation(out=p2a[:, 0, :], in_=stream[:, oc, :],
                                        func=AF.Sign), "eS")

            @mk.on("vector")
            def _(e, en):
                for oc in range(OC):
                    c = mk.sig(e.tensor_scalar_mul(out=p2a[:, 1, :], in0=stream[:, oc, :],
                                                   scalar1=scal[:, 1:2]), "eV")
                    mk.wait(e, en, "eS", XQ_S0 + oc + 1)
                    mk.wait(e, en, "eV", c)
                    c = mk.sig(e.scalar_tensor_tensor(out=p2b[:, :], in0=p2a[:, 0, :],
                                                      scalar=0.0, in1=p2a[:, 1, :],
                                                      op0=ALU.mult, op1=ALU.add), "eV")
                    mk.wait(e, en, "eV", c)
                    c = mk.sig(e.tensor_scalar(out=xq[:, oc, :], in0=p2b[:, :],
                                               scalar1=-128.0, scalar2=127.0,
                                               op0=ALU.max, op1=ALU.min), "eV")
                    mk.wait(e, en, "eV", c)
            mk.gate()

        @mk.on("vector")
        def _(e, en):
            mk.sig(e.tensor_scalar(out=csg_sb[:], in0=wsg_sb[:],
                                   scalar1=scal[:, 0:1], scalar2=1.0 / 127.0,
                                   op0=ALU.mult, op1=ALU.mult), "eV")
            mk.sig(e.tensor_scalar(out=csu_sb[:], in0=wsu_sb[:],
                                   scalar1=scal[:, 0:1], scalar2=1.0 / 127.0,
                                   op0=ALU.mult, op1=ALU.mult), "eV")
        mk.gate()

        ICGS = [(0, 4), (4, 4), (8, 4), (12, 4), (16, 4), (20, 2)]
        with nc.psum_tensor("g_ps", [128, 2, 512], F32) as g_ps, \
             nc.psum_tensor("u_ps", [128, 2, 512], F32) as u_ps:
            guT0, guV0, guS0 = C["eT"], C["eV"], C["eS"]
            guQS0, guQC0 = C["qS"], C["qC"]
            tiles_before = [0]
            for _, gn in ICGS:
                tiles_before.append(tiles_before[-1] + gn * T4)

            @mk.on("sync")
            def _(e, en):
                for gi, (ic0, gn) in enumerate(ICGS):
                    if gi >= 1:
                        mk.wait(e, en, "eT", guT0 + tiles_before[gi] * 2)
                    for j, wsrc in enumerate([w8g_d, w8u_d]):
                        for fc in range(OC):
                            mk.sig(e.dma_start(
                                out=wTgu[:, j, fc, 0:gn * 128],
                                in_=wsrc.ap()[ic0 * 128:(ic0 + gn) * 128,
                                              fc * 128:(fc + 1) * 128],
                                transpose=True), "qS", 16)

            @mk.on("tensor")
            def _(e, en):
                k = 0
                for gi, (ic0, gn) in enumerate(ICGS):
                    mk.wait(e, en, "qS", guQS0 + (gi + 1) * 16 * 16)
                    for ii in range(gn):
                        ic = ic0 + ii
                        for t4 in range(T4):
                            r = k % 2
                            if k >= 2:
                                mk.wait(e, en, "eS", guS0 + (k - 1) * 2)
                            for fc in range(OC):
                                mm = e.matmul(g_ps[:, r, :],
                                              wTgu[:, 0, fc, ii * 128:(ii + 1) * 128],
                                              xq[:, fc, t4 * 512:(t4 + 1) * 512],
                                              start=(fc == 0), stop=(fc == OC - 1))
                            mk.sig(mm, "eT")
                            for fc in range(OC):
                                mm = e.matmul(u_ps[:, r, :],
                                              wTgu[:, 1, fc, ii * 128:(ii + 1) * 128],
                                              xq[:, fc, t4 * 512:(t4 + 1) * 512],
                                              start=(fc == 0), stop=(fc == OC - 1))
                            mk.sig(mm, "eT")
                            k += 1

            @mk.on("scalar")
            def _(e, en):
                k = 0
                for gi, (ic0, gn) in enumerate(ICGS):
                    for ii in range(gn):
                        ic = ic0 + ii
                        for t4 in range(T4):
                            r = k % 2
                            mk.wait(e, en, "eT", guT0 + k * 2 + 1)
                            if k >= 1:
                                mk.wait(e, en, "eV", guV0 + k * 3)
                            mk.sig(e.activation(out=p2a[:, 0, r * 512:r * 512 + 512],
                                                in_=g_ps[:, r, :], func=AF.Silu,
                                                scale=csg_sb[:, ic:ic + 1]), "eS")
                            mk.wait(e, en, "eT", guT0 + k * 2 + 2)
                            mk.sig(e.activation(out=p2a[:, 1, r * 512:r * 512 + 512],
                                                in_=u_ps[:, r, :], func=AF.Identity,
                                                scale=csu_sb[:, ic:ic + 1]), "eS")
                            mk.wait(e, en, "eV", guV0 + k * 3 + 1)
                            mk.sig(e.dma_start(
                                out=hpre_d.ap()[ic, :, t4 * 512:(t4 + 1) * 512],
                                in_=p2a[:, 0, 1024 + r * 512:1024 + r * 512 + 512]), "qC", 16)
                            k += 1

            @mk.on("vector")
            def _(e, en):
                k = 0
                for gi, (ic0, gn) in enumerate(ICGS):
                    for ii in range(gn):
                        ic = ic0 + ii
                        for t4 in range(T4):
                            r = k % 2
                            mk.wait(e, en, "eS", guS0 + (k + 1) * 2)
                            if k >= 2:
                                mk.wait(e, en, "qC", guQC0 + (k - 1) * 16)
                            c = mk.sig(e.tensor_mul(out=p2a[:, 0, 1024 + r * 512:1536 + r * 512],
                                                    in0=p2a[:, 0, r * 512:r * 512 + 512],
                                                    in1=p2a[:, 1, r * 512:r * 512 + 512]), "eV")
                            mk.wait(e, en, "eV", c)
                            c = mk.sig(e.reduce_max(out=scal[:, 6:7],
                                                    in_=p2a[:, 0, 1024 + r * 512:1536 + r * 512],
                                                    axis=AX.X, apply_absolute_value=True), "eV")
                            mk.wait(e, en, "eV", c)
                            if t4 == 0:
                                c = mk.sig(e.tensor_copy(out=maxc_sb[:, ic:ic + 1],
                                                         in_=scal[:, 6:7]), "eV")
                            else:
                                c = mk.sig(e.tensor_max(out=maxc_sb[:, ic:ic + 1],
                                                        in0=maxc_sb[:, ic:ic + 1],
                                                        in1=scal[:, 6:7]), "eV")
                            mk.wait(e, en, "eV", c)
                            k += 1
            mk.gate()

        @mk.on("vector")
        def _(e, en):
            c = mk.sig(e.reduce_max(out=maxc_sb[:, 32:33], in_=maxc_sb[:, 0:IC0],
                                    axis=AX.X), "eV")
            mk.wait(e, en, "eV", c)
        emit_armax(1, maxc_sb[:, 32:33], maxc_sb[:, 33:34])

        with nc.psum_tensor("bc_ps2", [128, 1], F32) as bc_ps2:
            @mk.on("vector")
            def _(e, en):
                mk.wait(e, en, "qG")
                c = mk.sig(e.tensor_mul(out=row16[:, 56:57], in0=row16[:, 8:9], in1=sig_cd[:]), "eV")
                mk.wait(e, en, "eV", c)
                c = mk.sig(e.tensor_scalar_mul(out=row16[:, 57:58], in0=row16[:, 56:57],
                                               scalar1=1.0 / 127.0), "eV")
                mk.wait(e, en, "eV", c)
                c = mk.sig(e.reciprocal(out=row16[:, 58:59], in_=row16[:, 57:58]), "eV")
                mk.wait(e, en, "eV", c)
            emit_bcast(row16[:, 57:58], 4, bc_ps2[:, 0:1])
            emit_bcast(row16[:, 58:59], 5, bc_ps2[:, 0:1])

            @mk.on("vector")
            def _(e, en):
                mk.sig(e.tensor_scalar(out=csd_sb[:], in0=wsd_sb[:],
                                       scalar1=scal[:, 4:5], scalar2=1.0 / 127.0,
                                       op0=ALU.mult, op1=ALU.mult), "eV")
            mk.gate()

        with nc.psum_tensor("d_ps", [128, 4, 512], F32) as d_ps, \
             nc.psum_tensor("st2_ps", [128, 2, 512], F32) as st2_ps:
            for t4 in range(T4):
                hqQ0, hqS0, hqV0 = C["qC"], C["eS"], C["eV"]

                @mk.on("scalar")
                def _(e, en, t4=t4, hqV0=hqV0):
                    for ic in range(IC0):
                        r = ic % 2
                        if ic >= 2:
                            mk.wait(e, en, "eV", hqV0 + (ic - 1) * 3)
                        mk.sig(e.dma_start(out=p2a[:, 1, 1024 + r * 512:1536 + r * 512],
                                           in_=hpre_d.ap()[ic, :, t4 * 512:(t4 + 1) * 512]), "qC", 16)
                        mk.wait(e, en, "qC")
                        mk.sig(e.activation(out=p2a[:, 0, r * 512:r * 512 + 512],
                                            in_=p2a[:, 1, 1024 + r * 512:1536 + r * 512],
                                            func=AF.Sign), "eS")

                @mk.on("vector")
                def _(e, en, t4=t4, hqS0=hqS0):
                    for ic in range(IC0):
                        r = ic % 2
                        mk.wait(e, en, "eS", hqS0 + ic + 1)
                        c = mk.sig(e.tensor_scalar_mul(out=p2a[:, 1, r * 512:r * 512 + 512],
                                                       in0=p2a[:, 1, 1024 + r * 512:1536 + r * 512],
                                                       scalar1=scal[:, 5:6]), "eV")
                        mk.wait(e, en, "eV", c)
                        c = mk.sig(e.scalar_tensor_tensor(out=p2b[:, r * 512:r * 512 + 512],
                                                          in0=p2a[:, 0, r * 512:r * 512 + 512],
                                                          scalar=0.0,
                                                          in1=p2a[:, 1, r * 512:r * 512 + 512],
                                                          op0=ALU.mult, op1=ALU.add), "eV")
                        mk.wait(e, en, "eV", c)
                        c = mk.sig(e.tensor_scalar(out=hintC[:, ic, :],
                                                   in0=p2b[:, r * 512:r * 512 + 512],
                                                   scalar1=-128.0, scalar2=127.0,
                                                   op0=ALU.max, op1=ALU.min), "eV")
                        mk.wait(e, en, "eV", c)
                HQ_EV = C["eV"]

                dT0, dV0, dS0, dQS0 = C["eT"], C["eV"], C["eS"], C["qS"]

                @mk.on("sync")
                def _(e, en, t4=t4, dT0=dT0):
                    for ocg in range(2):
                        for ic in range(IC0):
                            r = ic % 2
                            li = ocg * IC0 + ic
                            if li >= 2:
                                mk.wait(e, en, "eT", dT0 + (li - 1))
                            else:
                                mk.wait(e, en, "eT", dT0)
                            mk.sig(e.dma_start(out=wTd[:, r, :],
                                               in_=w8d_d.ap()[ocg * 512:(ocg + 1) * 512,
                                                              ic * 128:(ic + 1) * 128],
                                               transpose=True), "qS", 16)

                @mk.on("tensor")
                def _(e, en, t4=t4, dQS0=dQS0, HQ_EV=HQ_EV, dV0=dV0):
                    mk.wait(e, en, "eV", HQ_EV)
                    for ocg in range(2):
                        if ocg == 1:
                            mk.wait(e, en, "eV", dV0 + 8)
                        for ic in range(IC0):
                            r = ic % 2
                            li = ocg * IC0 + ic
                            mk.wait(e, en, "qS", dQS0 + (li + 1) * 16)
                            for j in range(4):
                                mm = e.matmul(d_ps[:, j, :],
                                              wTd[:, r, j * 128:(j + 1) * 128],
                                              hintC[:, ic, :],
                                              start=(ic == 0), stop=(ic == IC0 - 1))
                            mk.sig(mm, "eT")

                @mk.on("vector")
                def _(e, en, t4=t4, dT0=dT0):
                    for ocg in range(2):
                        mk.wait(e, en, "eT", dT0 + (ocg + 1) * IC0)
                        for j in range(4):
                            oc = ocg * 4 + j
                            sl = stream[:, oc, t4 * 512:(t4 + 1) * 512]
                            c = mk.sig(e.scalar_tensor_tensor(
                                out=sl, in0=d_ps[:, j, :], scalar=csd_sb[:, oc:oc + 1],
                                in1=sl, op0=ALU.mult, op1=ALU.add), "eV")
                            mk.wait(e, en, "eV", c)
                            mk.sig(e.tensor_copy(out=ybf[:, oc, :], in_=sl), "eV")
                DW_EV = C["eV"]
                DW_S0 = C["eS"]

                @mk.on("scalar")
                def _(e, en, t4=t4, dV0=dV0):
                    for oc in range(OC):
                        mk.wait(e, en, "eV", dV0 + oc * 2 + 2)
                        mk.sig(e.activation(out=y2bf[:, oc, :], in_=ybf[:, oc, :],
                                            func=AF.Square), "eS")

                @mk.on("tensor")
                def _(e, en, t4=t4, dV0=dV0, DW_S0=DW_S0):
                    for oc in range(OC):
                        mk.wait(e, en, "eV", dV0 + oc * 2 + 2)
                        mm = e.matmul(st2_ps[:, 0, :], ones_blk[:], ybf[:, oc, :],
                                      start=(oc == 0), stop=(oc == OC - 1))
                    mk.sig(mm, "eT")
                    for oc in range(OC):
                        mk.wait(e, en, "eS", DW_S0 + oc + 1)
                        mm = e.matmul(st2_ps[:, 1, :], ones_blk[:], y2bf[:, oc, :],
                                      start=(oc == 0), stop=(oc == OC - 1))
                    mk.sig(mm, "eT")

                ln2T = C["eT"]

                @mk.on("vector")
                def _(e, en, t4=t4, ln2T=ln2T):
                    mk.wait(e, en, "eT", ln2T)
                    c1 = mk.sig(e.tensor_copy(out=scr[:, 0, :], in_=st2_ps[:, 0, :]), "eV")
                    mk.wait(e, en, "eV", c1)
                    c2 = mk.sig(e.scalar_tensor_tensor(out=scr[:, 1, :], in0=scr[:, 0, :],
                                                       scalar=1.0 / (D * D), in1=scr[:, 0, :],
                                                       op0=ALU.mult, op1=ALU.mult), "eV")
                    mk.wait(e, en, "eV", c2)
                    mk.sig(e.scalar_tensor_tensor(out=scr[:, 2, :], in0=st2_ps[:, 1, :],
                                                  scalar=1.0 / D, in1=scr[:, 1, :],
                                                  op0=ALU.mult, op1=ALU.subtract), "eV")
                LV = C["eV"]

                @mk.on("scalar")
                def _(e, en, t4=t4, LV=LV):
                    mk.wait(e, en, "eV", LV)
                    mk.sig(e.activation(out=r_bc[:, 0, :], in_=scr[:, 2, :], func=AF.Sqrt,
                                        bias=epsln_col[:, 0:1]), "eS")
                LS = C["eS"]

                @mk.on("vector")
                def _(e, en, t4=t4, LS=LS):
                    mk.wait(e, en, "eS", LS)
                    c1 = mk.sig(e.reciprocal(out=r_bc[:, 1, :], in_=r_bc[:, 0, :]), "eV")
                    mk.wait(e, en, "eV", c1)
                    c2 = mk.sig(e.scalar_tensor_tensor(out=scr[:, 3, :], in0=scr[:, 0, :],
                                                       scalar=1.0 / D, in1=r_bc[:, 1, :],
                                                       op0=ALU.mult, op1=ALU.mult), "eV")
                    mk.wait(e, en, "eV", c2)
                    for oc in range(OC):
                        sl = stream[:, oc, t4 * 512:(t4 + 1) * 512]
                        c3 = mk.sig(e.tensor_mul(out=scr[:, 1, :], in0=sl,
                                                 in1=r_bc[:, 1, :]), "eV")
                        mk.wait(e, en, "eV", c3)
                        c4 = mk.sig(e.tensor_sub(out=sl, in0=scr[:, 1, :],
                                                 in1=scr[:, 3, :]), "eV")
                        mk.wait(e, en, "eV", c4)
            mk.gate()


        mk.emit = (upto >= 7)

        @mk.on("vector")
        def _(e, en):
            for oc in range(OC):
                mk.sig(e.tensor_copy(out=xbf[:, oc, :], in_=stream[:, oc, :]), "eV")
        P3C_EV = C["eV"]

        @mk.on("scalar")
        def _(e, en):
            for oc in range(OC):
                mk.sig(e.dma_start(out=xspill.ap()[oc * 128:(oc + 1) * 128, :],
                                   in_=stream[:, oc, :]), "qC", 16)
        P3SP_QC = C["qC"]

        @mk.on("gpsimd")
        def _(e, en):
            mk.wait(e, en, "qC", P3SP_QC)
            for j, w in enumerate([wqt1, wkt1]):
                for fc in range(OC):
                    mk.sig(e.dma_start(out=w2[:, j, fc, :],
                                       in_=w.ap()[fc * 128:(fc + 1) * 128, :]), "qG", 16)
            for fc in range(OC):
                mk.sig(e.dma_start(out=wo_sb[:, fc, :],
                                   in_=wut1.ap()[fc * 128:(fc + 1) * 128, :]), "qG", 16)
        P3W_QG = C["qG"]

        with nc.psum_tensor("p3q_ps", [128, 2, 512], F32) as p3q_ps:
            NQT3 = OC * T4
            q3T0, q3S0 = C["eT"], C["eS"]

            @mk.on("tensor")
            def _(e, en):
                mk.wait(e, en, "qG", P3W_QG)
                mk.wait(e, en, "eV", P3C_EV)
                for j in range(2):
                    for i in range(NQT3):
                        oc, t4 = divmod(i, T4)
                        gi = j * NQT3 + i
                        r = gi % 2
                        if gi >= 2:
                            mk.wait(e, en, "eS", q3S0 + gi - 1)
                        for fc in range(OC):
                            mm = e.matmul(p3q_ps[:, r, :],
                                          w2[:, j, fc, oc * 128:(oc + 1) * 128],
                                          xbf[:, fc, t4 * 512:(t4 + 1) * 512],
                                          start=(fc == 0), stop=(fc == OC - 1))
                        mk.sig(mm, "eT")

            @mk.on("scalar")
            def _(e, en):
                for j in range(2):
                    for i in range(NQT3):
                        oc, t4 = divmod(i, T4)
                        gi = j * NQT3 + i
                        r = gi % 2
                        mk.wait(e, en, "eT", q3T0 + gi + 1)
                        dst = qbf if j == 0 else p3k
                        mk.sig(e.activation(out=dst[:, oc, t4 * 512:(t4 + 1) * 512],
                                            in_=p3q_ps[:, r, :], func=AF.Identity), "eS")
            mk.gate()

            @mk.on("gpsimd")
            def _(e, en):
                for oc in range(OC):
                    mk.sig(e.dma_start(out=kag_in.ap()[oc * 128:(oc + 1) * 128, :],
                                       in_=p3k[:, oc, :]), "qG", 16)
                for fc in range(OC):
                    mk.sig(e.dma_start(out=w2[:, 0, fc, :],
                                       in_=wvt1.ap()[fc * 128:(fc + 1) * 128, :]), "qG", 16)
            P3KB_QG = C["qG"]
            mk.gate()

        with nc.psum_tensor("p3v_ps", [128, 2, D], F32) as p3v_ps:
            v3T0, v3S0, v3QG0 = C["eT"], C["eS"], C["qG"]

            @mk.on("tensor")
            def _(e, en):
                for tt in range(TT):
                    r = tt % 2
                    if tt >= 2:
                        mk.wait(e, en, "eS", v3S0 + tt - 1)
                    for fc in range(OC):
                        for o2 in range(2):
                            mm = e.matmul(p3v_ps[:, r, o2 * 512:(o2 + 1) * 512],
                                          xbf[:, fc, tt * 128:(tt + 1) * 128],
                                          w2[:, 0, fc, o2 * 512:(o2 + 1) * 512],
                                          start=(fc == 0), stop=(fc == OC - 1))
                    mk.sig(mm, "eT")

            @mk.on("scalar")
            def _(e, en):
                for tt in range(TT):
                    r = tt % 2
                    mk.wait(e, en, "eT", v3T0 + tt + 1)
                    if tt >= 2:
                        mk.wait(e, en, "qG", v3QG0 + (tt - 1) * 16)
                    mk.sig(e.activation(out=vstg[:, r, :], in_=p3v_ps[:, r, :],
                                        func=AF.Identity), "eS")

            @mk.on("gpsimd")
            def _(e, en):
                for tt in range(TT):
                    r = tt % 2
                    mk.wait(e, en, "eS", v3S0 + tt + 1)
                    mk.sig(e.dma_start(
                        out=vag_in.ap().rearrange("h t d -> t h d")[tt * 128:(tt + 1) * 128],
                        in_=vstg[:, r, :].rearrange("p (h d) -> p h d", d=64)), "qG", 16)
            mk.gate()

        @mk.on("gpsimd")
        def _(e, en):
            mk.sig(e.collective_compute(
                "AllGather", ALU.bypass, replica_groups=RG_PAIR,
                ins=[kag_in.ap().opt()], outs=[kag_out.ap().opt()]), "cc")
            mk.sig(e.collective_compute(
                "AllGather", ALU.bypass, replica_groups=RG_PAIR,
                ins=[vag_in.ap().opt()], outs=[vag_out.ap().opt()]), "cc")
            mk.wait(e, en, "cc")
        mk.gate()

        with nc.psum_tensor("sT_ps", [128, 2, 1024], F32) as sT_ps, \
             nc.psum_tensor("av_ps", [128, 4, 512], F32) as av_ps:

            @mk.on("vector")
            def _(e, en):
                mk.sig(e.memset(vh[:, :, 64:65], 1.0), "eV")
            mk.gate()

            for h in range(NH):
                p0 = (h % 2) * 64
                ch = h // 2
                hT0, hS0, hV0, hQS0 = C["eT"], C["eS"], C["eV"], C["qS"]

                @mk.on("sync")
                def _(e, en, h=h, p0=p0, hT0=hT0):
                    for half in range(2):
                        if h + half >= 1:
                            pass
                        if h >= 1:
                            mk.wait(e, en, "eT", hT0)
                        mk.sig(e.dma_start(
                            out=kh[p0:p0 + 64, half, :],
                            in_=kag_out.ap()[half * D + h * 64:half * D + (h + 1) * 64, :]), "qS", 16)
                        mk.sig(e.dma_start(
                            out=vh[:, half * 16:(half + 1) * 16, 0:64],
                            in_=vag_out.ap()[half, h, :, :].rearrange("(g p) d -> p g d", p=128)), "qS", 16)

                NI = 64

                @mk.on("tensor")
                def _(e, en, h=h, p0=p0, ch=ch, hT0=hT0, hS0=hS0, hV0=hV0, hQS0=hQS0):
                    if h >= 1:
                        mk.wait(e, en, "eV", hV0)
                    mk.wait(e, en, "qS", hQS0 + 64)
                    for i in range(NI):
                        kt, qh = divmod(i, 2)
                        half = kt // 16
                        ktl = kt % 16
                        r = i % 2
                        if i >= 2:
                            mk.wait(e, en, "eS", hS0 + i - 1)
                        for half512 in range(2):
                            mm = e.matmul(
                                sT_ps[:, r, half512 * 512:(half512 + 1) * 512],
                                kh[p0:p0 + 64, half, ktl * 128:(ktl + 1) * 128],
                                qbf[p0:p0 + 64, ch,
                                    qh * 1024 + half512 * 512:qh * 1024 + (half512 + 1) * 512],
                                start=True, stop=True)
                        mk.sig(mm, "eT")
                        if i >= 1:
                            self_i = i - 1
                            kt2, qh2 = divmod(self_i, 2)
                            r2 = self_i % 2
                            mk.wait(e, en, "eS", hS0 + self_i + 1)
                            for j2 in range(2):
                                mm = e.matmul(av_ps[0:65, qh2 * 2 + j2, :],
                                              vh[:, kt2, :],
                                              pt[:, r2, j2 * 512:(j2 + 1) * 512],
                                              start=(kt2 == 0), stop=(kt2 == 31))
                            mk.sig(mm, "eT")
                    i = NI - 1
                    kt2, qh2 = divmod(i, 2)
                    r2 = i % 2
                    mk.wait(e, en, "eS", hS0 + i + 1)
                    for j2 in range(2):
                        mm = e.matmul(av_ps[0:65, qh2 * 2 + j2, :],
                                      vh[:, kt2, :],
                                      pt[:, r2, j2 * 512:(j2 + 1) * 512],
                                      start=(kt2 == 0), stop=(kt2 == 31))
                    mk.sig(mm, "eT")
                    AV_DONE = C["eT"]
                    mk.wait(e, en, "eV", hV0 + 1)
                    for j in range(4):
                        mm = e.matmul(sT_ps[0:64, j // 2, (j % 2) * 512:(j % 2) * 512 + 512],
                                      ones_r1f[:, 0:64],
                                      scr[0:1, j, :], start=True, stop=True)
                    mk.sig(mm, "eT")

                @mk.on("scalar")
                def _(e, en, h=h, hT0=hT0, hV0=hV0):
                    for i in range(NI):
                        r = i % 2
                        tsig = 1 if i == 0 else 2 * i
                        mk.wait(e, en, "eT", hT0 + tsig)
                        if i >= 2:
                            mk.wait(e, en, "eT", hT0 + 2 * (i - 2) + 3)
                        mk.sig(e.activation(out=pt[:, r, :], in_=sT_ps[:, r, :],
                                            func=AF.Exp, scale=1.0 / 32.0), "eS")

                @mk.on("vector")
                def _(e, en, h=h, p0=p0, ch=ch, hT0=hT0):
                    mk.wait(e, en, "eT", hT0 + 2 * (NI - 2) + 3 + 1)
                    c = mk.sig(e.reciprocal(out=scr[0:1, :, :], in_=av_ps[64:65, :, :]), "eV")
                    mk.wait(e, en, "eV", c)
                    mk.wait(e, en, "eT", hT0 + 2 * (NI - 2) + 3 + 1 + 1)
                    for j in range(4):
                        c = mk.sig(e.tensor_copy(out=scr[0:64, j, :],
                                                 in_=sT_ps[0:64, j // 2,
                                                           (j % 2) * 512:(j % 2) * 512 + 512]), "eV")
                        mk.wait(e, en, "eV", c)
                        if h % 2 == 0:
                            dst = p3att[0:64, ch, j * 512:(j + 1) * 512]
                        else:
                            dst = strip[:, j * 512:(j + 1) * 512]
                        c = mk.sig(e.tensor_mul(out=dst, in0=av_ps[0:64, j, :],
                                                in1=scr[0:64, j, :]), "eV")
                        mk.wait(e, en, "eV", c)

                if h % 2 == 1:
                    @mk.on("gpsimd")
                    def _(e, en, h=h, ch=ch):
                        mk.wait(e, en, "eV")
                        mk.sig(e.dma_start(out=p3att[64:128, ch, :], in_=strip[:, :]), "qG", 16)
                        mk.wait(e, en, "qG")
            mk.gate()

        with nc.psum_tensor("wu_ps", [128, 2, 512], F32) as wu_ps, \
             nc.psum_tensor("st3_ps", [128, 2, 512], F32) as st3_ps:
            prev_spill_qc = 0
            for t4 in range(T4):
                t4T0, t4V0, t4S0 = C["eT"], C["eV"], C["eS"]

                @mk.on("sync")
                def _(e, en, t4=t4, t4V0=t4V0, prev_spill_qc=prev_spill_qc):
                    if t4 >= 1:
                        mk.wait(e, en, "eV", t4V0)
                        mk.wait(e, en, "qC", prev_spill_qc)
                    for oc in range(OC):
                        mk.sig(e.dma_start(out=xres[:, oc, :],
                                           in_=xspill.ap()[oc * 128:(oc + 1) * 128,
                                                           t4 * 512:(t4 + 1) * 512]), "qS", 16)
                XR_QS = C["qS"]

                @mk.on("tensor")
                def _(e, en, t4=t4, t4V0=t4V0):
                    for oc in range(OC):
                        rr = oc % 2
                        if oc >= 2:
                            mk.wait(e, en, "eV", t4V0 + (oc - 1) * 2)
                        for hp in range(8):
                            mm = e.matmul(wu_ps[:, rr, :],
                                          wo_sb[:, hp, oc * 128:(oc + 1) * 128],
                                          p3att[:, hp, t4 * 512:(t4 + 1) * 512],
                                          start=(hp == 0), stop=(hp == 7))
                        mk.sig(mm, "eT")
                    for oc in range(OC):
                        mk.wait(e, en, "eV", t4V0 + oc * 2 + 2)
                        mm = e.matmul(st3_ps[:, 0, :], ones_blk[:], ybf[:, oc, :],
                                      start=(oc == 0), stop=(oc == OC - 1))
                    mk.sig(mm, "eT")
                    for oc in range(OC):
                        mk.wait(e, en, "eS", t4S0 + oc + 1)
                        mm = e.matmul(st3_ps[:, 1, :], ones_blk[:], y2bf[:, oc, :],
                                      start=(oc == 0), stop=(oc == OC - 1))
                    mk.sig(mm, "eT")

                @mk.on("vector")
                def _(e, en, t4=t4, t4T0=t4T0, XR_QS=XR_QS):
                    mk.wait(e, en, "qS", XR_QS)
                    for oc in range(OC):
                        rr = oc % 2
                        mk.wait(e, en, "eT", t4T0 + oc + 1)
                        mk.sig(e.scalar_tensor_tensor(
                            out=xres[:, oc, :],
                            in0=wu_ps[:, rr, :], scalar=bu1_sb[:, oc:oc + 1],
                            in1=xres[:, oc, :], op0=ALU.add, op1=ALU.add), "eV")
                        mk.sig(e.tensor_copy(out=ybf[:, oc, :],
                                             in_=xres[:, oc, :]), "eV")

                @mk.on("scalar")
                def _(e, en, t4=t4, t4V0=t4V0):
                    for oc in range(OC):
                        mk.wait(e, en, "eV", t4V0 + oc * 2 + 2)
                        mk.sig(e.activation(out=y2bf[:, oc, :], in_=ybf[:, oc, :],
                                            func=AF.Square), "eS")

                ln3T = C["eT"]

                @mk.on("vector")
                def _(e, en, t4=t4, ln3T=ln3T):
                    mk.wait(e, en, "eT", ln3T)
                    c1 = mk.sig(e.tensor_copy(out=scr[:, 0, :], in_=st3_ps[:, 0, :]), "eV")
                    mk.wait(e, en, "eV", c1)
                    c2 = mk.sig(e.scalar_tensor_tensor(out=scr[:, 1, :], in0=scr[:, 0, :],
                                                       scalar=1.0 / (D * D), in1=scr[:, 0, :],
                                                       op0=ALU.mult, op1=ALU.mult), "eV")
                    mk.wait(e, en, "eV", c2)
                    mk.sig(e.scalar_tensor_tensor(out=scr[:, 2, :], in0=st3_ps[:, 1, :],
                                                  scalar=1.0 / D, in1=scr[:, 1, :],
                                                  op0=ALU.mult, op1=ALU.subtract), "eV")
                LV = C["eV"]

                @mk.on("scalar")
                def _(e, en, t4=t4, LV=LV):
                    mk.wait(e, en, "eV", LV)
                    mk.sig(e.activation(out=r_bc[:, 0, :], in_=scr[:, 2, :], func=AF.Sqrt,
                                        bias=epsln_col[:, 0:1]), "eS")
                LS = C["eS"]

                @mk.on("vector")
                def _(e, en, t4=t4, LS=LS):
                    mk.wait(e, en, "eS", LS)
                    c1 = mk.sig(e.reciprocal(out=r_bc[:, 1, :], in_=r_bc[:, 0, :]), "eV")
                    mk.wait(e, en, "eV", c1)
                    c2 = mk.sig(e.scalar_tensor_tensor(out=scr[:, 3, :], in0=scr[:, 0, :],
                                                       scalar=1.0 / D, in1=r_bc[:, 1, :],
                                                       op0=ALU.mult, op1=ALU.mult), "eV")
                    mk.wait(e, en, "eV", c2)
                    for oc in range(OC):
                        sl = xres[:, oc, :]
                        c3 = mk.sig(e.tensor_mul(out=scr[:, 1, :], in0=sl,
                                                 in1=r_bc[:, 1, :]), "eV")
                        mk.wait(e, en, "eV", c3)
                        c4 = mk.sig(e.tensor_sub(out=sl, in0=scr[:, 1, :],
                                                 in1=scr[:, 3, :]), "eV")
                        mk.wait(e, en, "eV", c4)
                        mk.sig(e.reduce_max(out=maxc_sb[:, t4 * OC + oc:t4 * OC + oc + 1],
                                            in_=sl, axis=AX.X,
                                            apply_absolute_value=True), "eV")
                XN_EV = C["eV"]

                @mk.on("scalar")
                def _(e, en, t4=t4, XN_EV=XN_EV):
                    mk.wait(e, en, "eV", XN_EV)
                    for oc in range(OC):
                        mk.sig(e.dma_start(out=xspill.ap()[oc * 128:(oc + 1) * 128,
                                                           t4 * 512:(t4 + 1) * 512],
                                           in_=xres[:, oc, :]), "qC", 16)
                prev_spill_qc = C["qC"]
            mk.gate()

        @mk.on("vector")
        def _(e, en):
            c1 = mk.sig(e.reduce_max(out=maxc_sb[:, 32:33], in_=maxc_sb[:, 0:32],
                                     axis=AX.X), "eV")
            mk.wait(e, en, "eV", c1)
        emit_armax(2, maxc_sb[:, 32:33], maxc_sb[:, 33:34])

        @mk.on("gpsimd")
        def _(e, en):
            for oc in range(OC):
                mk.sig(e.dma_start(out=stream[:, oc, :],
                                   in_=xspill.ap()[oc * 128:(oc + 1) * 128, :]), "qG", 16)
        mk.gate()


        mk.emit = (upto >= 8)

        def prep_lpbq(raw, dst, nchunk, N, colparts=1):
            NB = N // 32
            for ici in range(nchunk * colparts):
                ic, cp = divmod(ici, colparts)
                @mk.on("sync")
                def _(e, en, ic=ic, cp=cp, N=N):
                    mk.wait(e, en, "qC")
                    mk.wait(e, en, "eV")
                    mk.sig(e.dma_start(out=p2a[:, 0, 0:N],
                                       in_=raw.ap()[ic * 128:(ic + 1) * 128,
                                                    cp * N:(cp + 1) * N]), "qS", 16)

                @mk.on("vector")
                def _(e, en, ic=ic, NB=NB, N=N):
                    mk.wait(e, en, "qS")
                    wv = p2a[:, 0, 0:N].rearrange("p (b e) -> p b e", e=32)
                    c = mk.sig(e.reduce_max(out=scr[:, 0, 0:NB], in_=wv, axis=AX.X,
                                            apply_absolute_value=True), "eV")
                    mk.wait(e, en, "eV", c)
                    c = mk.sig(e.tensor_scalar(out=scr[:, 1, 0:NB], in0=scr[:, 0, 0:NB],
                                               scalar1=1.0 / 7.0, scalar2=1e-12,
                                               op0=ALU.mult, op1=ALU.max), "eV")
                    mk.wait(e, en, "eV", c)
                    c = mk.sig(e.reciprocal(out=scr[:, 2, 0:NB], in_=scr[:, 1, 0:NB]), "eV")
                    mk.wait(e, en, "eV", c)
                    c = mk.sig(e.tensor_tensor(
                        out=wv, in0=wv,
                        in1=scr[:, 2, 0:NB].rearrange("p (b o) -> p b o", o=1).broadcast_to((128, NB, 32)),
                        op=ALU.mult), "eV")
                    mk.wait(e, en, "eV", c)

                @mk.on("scalar")
                def _(e, en, ic=ic, N=N):
                    mk.wait(e, en, "eV")
                    mk.sig(e.activation(out=p2a[:, 1, 0:N], in_=p2a[:, 0, 0:N],
                                        func=AF.Sign), "eS")

                @mk.on("vector")
                def _(e, en, ic=ic, NB=NB, N=N):
                    mk.wait(e, en, "eS")
                    src_w = p2a[:, 0, 0:N]
                    src_s = p2a[:, 1, 0:N]
                    c = mk.sig(e.scalar_tensor_tensor(out=p4i[:, 0:N], in0=src_s,
                                                      scalar=0.0, in1=src_w,
                                                      op0=ALU.mult, op1=ALU.add), "eV")
                    mk.wait(e, en, "eV", c)
                    c = mk.sig(e.tensor_tensor(
                        out=p2bh4[:, 0:N],
                        in0=p4i[:, 0:N].rearrange("p (b e) -> p b e", e=32),
                        in1=scr[:, 1, 0:NB].rearrange("p (b o) -> p b o", o=1).broadcast_to((128, NB, 32)),
                        op=ALU.mult), "eV")
                    mk.wait(e, en, "eV", c)

                @mk.on("scalar")
                def _(e, en, ic=ic, cp=cp, N=N):
                    mk.wait(e, en, "eV")
                    mk.sig(e.dma_start(out=dst.ap()[ic * 128:(ic + 1) * 128,
                                                    cp * N:(cp + 1) * N],
                                       in_=p2bh4[:, 0:N]), "qC", 16)

        prep_lpbq(wg1, wq1g, IC1, D)
        prep_lpbq(wup1, wq1u, IC1, D)
        prep_lpbq(wd1, wq1d, OC, I1 // 2, colparts=2)
        mk.gate()

        with nc.psum_tensor("bc4_ps", [128, 1], F32) as bc4_ps:
            @mk.on("vector")
            def _(e, en):
                mk.wait(e, en, "qG")
                c = mk.sig(e.tensor_scalar(out=row16[:, 56:57], in0=row16[:, 16:17],
                                           scalar1=1.0 / 32767.0, scalar2=1e-12,
                                           op0=ALU.mult, op1=ALU.max), "eV")
                mk.wait(e, en, "eV", c)
                c = mk.sig(e.reciprocal(out=row16[:, 57:58], in_=row16[:, 56:57]), "eV")
                mk.wait(e, en, "eV", c)
            emit_bcast(row16[:, 56:57], 2, bc4_ps[:, 0:1])
            emit_bcast(row16[:, 57:58], 3, bc4_ps[:, 0:1])

            XQ4_V0 = C["eV"]
            XQ4_S0 = C["eS"]

            @mk.on("scalar")
            def _(e, en):
                for oc in range(OC):
                    if oc >= 1:
                        mk.wait(e, en, "eV", XQ4_V0 + (oc - 1) * 3 + 2)
                    mk.sig(e.activation(out=p2a[:, 0, :], in_=stream[:, oc, :],
                                        func=AF.Sign), "eS")

            @mk.on("vector")
            def _(e, en):
                for oc in range(OC):
                    c = mk.sig(e.tensor_scalar_mul(out=p2a[:, 1, :], in0=stream[:, oc, :],
                                                   scalar1=scal[:, 3:4]), "eV")
                    mk.wait(e, en, "eS", XQ4_S0 + oc + 1)
                    mk.wait(e, en, "eV", c)
                    c = mk.sig(e.scalar_tensor_tensor(out=p2b[:, :], in0=p2a[:, 0, :],
                                                      scalar=0.0, in1=p2a[:, 1, :],
                                                      op0=ALU.mult, op1=ALU.add), "eV")
                    mk.wait(e, en, "eV", c)
                    c = mk.sig(e.tensor_scalar_mul(out=xq[:, oc, :], in0=p2b[:, :],
                                                   scalar1=scal[:, 2:3]), "eV")
                    mk.wait(e, en, "eV", c)
            mk.gate()

        ICGS4 = [(0, 4), (4, 4), (8, 4), (12, 4), (16, 4), (20, 4)]
        with nc.psum_tensor("g4_ps", [128, 2, 512], F32) as g4_ps, \
             nc.psum_tensor("u4_ps", [128, 2, 512], F32) as u4_ps:
            gu4T0, gu4V0, gu4S0 = C["eT"], C["eV"], C["eS"]
            gu4QS0, gu4QC0 = C["qS"], C["qC"]
            tiles_before4 = [0]
            for _, gn in ICGS4:
                tiles_before4.append(tiles_before4[-1] + gn * T4)

            @mk.on("sync")
            def _(e, en):
                for gi, (ic0, gn) in enumerate(ICGS4):
                    if gi >= 1:
                        mk.wait(e, en, "eT", gu4T0 + tiles_before4[gi] * 2)
                    for j, wsrc in enumerate([wq1g, wq1u]):
                        for fc in range(OC):
                            mk.sig(e.dma_start(
                                out=wTgu[:, j, fc, 0:gn * 128],
                                in_=wsrc.ap()[ic0 * 128:(ic0 + gn) * 128,
                                              fc * 128:(fc + 1) * 128],
                                transpose=True), "qS", 16)

            @mk.on("tensor")
            def _(e, en):
                k = 0
                for gi, (ic0, gn) in enumerate(ICGS4):
                    mk.wait(e, en, "qS", gu4QS0 + (gi + 1) * 16 * 16)
                    for ii in range(gn):
                        for t4 in range(T4):
                            r = k % 2
                            if k >= 2:
                                mk.wait(e, en, "eS", gu4S0 + (k - 1) * 2)
                            for fc in range(OC):
                                mm = e.matmul(g4_ps[:, r, :],
                                              wTgu[:, 0, fc, ii * 128:(ii + 1) * 128],
                                              xq[:, fc, t4 * 512:(t4 + 1) * 512],
                                              start=(fc == 0), stop=(fc == OC - 1))
                            mk.sig(mm, "eT")
                            for fc in range(OC):
                                mm = e.matmul(u4_ps[:, r, :],
                                              wTgu[:, 1, fc, ii * 128:(ii + 1) * 128],
                                              xq[:, fc, t4 * 512:(t4 + 1) * 512],
                                              start=(fc == 0), stop=(fc == OC - 1))
                            mk.sig(mm, "eT")
                            k += 1

            @mk.on("scalar")
            def _(e, en):
                k = 0
                for gi, (ic0, gn) in enumerate(ICGS4):
                    for ii in range(gn):
                        ic = ic0 + ii
                        for t4 in range(T4):
                            r = k % 2
                            mk.wait(e, en, "eT", gu4T0 + k * 2 + 1)
                            if k >= 2:
                                mk.wait(e, en, "qC", gu4QC0 + (k - 1) * 32)
                                mk.wait(e, en, "eV", gu4V0 + (k - 1) * 4)
                            mk.sig(e.activation(out=p4st[:, 0, r * 512:r * 512 + 512],
                                                in_=g4_ps[:, r, :], func=AF.Identity), "eS")
                            mk.wait(e, en, "eT", gu4T0 + k * 2 + 2)
                            mk.sig(e.activation(out=p4st[:, 1, r * 512:r * 512 + 512],
                                                in_=u4_ps[:, r, :], func=AF.Identity), "eS")
                            mk.sig(e.dma_start(
                                out=graw_d.ap()[ic, :, t4 * 512:(t4 + 1) * 512],
                                in_=p4st[:, 0, r * 512:r * 512 + 512]), "qC", 16)
                            mk.sig(e.dma_start(
                                out=uraw_d.ap()[ic, :, t4 * 512:(t4 + 1) * 512],
                                in_=p4st[:, 1, r * 512:r * 512 + 512]), "qC", 16)
                            k += 1

            @mk.on("vector")
            def _(e, en):
                k = 0
                for gi, (ic0, gn) in enumerate(ICGS4):
                    for ii in range(gn):
                        for t4 in range(T4):
                            r = k % 2
                            mk.wait(e, en, "eS", gu4S0 + k * 2 + 2)
                            c = mk.sig(e.reduce_max(out=scal[:, 6:7],
                                                    in_=p4st[:, 0, r * 512:r * 512 + 512],
                                                    axis=AX.X, apply_absolute_value=True), "eV")
                            mk.wait(e, en, "eV", c)
                            if k == 0:
                                c = mk.sig(e.tensor_copy(out=maxc_sb[:, 34:35],
                                                         in_=scal[:, 6:7]), "eV")
                            else:
                                c = mk.sig(e.tensor_max(out=maxc_sb[:, 34:35],
                                                        in0=maxc_sb[:, 34:35],
                                                        in1=scal[:, 6:7]), "eV")
                            mk.wait(e, en, "eV", c)
                            c = mk.sig(e.reduce_max(out=scal[:, 6:7],
                                                    in_=p4st[:, 1, r * 512:r * 512 + 512],
                                                    axis=AX.X, apply_absolute_value=True), "eV")
                            mk.wait(e, en, "eV", c)
                            if k == 0:
                                c = mk.sig(e.tensor_copy(out=maxc_sb[:, 35:36],
                                                         in_=scal[:, 6:7]), "eV")
                            else:
                                c = mk.sig(e.tensor_max(out=maxc_sb[:, 35:36],
                                                        in0=maxc_sb[:, 35:36],
                                                        in1=scal[:, 6:7]), "eV")
                            mk.wait(e, en, "eV", c)
                            k += 1
            mk.gate()

        emit_armax(3, maxc_sb[:, 35:36], maxc_sb[:, 33:34])
        emit_armax(4, maxc_sb[:, 34:35], maxc_sb[:, 33:34])
        with nc.psum_tensor("bc5_ps", [128, 1], F32) as bc5_ps:
            @mk.on("vector")
            def _(e, en):
                mk.wait(e, en, "qG")
                for j, col in [(0, 24), (1, 32)]:
                    c = mk.sig(e.tensor_scalar(out=row16[:, 56 + j:57 + j],
                                               in0=row16[:, col:col + 1],
                                               scalar1=1.0 / 32767.0, scalar2=1e-12,
                                               op0=ALU.mult, op1=ALU.max), "eV")
                    mk.wait(e, en, "eV", c)
                    c = mk.sig(e.reciprocal(out=row16[:, 58 + j:59 + j],
                                            in_=row16[:, 56 + j:57 + j]), "eV")
                    mk.wait(e, en, "eV", c)
            emit_bcast(row16[:, 56:57], 0, bc5_ps[:, 0:1])
            emit_bcast(row16[:, 58:59], 1, bc5_ps[:, 0:1])
            emit_bcast(row16[:, 57:58], 4, bc5_ps[:, 0:1])
            emit_bcast(row16[:, 59:60], 5, bc5_ps[:, 0:1])

            B_V0, B_S0 = C["eV"], C["eS"]

            @mk.on("scalar")
            def _(e, en):
                for ic in range(IC1):
                    if ic >= 1:
                        mk.wait(e, en, "eV", B_V0 + (ic - 1) * 7 + 7)
                    mk.sig(e.dma_start(out=p4st[:, 0, :],
                                       in_=graw_d.ap()[ic, :, :]), "qC", 16)
                    mk.wait(e, en, "qC")
                    mk.sig(e.activation(out=p2a[:, 0, :], in_=p4st[:, 0, :],
                                        func=AF.Sign), "eS")
                    mk.wait(e, en, "eV", B_V0 + ic * 7 + 2)
                    mk.sig(e.activation(out=p2a[:, 0, :], in_=p2b[:, :],
                                        func=AF.Sigmoid, scale=scal[:, 4:5]), "eS")
                    mk.wait(e, en, "eV", B_V0 + ic * 7 + 5)
                    mk.sig(e.dma_start(out=gs_d.ap()[ic, :, :],
                                       in_=p2bh4[:, 0:T]), "qC", 16)

            @mk.on("vector")
            def _(e, en):
                for ic in range(IC1):
                    mk.wait(e, en, "eS", B_S0 + ic * 2 + 1)
                    c = mk.sig(e.tensor_scalar_mul(out=p2a[:, 1, :], in0=p4st[:, 0, :],
                                                   scalar1=scal[:, 5:6]), "eV")
                    mk.wait(e, en, "eV", c)
                    c = mk.sig(e.scalar_tensor_tensor(out=p2b[:, :], in0=p2a[:, 0, :],
                                                      scalar=0.0, in1=p2a[:, 1, :],
                                                      op0=ALU.mult, op1=ALU.add), "eV")
                    mk.wait(e, en, "eV", c)
                    mk.wait(e, en, "eS", B_S0 + ic * 2 + 2)
                    c = mk.sig(e.tensor_scalar(out=p4i[:, 0:T], in0=p2a[:, 0, :],
                                               scalar1=65536.0, scalar2=0.0,
                                               op0=ALU.mult, op1=ALU.add), "eV")
                    mk.wait(e, en, "eV", c)
                    c = mk.sig(e.tensor_scalar(out=p2a[:, 1, :], in0=p4i[:, 0:T],
                                               scalar1=65535.0, scalar2=2.0 ** -16,
                                               op0=ALU.min, op1=ALU.mult), "eV")
                    mk.wait(e, en, "eV", c)
                    c = mk.sig(e.scalar_tensor_tensor(out=p2bh4[:, 0:T], in0=p2b[:, :],
                                                      scalar=scal[:, 4:5], in1=p2a[:, 1, :],
                                                      op0=ALU.mult, op1=ALU.mult), "eV")
                    mk.wait(e, en, "eV", c)
                    c = mk.sig(e.reduce_max(out=scal[:, 6:7], in_=p2bh4[:, 0:T],
                                            axis=AX.X, apply_absolute_value=True), "eV")
                    mk.wait(e, en, "eV", c)
                    if ic == 0:
                        c = mk.sig(e.tensor_copy(out=maxc_sb[:, 34:35], in_=scal[:, 6:7]), "eV")
                    else:
                        c = mk.sig(e.tensor_max(out=maxc_sb[:, 34:35],
                                                in0=maxc_sb[:, 34:35], in1=scal[:, 6:7]), "eV")
                    e.wait_ge(mk.sems["eV"], c)
            mk.gate()

        emit_armax(5, maxc_sb[:, 34:35], maxc_sb[:, 33:34])
        with nc.psum_tensor("bc6_ps", [128, 1], F32) as bc6_ps:
            @mk.on("vector")
            def _(e, en):
                mk.wait(e, en, "qG")
                c = mk.sig(e.tensor_scalar(out=row16[:, 56:57], in0=row16[:, 40:41],
                                           scalar1=1.0 / 32767.0, scalar2=1e-12,
                                           op0=ALU.mult, op1=ALU.max), "eV")
                mk.wait(e, en, "eV", c)
                c = mk.sig(e.reciprocal(out=row16[:, 57:58], in_=row16[:, 56:57]), "eV")
                mk.wait(e, en, "eV", c)
            emit_bcast(row16[:, 57:58], 2, bc6_ps[:, 0:1])
            emit_bcast(row16[:, 56:57], 7, bc6_ps[:, 0:1])

            @mk.on("vector")
            def _(e, en):
                c = mk.sig(e.tensor_mul(out=scal[:, 4:5], in0=scal[:, 0:1],
                                        in1=scal[:, 7:8]), "eV")
                mk.wait(e, en, "eV", c)
            mk.gate()

            C_V0, C_S0 = C["eV"], C["eS"]

            @mk.on("scalar")
            def _(e, en):
                for ic in range(IC1):
                    if ic >= 1:
                        mk.wait(e, en, "eV", C_V0 + (ic - 1) * 7 + 7)
                    mk.sig(e.dma_start(out=p4st[:, 0, :], in_=gs_d.ap()[ic, :, :]), "qC", 16)
                    mk.wait(e, en, "qC")
                    mk.sig(e.activation(out=p2a[:, 0, :], in_=p4st[:, 0, :],
                                        func=AF.Sign), "eS")
                    mk.wait(e, en, "eV", C_V0 + ic * 7 + 2)
                    mk.sig(e.dma_start(out=p4st[:, 1, :], in_=uraw_d.ap()[ic, :, :]), "qC", 16)
                    mk.wait(e, en, "qC")
                    mk.sig(e.activation(out=p2a[:, 0, :], in_=p4st[:, 1, :],
                                        func=AF.Sign), "eS")
                    mk.wait(e, en, "eV", C_V0 + ic * 7 + 5)
                    mk.sig(e.dma_start(out=opre_d.ap()[ic, :, :], in_=p2bh4[:, 0:T]), "qC", 16)

            @mk.on("vector")
            def _(e, en):
                for ic in range(IC1):
                    mk.wait(e, en, "eS", C_S0 + ic * 2 + 1)
                    c = mk.sig(e.tensor_scalar_mul(out=p2a[:, 1, :], in0=p4st[:, 0, :],
                                                   scalar1=scal[:, 2:3]), "eV")
                    mk.wait(e, en, "eV", c)
                    c = mk.sig(e.scalar_tensor_tensor(out=p2b[:, :], in0=p2a[:, 0, :],
                                                      scalar=0.0, in1=p2a[:, 1, :],
                                                      op0=ALU.mult, op1=ALU.add), "eV")
                    mk.wait(e, en, "eV", c)
                    mk.wait(e, en, "eS", C_S0 + ic * 2 + 2)
                    c = mk.sig(e.tensor_scalar_mul(out=p2a[:, 1, :], in0=p4st[:, 1, :],
                                                   scalar1=scal[:, 1:2]), "eV")
                    mk.wait(e, en, "eV", c)
                    c = mk.sig(e.scalar_tensor_tensor(out=p4i[:, 0:T], in0=p2a[:, 0, :],
                                                      scalar=0.0, in1=p2a[:, 1, :],
                                                      op0=ALU.mult, op1=ALU.add), "eV")
                    mk.wait(e, en, "eV", c)
                    c = mk.sig(e.scalar_tensor_tensor(out=p2bh4[:, 0:T], in0=p2b[:, :],
                                                      scalar=scal[:, 4:5], in1=p4i[:, 0:T],
                                                      op0=ALU.mult, op1=ALU.mult), "eV")
                    mk.wait(e, en, "eV", c)
                    c = mk.sig(e.reduce_max(out=scal[:, 6:7], in_=p2bh4[:, 0:T],
                                            axis=AX.X, apply_absolute_value=True), "eV")
                    mk.wait(e, en, "eV", c)
                    if ic == 0:
                        c = mk.sig(e.tensor_copy(out=maxc_sb[:, 34:35], in_=scal[:, 6:7]), "eV")
                    else:
                        c = mk.sig(e.tensor_max(out=maxc_sb[:, 34:35],
                                                in0=maxc_sb[:, 34:35], in1=scal[:, 6:7]), "eV")
                    e.wait_ge(mk.sems["eV"], c)
            mk.gate()

        emit_armax(6, maxc_sb[:, 34:35], maxc_sb[:, 33:34])
        with nc.psum_tensor("bc7_ps", [128, 1], F32) as bc7_ps:
            @mk.on("vector")
            def _(e, en):
                mk.wait(e, en, "qG")
                c = mk.sig(e.tensor_scalar(out=row16[:, 56:57], in0=row16[:, 48:49],
                                           scalar1=1.0 / 32767.0, scalar2=1e-12,
                                           op0=ALU.mult, op1=ALU.max), "eV")
                mk.wait(e, en, "eV", c)
                c = mk.sig(e.reciprocal(out=row16[:, 57:58], in_=row16[:, 56:57]), "eV")
                mk.wait(e, en, "eV", c)
            emit_bcast(row16[:, 56:57], 0, bc7_ps[:, 0:1])
            emit_bcast(row16[:, 57:58], 1, bc7_ps[:, 0:1])

        with nc.psum_tensor("d4_ps", [128, 4, 512], F32) as d4_ps, \
             nc.psum_tensor("st4_ps", [128, 2, 512], F32) as st4_ps:
            for t4 in range(T4):
                oq4S0, oq4V0 = C["eS"], C["eV"]

                @mk.on("scalar")
                def _(e, en, t4=t4, oq4V0=oq4V0):
                    for ic in range(IC1):
                        r = ic % 2
                        if ic >= 2:
                            mk.wait(e, en, "eV", oq4V0 + (ic - 1) * 3)
                        mk.sig(e.dma_start(out=w8st[:, r, 0:512],
                                           in_=opre_d.ap()[ic, :, t4 * 512:(t4 + 1) * 512]), "qC", 16)
                        mk.wait(e, en, "qC")
                        mk.sig(e.activation(out=p2a[:, 0, r * 512:r * 512 + 512],
                                            in_=w8st[:, r, 0:512], func=AF.Sign), "eS")

                @mk.on("vector")
                def _(e, en, t4=t4, oq4S0=oq4S0):
                    for ic in range(IC1):
                        r = ic % 2
                        mk.wait(e, en, "eS", oq4S0 + ic + 1)
                        c = mk.sig(e.tensor_scalar_mul(out=p2a[:, 1, r * 512:r * 512 + 512],
                                                       in0=w8st[:, r, 0:512],
                                                       scalar1=scal[:, 1:2]), "eV")
                        mk.wait(e, en, "eV", c)
                        c = mk.sig(e.scalar_tensor_tensor(out=p2b[:, r * 512:r * 512 + 512],
                                                          in0=p2a[:, 0, r * 512:r * 512 + 512],
                                                          scalar=0.0,
                                                          in1=p2a[:, 1, r * 512:r * 512 + 512],
                                                          op0=ALU.mult, op1=ALU.add), "eV")
                        mk.wait(e, en, "eV", c)
                        c = mk.sig(e.tensor_scalar_mul(out=p4h[:, ic, :],
                                                       in0=p2b[:, r * 512:r * 512 + 512],
                                                       scalar1=scal[:, 0:1]), "eV")
                        mk.wait(e, en, "eV", c)
                HQ4_EV = C["eV"]

                d4T0, d4V0, d4QS0 = C["eT"], C["eV"], C["qS"]

                @mk.on("sync")
                def _(e, en, t4=t4, d4T0=d4T0):
                    for ocg in range(2):
                        for ic in range(IC1):
                            r = ic % 2
                            li = ocg * IC1 + ic
                            if li >= 2:
                                mk.wait(e, en, "eT", d4T0 + (li - 1))
                            else:
                                mk.wait(e, en, "eT", d4T0)
                            mk.sig(e.dma_start(out=wTd[:, r, :],
                                               in_=wq1d.ap()[ocg * 512:(ocg + 1) * 512,
                                                             ic * 128:(ic + 1) * 128],
                                               transpose=True), "qS", 16)

                @mk.on("tensor")
                def _(e, en, t4=t4, d4QS0=d4QS0, HQ4_EV=HQ4_EV, d4V0=d4V0):
                    mk.wait(e, en, "eV", HQ4_EV)
                    for ocg in range(2):
                        if ocg == 1:
                            mk.wait(e, en, "eV", d4V0 + 8)
                        for ic in range(IC1):
                            r = ic % 2
                            li = ocg * IC1 + ic
                            mk.wait(e, en, "qS", d4QS0 + (li + 1) * 16)
                            for j in range(4):
                                mm = e.matmul(d4_ps[:, j, :],
                                              wTd[:, r, j * 128:(j + 1) * 128],
                                              p4h[:, ic, :],
                                              start=(ic == 0), stop=(ic == IC1 - 1))
                            mk.sig(mm, "eT")

                @mk.on("vector")
                def _(e, en, t4=t4, d4T0=d4T0):
                    for ocg in range(2):
                        mk.wait(e, en, "eT", d4T0 + (ocg + 1) * IC1)
                        for j in range(4):
                            oc = ocg * 4 + j
                            sl = stream[:, oc, t4 * 512:(t4 + 1) * 512]
                            c = mk.sig(e.tensor_add(out=sl, in0=d4_ps[:, j, :],
                                                    in1=sl), "eV")
                            mk.wait(e, en, "eV", c)
                            mk.sig(e.tensor_copy(out=ybf[:, oc, :], in_=sl), "eV")
                DW4_EV = C["eV"]
                DW4_S0 = C["eS"]

                @mk.on("scalar")
                def _(e, en, t4=t4, d4V0=d4V0):
                    for oc in range(OC):
                        mk.wait(e, en, "eV", d4V0 + oc * 2 + 2)
                        mk.sig(e.activation(out=y2bf[:, oc, :], in_=ybf[:, oc, :],
                                            func=AF.Square), "eS")

                @mk.on("tensor")
                def _(e, en, t4=t4, d4V0=d4V0, DW4_S0=DW4_S0):
                    for oc in range(OC):
                        mk.wait(e, en, "eV", d4V0 + oc * 2 + 2)
                        mm = e.matmul(st4_ps[:, 0, :], ones_blk[:], ybf[:, oc, :],
                                      start=(oc == 0), stop=(oc == OC - 1))
                    mk.sig(mm, "eT")
                    for oc in range(OC):
                        mk.wait(e, en, "eS", DW4_S0 + oc + 1)
                        mm = e.matmul(st4_ps[:, 1, :], ones_blk[:], y2bf[:, oc, :],
                                      start=(oc == 0), stop=(oc == OC - 1))
                    mk.sig(mm, "eT")

                ln4T = C["eT"]

                @mk.on("vector")
                def _(e, en, t4=t4, ln4T=ln4T):
                    mk.wait(e, en, "eT", ln4T)
                    c1 = mk.sig(e.tensor_copy(out=scr[:, 0, :], in_=st4_ps[:, 0, :]), "eV")
                    mk.wait(e, en, "eV", c1)
                    c2 = mk.sig(e.scalar_tensor_tensor(out=scr[:, 1, :], in0=scr[:, 0, :],
                                                       scalar=1.0 / (D * D), in1=scr[:, 0, :],
                                                       op0=ALU.mult, op1=ALU.mult), "eV")
                    mk.wait(e, en, "eV", c2)
                    mk.sig(e.scalar_tensor_tensor(out=scr[:, 2, :], in0=st4_ps[:, 1, :],
                                                  scalar=1.0 / D, in1=scr[:, 1, :],
                                                  op0=ALU.mult, op1=ALU.subtract), "eV")
                LV4 = C["eV"]

                @mk.on("scalar")
                def _(e, en, t4=t4, LV4=LV4):
                    mk.wait(e, en, "eV", LV4)
                    mk.sig(e.activation(out=r_bc[:, 0, :], in_=scr[:, 2, :], func=AF.Sqrt,
                                        bias=epsln_col[:, 0:1]), "eS")
                LS4 = C["eS"]

                @mk.on("vector")
                def _(e, en, t4=t4, LS4=LS4):
                    mk.wait(e, en, "eS", LS4)
                    c1 = mk.sig(e.reciprocal(out=r_bc[:, 1, :], in_=r_bc[:, 0, :]), "eV")
                    mk.wait(e, en, "eV", c1)
                    c2 = mk.sig(e.scalar_tensor_tensor(out=scr[:, 3, :], in0=scr[:, 0, :],
                                                       scalar=1.0 / D, in1=r_bc[:, 1, :],
                                                       op0=ALU.mult, op1=ALU.mult), "eV")
                    mk.wait(e, en, "eV", c2)
                    for oc in range(OC):
                        sl = stream[:, oc, t4 * 512:(t4 + 1) * 512]
                        c3 = mk.sig(e.tensor_mul(out=scr[:, 1, :], in0=sl,
                                                 in1=r_bc[:, 1, :]), "eV")
                        mk.wait(e, en, "eV", c3)
                        c4 = mk.sig(e.tensor_sub(out=sl, in0=scr[:, 1, :],
                                                 in1=scr[:, 3, :]), "eV")
                        mk.wait(e, en, "eV", c4)
            mk.gate()

        mk.emit = True
        @mk.on("scalar")
        def _(e, en):
            for s in SEMS:
                mk.wait(e, en, s)
            for oc in range(OC):
                mk.sig(e.dma_start(out=out.ap()[oc * 128:(oc + 1) * 128, :],
                                   in_=stream[:, oc, :]), "qC", 16)

        @mk.on("scalar")
        def _(e, en):
            mk.wait(e, en, "qC")

    nc.finalize()
    return nc


def kernel(L0_attn_x, L0_attn_Wq, L0_attn_bq, L0_attn_Wk, L0_attn_bk, L0_attn_Wv,
           L0_attn_bv, L0_attn_Wo, L0_attn_bo, L0_ffn_clip_x, L0_ffn_clip_down,
           L0_ffn_w_gate, L0_ffn_w_up, L0_ffn_w_down, L1_attn_Wq, L1_attn_Wk,
           L1_attn_Wv, L1_attn_Wu, L1_attn_bu, L1_ffn_w_gate, L1_ffn_w_up,
           L1_ffn_w_down):
    x = np.ascontiguousarray(np.asarray(L0_attn_x, dtype=np.float32))
    B, N, Dm = x.shape
    tr = lambda w: np.ascontiguousarray(np.asarray(w, dtype=np.float32).T)
    col = lambda b: np.ascontiguousarray(np.asarray(b, dtype=np.float32).reshape(8, 128).T)
    row = lambda b: np.ascontiguousarray(np.asarray(b, dtype=np.float32).reshape(1, -1))
    shared = {
        "wqt0": tr(L0_attn_Wq), "wkt0": tr(L0_attn_Wk), "wvt0": tr(L0_attn_Wv),
        "wot0": tr(L0_attn_Wo),
        "bq0": col(L0_attn_bq), "bk0r": row(L0_attn_bk), "bv0r": row(L0_attn_bv),
        "bo0": col(L0_attn_bo),
        "cx0": np.asarray(L0_ffn_clip_x, dtype=np.float32).reshape(1, 1),
        "cd0": np.asarray(L0_ffn_clip_down, dtype=np.float32).reshape(1, 1),
        "wg0": np.ascontiguousarray(np.asarray(L0_ffn_w_gate, dtype=np.float32)),
        "wu0": np.ascontiguousarray(np.asarray(L0_ffn_w_up, dtype=np.float32)),
        "wd0": np.ascontiguousarray(np.asarray(L0_ffn_w_down, dtype=np.float32)),
        "wqt1": tr(L1_attn_Wq), "wkt1": tr(L1_attn_Wk), "wvt1": tr(L1_attn_Wv),
        "wut1": tr(L1_attn_Wu), "bu1": col(L1_attn_bu),
        "wg1": np.ascontiguousarray(np.asarray(L1_ffn_w_gate, dtype=np.float32)),
        "wup1": np.ascontiguousarray(np.asarray(L1_ffn_w_up, dtype=np.float32)),
        "wd1": np.ascontiguousarray(np.asarray(L1_ffn_w_down, dtype=np.float32)),
    }
    in_maps = []
    for c in range(8):
        b, half = divmod(c, 2)
        m = dict(shared)
        m["xT"] = np.ascontiguousarray(x[b, half * 2048:(half + 1) * 2048, :].T)
        in_maps.append(m)
    nc = _build(_UPTO)
    res = run_bass_kernel_spmd(nc, in_maps, core_ids=list(range(8)))
    outp = np.empty((B, N, Dm), dtype=np.float32)
    for c in range(8):
        b, half = divmod(c, 2)
        outp[b, half * 2048:(half + 1) * 2048, :] = res.results[c]["out"].T
    return outp


# version 05 (final)
# speedup vs baseline: 1.0288x; 1.0288x over previous
import numpy as np
import concourse.bass as bass
import concourse.mybir as mybir
from concourse.bass_utils import run_bass_kernel_spmd

F32 = mybir.dt.float32
BF16 = mybir.dt.bfloat16
I32 = mybir.dt.int32
AF = mybir.ActivationFunctionType
ALU = mybir.AluOpType
AX = mybir.AxisListType

D = 1024
OC = 8
T = 2048
TT = 16
T4 = 4
NH = 16
HD = 64
I0 = 2816
IC0 = 22
I1 = 3072
IC1 = 24
LN_EPS = 1e-5
ATT_EPS = 1e-6
RG_PAIR = [[0, 1], [2, 3], [4, 5], [6, 7]]
RG_ALL = [[0, 1, 2, 3, 4, 5, 6, 7]]

_UPTO = 99

ENGS = ["tensor", "vector", "scalar", "gpsimd", "sync"]
SEMS = ["qS", "qC", "qG", "eT", "eV", "eS", "eG", "cc", "x1", "x2", "x3"]


class MK:
    def __init__(self, nc, block):
        self.nc = nc
        self.block = block
        self.emit = True
        self.sems = {}
        self.ctr = {}
        self.waited = {}
        for s in SEMS:
            self.sems[s] = nc.alloc_semaphore(s)
            self.ctr[s] = 0

    def sig(self, inst, name, by=1):
        inst.then_inc(self.sems[name], by)
        self.ctr[name] += by
        return self.ctr[name]

    def wait(self, eng, engname, name, val=None):
        if val is None:
            val = self.ctr[name]
        if val <= 0:
            return
        key = (engname, name)
        if self.waited.get(key, 0) >= val:
            return
        eng.wait_ge(self.sems[name], val)
        self.waited[key] = val

    def on(self, engname):
        def run(f):
            if not self.emit:
                return

            def wrapped(e):
                n0 = self.nc.n_instructions()
                f(e, engname)
                if self.nc.n_instructions() == n0:
                    e.nop()
            getattr(self.block, engname)(wrapped)
        return run

    def gate(self):
        if not self.emit:
            return
        snap = dict(self.ctr)
        for en in ENGS:
            @self.on(en)
            def _(e, en):
                for s in SEMS:
                    self.wait(e, en, s, snap[s])


def _build(upto=4):
    nc = bass.Bass("TRN2", target_bir_lowering=False, debug=False,
                   detect_race_conditions=False)

    xT = nc.dram_tensor("xT", [D, T], F32, kind="ExternalInput")
    out = nc.dram_tensor("out", [D, T], F32, kind="ExternalOutput")
    wqt0 = nc.dram_tensor("wqt0", [D, D], F32, kind="ExternalInput")
    wkt0 = nc.dram_tensor("wkt0", [D, D], F32, kind="ExternalInput")
    wvt0 = nc.dram_tensor("wvt0", [D, D], F32, kind="ExternalInput")
    wot0 = nc.dram_tensor("wot0", [D, D], F32, kind="ExternalInput")
    bq0 = nc.dram_tensor("bq0", [128, 8], F32, kind="ExternalInput")
    bk0r = nc.dram_tensor("bk0r", [1, D], F32, kind="ExternalInput")
    bv0r = nc.dram_tensor("bv0r", [1, D], F32, kind="ExternalInput")
    bo0 = nc.dram_tensor("bo0", [128, 8], F32, kind="ExternalInput")
    cx0 = nc.dram_tensor("cx0", [1, 1], F32, kind="ExternalInput")
    cd0 = nc.dram_tensor("cd0", [1, 1], F32, kind="ExternalInput")
    wg0 = nc.dram_tensor("wg0", [I0, D], F32, kind="ExternalInput")
    wu0 = nc.dram_tensor("wu0", [I0, D], F32, kind="ExternalInput")
    wd0 = nc.dram_tensor("wd0", [D, I0], F32, kind="ExternalInput")
    wqt1 = nc.dram_tensor("wqt1", [D, D], F32, kind="ExternalInput")
    wkt1 = nc.dram_tensor("wkt1", [D, D], F32, kind="ExternalInput")
    wvt1 = nc.dram_tensor("wvt1", [D, D], F32, kind="ExternalInput")
    wut1 = nc.dram_tensor("wut1", [D, D], F32, kind="ExternalInput")
    bu1 = nc.dram_tensor("bu1", [128, 8], F32, kind="ExternalInput")
    wg1 = nc.dram_tensor("wg1", [I1, D], F32, kind="ExternalInput")
    wup1 = nc.dram_tensor("wup1", [I1, D], F32, kind="ExternalInput")
    wd1 = nc.dram_tensor("wd1", [D, I1], F32, kind="ExternalInput")

    ctx_bin = nc.dram_tensor("ctx_bin", [128, 8 * 65], F32)
    ctx_bout = nc.dram_tensor("ctx_bout", [128, 8 * 65], F32)
    amax_in = [nc.dram_tensor(f"amax_in{i}", [1, 8], F32) for i in range(8)]
    amax_out = [nc.dram_tensor(f"amax_out{i}", [1, 8], F32) for i in range(8)]
    w8g_d = nc.dram_tensor("w8g_d", [I0, D], BF16)
    w8u_d = nc.dram_tensor("w8u_d", [I0, D], BF16)
    w8d_d = nc.dram_tensor("w8d_d", [D, I0], BF16)
    wq1g = nc.dram_tensor("wq1g", [I1, D], BF16)
    wq1u = nc.dram_tensor("wq1u", [I1, D], BF16)
    wq1d = nc.dram_tensor("wq1d", [D, I1], BF16)
    hpre_d = nc.dram_tensor("hpre_d", [IC0, 128, T], F32)
    kag_in = nc.dram_tensor("kag_in", [D, T], BF16)
    kag_out = nc.dram_tensor("kag_out", [2 * D, T], BF16)
    vag_in = nc.dram_tensor("vag_in", [NH, T, HD], BF16)
    vag_out = nc.dram_tensor("vag_out", [2, NH, T, HD], BF16)
    xspill = nc.dram_tensor("xspill", [D, T], F32)
    graw_d = nc.dram_tensor("graw_d", [IC1, 128, T], BF16)
    uraw_d = nc.dram_tensor("uraw_d", [IC1, 128, T], BF16)
    gs_d = nc.dram_tensor("gs_d", [IC1, 128, T], BF16)
    opre_d = nc.dram_tensor("opre_d", [IC1, 128, T], BF16)

    from contextlib import ExitStack
    es = ExitStack()
    sbp = lambda name, shape, dt: es.enter_context(nc.sbuf_tensor(name, shape, dt))

    ones_blk = sbp("ones_blk", [128, 128], BF16)
    identf = sbp("identf", [128, 128], F32)
    ones_r1 = sbp("ones_r1", [1, 128], BF16)
    ones_r1f = sbp("ones_r1f", [1, 128], F32)
    eps_r512 = sbp("eps_r512", [1, 512], BF16)
    epsln_col = sbp("epsln_col", [128, 1], F32)
    blockmask = sbp("blockmask", [128, 128], BF16)
    bq0_sb = sbp("bq0_sb", [128, 8], F32)
    bo0_sb = sbp("bo0_sb", [128, 8], F32)
    bu1_sb = sbp("bu1_sb", [128, 8], F32)
    bkv_sb = sbp("bkv_sb", [1, 2 * D], BF16)
    sig_cx = sbp("sig_cx", [1, 1], F32)
    sig_cd = sbp("sig_cd", [1, 1], F32)
    row16 = sbp("row16", [1, 64], F32)
    scal = sbp("scal", [128, 8], F32)
    wsg_sb = sbp("wsg_sb", [128, IC0], F32)
    wsu_sb = sbp("wsu_sb", [128, IC0], F32)
    wsd_sb = sbp("wsd_sb", [128, OC], F32)
    csg_sb = sbp("csg_sb", [128, IC0], F32)
    csu_sb = sbp("csu_sb", [128, IC0], F32)
    csd_sb = sbp("csd_sb", [128, OC], F32)
    maxc_sb = sbp("maxc_sb", [128, 36], F32)

    def _align(v, a=64):
        return (v + a - 1) // a * a

    MB = _align(nc.sbuf_base)
    A0 = MB
    B0 = A0 + 65536
    C0 = B0 + 32768
    D0 = C0 + 32768
    E0 = D0 + 16384
    ETOP = nc.sbuf_top
    assert ETOP - E0 >= 50 * 1024, f"E region too small: {ETOP - E0}"

    def at(off, name, shape, dt):
        h = nc.alloc_sbuf_tensor_at(name, shape, dt, offset=off)
        return h

    stream = at(A0, "stream", [128, OC, T], F32)
    w2 = at(A0, "w2", [128, 2, OC, D], BF16)
    kv_t = at(A0 + 32768, "kv_t", [128, 2, 2, NH * 65], BF16)
    scr2 = at(A0 + 41120, "scr2", [128, 2, D], BF16)
    xbf = at(B0, "xbf", [128, OC, T], BF16)
    qbf = at(C0, "qbf", [128, OC, T], BF16)
    wo_sb = at(D0, "wo_sb", [128, OC, D], BF16)
    ctx_sb = at(E0, "ctx_sb", [128, 8, 65], F32)
    att_o = at(E0 + 2112, "att_o", [128, OC, 512], BF16)
    mtile = at(E0 + 10304, "mtile", [128, 2, 128], BF16)
    ctx_bf = at(E0 + 10816, "ctx_bf", [128, 8, 64], BF16)
    scr = at(E0 + 11840, "scr", [128, 4, 512], F32)
    dinvs = at(E0 + 11840, "dinvs", [128, 2, 512], BF16)
    r_bc = at(E0 + 20032, "r_bc", [128, 2, 512], F32)
    ybf = at(E0 + 24128, "ybf", [128, OC, 512], BF16)
    y2bf = at(E0 + 32320, "y2bf", [128, OC, 512], BF16)
    xres = at(E0 + 40512, "xres", [128, OC, 512], F32)
    assert E0 + 40512 + 16384 <= ETOP, (E0, ETOP)
    p2a = at(E0 + 40512, "p2a", [128, 2, T], F32)
    p2b = at(E0, "p2b", [128, T], I32)
    p2bh = at(E0, "p2bh", [128, 2, T], BF16)
    xq = at(B0, "xq", [128, OC, T], BF16)
    hintC = at(C0, "hintC", [128, IC0, 512], BF16)
    wrawA = at(D0, "wrawA", [128, 2, D], F32)
    w8st = at(D0 + 8192, "w8st", [128, 2, D], BF16)
    wTgu = at(D0, "wTgu", [128, 2, OC, 512], BF16)
    wTd = at(D0, "wTd", [128, 2, 512], BF16)
    wrawD = at(D0, "wrawD", [128, I0], F32)
    p3att = at(A0, "p3att", [128, OC, T], BF16)
    p4i = at(E0 + 24128, "p4i", [128, 3072], I32)
    p4h = at(C0 + 8192, "p4h", [128, IC1, 512], BF16)
    p4st = at(C0, "p4st", [128, 2, T], BF16)
    p3k = at(A0 + 32768, "p3k", [128, OC, T], BF16)
    kh = at(A0 + 32768, "kh", [128, 2, T], BF16)
    vh = at(A0 + 40960, "vh", [128, 32, 65], BF16)
    pt = at(A0 + 49344, "pt", [128, 2, 1024], BF16)
    strip = at(A0 + 53440, "strip", [64, T], BF16)
    vstg = at(A0 + 57536, "vstg", [128, 2, D], BF16)
    p2af = at(E0 + 40512, "p2af", [128, 2 * T], F32)
    p2i2 = at(E0 + 24128, "p2i2", [128, I0], I32)
    p2bh4 = at(E0, "p2bh4", [128, 2 * T], BF16)

    with nc.allow_low_precision("bf16 matmul operands / intermediates; tolerance is gated by the harness"), \
         nc.Block() as block:
        mk = MK(nc, block)
        C = mk.ctr
        
        def emit_armax(idx, src_col, out_col):
            with nc.psum_tensor(f"amx_ps{idx}", [1, 128], F32) as aps:
                @mk.on("tensor")
                def _(e, en):
                    for s in ("eV", "eS"):
                        mk.wait(e, en, s)
                    mk.sig(e.transpose(aps[0:1, :], src_col, identf[:]), "eT")

                @mk.on("vector")
                def _(e, en):
                    mk.wait(e, en, "eT")
                    c = mk.sig(e.reduce_max(out=row16[:, idx * 8:idx * 8 + 1],
                                            in_=aps[0:1, :], axis=AX.X), "eV")
                    mk.wait(e, en, "eV", c)
                    mk.sig(e.tensor_copy(out=row16[:, idx * 8:idx * 8 + 8],
                                         in_=row16[:, idx * 8:idx * 8 + 1].broadcast_to((1, 8))), "eV")

                @mk.on("gpsimd")
                def _(e, en):
                    mk.wait(e, en, "eV")
                    mk.sig(e.dma_start(out=amax_in[idx].ap(), in_=row16[:, idx * 8:idx * 8 + 8]), "qG", 16)
                    mk.wait(e, en, "qG")
                    mk.sig(e.collective_compute(
                        "AllReduce", ALU.max, replica_groups=RG_ALL,
                        ins=[amax_in[idx].ap().opt()], outs=[amax_out[idx].ap().opt()]), "cc")
                    mk.wait(e, en, "cc")
                    mk.sig(e.dma_start(out=row16[:, idx * 8:idx * 8 + 8], in_=amax_out[idx].ap()), "qG", 16)
                mk.gate()

        def emit_bcast(val_ap, slot, ps_ap):
            @mk.on("tensor")
            def _(e, en):
                mk.wait(e, en, "eV")
                mk.wait(e, en, "qG")
                mk.sig(e.matmul(ps_ap, ones_r1f[:, 0:128], val_ap, start=True, stop=True), "eT")

            @mk.on("vector")
            def _(e, en):
                mk.wait(e, en, "eT")
                mk.sig(e.tensor_copy(out=scal[:, slot:slot + 1], in_=ps_ap), "eV")
            mk.gate()

        @mk.on("sync")
        def _(e, en):
            mk.sig(e.dma_start(out=bq0_sb[:], in_=bq0.ap()), "qS", 16)
            mk.sig(e.dma_start(out=bo0_sb[:], in_=bo0.ap()), "qS", 16)
            mk.sig(e.dma_start(out=bu1_sb[:], in_=bu1.ap()), "qS", 16)
            mk.sig(e.dma_start(out=sig_cx[:], in_=cx0.ap()), "qS", 16)
            mk.sig(e.dma_start(out=sig_cd[:], in_=cd0.ap()), "qS", 16)

        @mk.on("gpsimd")
        def _(e, en):
            mk.sig(e.dma_start(out=bkv_sb[:, 0:D], in_=bk0r.ap()), "qG", 16)
            mk.sig(e.dma_start(out=bkv_sb[:, D:2 * D], in_=bv0r.ap()), "qG", 16)
            for oc in range(OC):
                mk.sig(e.dma_start(out=xbf[:, oc, :],
                                   in_=xT.ap()[oc * 128:(oc + 1) * 128, :]), "qG", 16)
            for fc in range(OC):
                mk.sig(e.dma_start(out=w2[:, 0, fc, :],
                                   in_=wqt0.ap()[fc * 128:(fc + 1) * 128, :]), "qG", 16)
        P0_QG = C["qG"]

        @mk.on("vector")
        def _(e, en):
            e.memset(ones_blk[:], 1.0)
            e.memset(identf[:], 1.0)
            e.memset(ones_r1[:], 1.0)
            e.memset(ones_r1f[:], 1.0)
            e.memset(eps_r512[:], ATT_EPS)
            e.memset(epsln_col[:], LN_EPS)
            e.memset(blockmask[:], 0.0)
            e.memset(blockmask[0:64, 0:64], 1.0)
            mk.sig(e.memset(blockmask[64:128, 64:128], 1.0), "eV")

        @mk.on("gpsimd")
        def _(e, en):
            mk.wait(e, en, "eV")
            mk.sig(e.affine_select(identf[:], identf[:], [[1, 128]], ALU.is_equal, 0.0,
                                   base=0, channel_multiplier=-1), "eG")

        @mk.on("scalar")
        def _(e, en):
            mk.wait(e, en, "qS")
            e.activation(out=sig_cx[:], in_=sig_cx[:], func=AF.Sigmoid)
            mk.sig(e.activation(out=sig_cd[:], in_=sig_cd[:], func=AF.Sigmoid), "eS")

        mk.gate()

        mk.emit = (upto >= 1)
        with nc.psum_tensor("q_ps", [128, 2, 512], F32) as q_ps:
            NQT = OC * T4
            qT0, qV0, qS0 = C["eT"], C["eV"], C["eS"]

            @mk.on("tensor")
            def _(e, en):
                mk.wait(e, en, "qG", P0_QG)
                for i in range(NQT):
                    oc, t4 = divmod(i, T4)
                    r = i % 2
                    if i >= 2:
                        mk.wait(e, en, "eV", qV0 + (i - 1) * 2)
                    for fc in range(OC):
                        mm = e.matmul(q_ps[:, r, :],
                                      w2[:, 0, fc, oc * 128:(oc + 1) * 128],
                                      xbf[:, fc, t4 * 512:(t4 + 1) * 512],
                                      start=(fc == 0), stop=(fc == OC - 1))
                    mk.sig(mm, "eT")

            @mk.on("scalar")
            def _(e, en):
                for i in range(NQT):
                    oc, t4 = divmod(i, T4)
                    r = i % 2
                    mk.wait(e, en, "eT", qT0 + i + 1)
                    if i >= 1:
                        mk.wait(e, en, "eV", qV0 + i * 2)
                    mk.sig(e.activation(out=scr[:, 2, :], in_=q_ps[:, r, :], func=AF.Relu,
                                        bias=bq0_sb[:, oc:oc + 1]), "eS")
                    mk.wait(e, en, "eV", qV0 + i * 2 + 1)
                    mk.sig(e.activation(out=scr[:, 1, :], in_=scr[:, 0, :], func=AF.Exp), "eS")

            @mk.on("vector")
            def _(e, en):
                for i in range(NQT):
                    oc, t4 = divmod(i, T4)
                    r = i % 2
                    mk.wait(e, en, "eT", qT0 + i + 1)
                    mk.wait(e, en, "eS", qS0 + i * 2 + 1)
                    mk.sig(e.tensor_scalar(out=scr[:, 0, :], in0=q_ps[:, r, :],
                                           scalar1=bq0_sb[:, oc:oc + 1], scalar2=0.0,
                                           op0=ALU.add, op1=ALU.min), "eV")
                    mk.wait(e, en, "eS", qS0 + (i + 1) * 2)
                    mk.sig(e.tensor_add(out=qbf[:, oc, t4 * 512:(t4 + 1) * 512],
                                        in0=scr[:, 1, :], in1=scr[:, 2, :]), "eV")
            assert not mk.emit or (C["eV"] == qV0 + NQT * 2 and C["eS"] == qS0 + NQT * 2)
            mk.gate()

        mk.emit = (upto >= 2)
        @mk.on("gpsimd")
        def _(e, en):
            for j, w in enumerate([wkt0, wvt0]):
                for fc in range(OC):
                    mk.sig(e.dma_start(out=w2[:, j, fc, :],
                                       in_=w.ap()[fc * 128:(fc + 1) * 128, :]), "qG", 16)
            for fc in range(OC):
                mk.sig(e.dma_start(out=wo_sb[:, fc, :],
                                   in_=wot0.ap()[fc * 128:(fc + 1) * 128, :]), "qG", 16)
        P1_WKV = C["qG"]

        with nc.psum_tensor("kv_ps", [128, 2, D], F32) as kv_ps, \
             nc.psum_tensor("ctx_ps", [128, 2, 512], F32) as ctx_ps:

            @mk.on("vector")
            def _(e, en):
                e.memset(ctx_sb[:, :, :].rearrange("p a b -> p (a b)"), 0.0)
                for r in range(2):
                    mk.sig(e.memset(
                        kv_t[:, r, 1, :].rearrange("p (h e) -> p h e", e=65)[:, :, 64:65],
                        1.0), "eV")
            KV1_EV = C["eV"]
            kT0, kS0 = C["eT"], C["eS"]
            EVTT, ESTT, ETTT = 3, 2, 3

            @mk.on("tensor")
            def _(e, en):
                mk.wait(e, en, "eV", KV1_EV)
                mk.wait(e, en, "qG", P1_WKV)
                for tt in range(TT):
                    r = tt % 2
                    if tt >= 2:
                        mk.wait(e, en, "eV", KV1_EV + (tt - 1) * EVTT)
                    for j in (0, 1):
                        dst = kv_ps[:, j, :]
                        for fc in range(OC):
                            for o2 in range(2):
                                e.matmul(dst[:, o2 * 512:(o2 + 1) * 512],
                                         xbf[:, fc, tt * 128:(tt + 1) * 128],
                                         w2[:, j, fc, o2 * 512:(o2 + 1) * 512],
                                         start=(fc == 0), stop=False)
                        for o2 in range(2):
                            mm = e.matmul(dst[:, o2 * 512:(o2 + 1) * 512],
                                          ones_r1[:, 0:128],
                                          bkv_sb[:, j * D + o2 * 512:j * D + (o2 + 1) * 512],
                                          start=False, stop=True)
                        mk.sig(mm, "eT")
                    mk.wait(e, en, "eV", KV1_EV + (tt + 1) * EVTT)
                    if tt >= 1:
                        mk.wait(e, en, "x1", tt)
                    for h in range(NH):
                        hp, sub = divmod(h, 2)
                        bank = hp // 4
                        col = (hp % 4) * 65
                        mm = e.matmul(ctx_ps[sub * 64:(sub + 1) * 64, bank, col:col + 65],
                                      kv_t[:, r, 0, h * 64:(h + 1) * 64],
                                      kv_t[:, r, 1, h * 65:(h + 1) * 65],
                                      start=True, stop=True,
                                      tile_position=(0, 64 * sub))
                    mk.sig(mm, "eT")

            @mk.on("scalar")
            def _(e, en):
                for tt in range(TT):
                    mk.wait(e, en, "eT", kT0 + tt * ETTT + 2)
                    if tt >= 1:
                        mk.wait(e, en, "eV", KV1_EV + (tt - 1) * EVTT + 2)
                    mk.sig(e.activation(out=scr2[:, 1, :], in_=kv_ps[:, 0, :],
                                        func=AF.Relu), "eS")
                    mk.wait(e, en, "eV", KV1_EV + tt * EVTT + 1)
                    mk.sig(e.activation(out=scr2[:, 0, :], in_=scr2[:, 0, :],
                                        func=AF.Exp), "eS")

            @mk.on("vector")
            def _(e, en):
                for tt in range(TT):
                    r = tt % 2
                    mk.wait(e, en, "eT", kT0 + tt * ETTT + 2)
                    mk.wait(e, en, "eS", kS0 + tt * ESTT + 1)
                    mk.sig(e.tensor_scalar_min(out=scr2[:, 0, :], in0=kv_ps[:, 0, :],
                                               scalar1=0.0), "eV")
                    mk.wait(e, en, "eS", kS0 + (tt + 1) * ESTT)
                    mk.sig(e.tensor_add(out=kv_t[:, r, 0, 0:D],
                                        in0=scr2[:, 0, :], in1=scr2[:, 1, :]), "eV")
                    mk.sig(e.tensor_copy(
                        out=kv_t[:, r, 1, :].rearrange("p (h e) -> p h e", e=65)[:, :, 0:64],
                        in_=kv_ps[:, 1, :].rearrange("p (h e) -> p h e", e=64)), "eV")
                    if tt >= 1:
                        mk.wait(e, en, "eT", kT0 + tt * ETTT)
                        cv = ctx_sb[:, :, :].rearrange("p (g h) e -> p g (h e)", g=2)
                        c1 = mk.sig(e.tensor_add(
                            out=cv, in0=cv, in1=ctx_ps[:, :, 0:260]), "x1")
                        e.wait_ge(mk.sems["x1"], c1)
                mk.wait(e, en, "eT", kT0 + TT * ETTT)
                cv = ctx_sb[:, :, :].rearrange("p (g h) e -> p g (h e)", g=2)
                mk.sig(e.tensor_add(out=cv, in0=cv, in1=ctx_ps[:, :, 0:260]), "x1")
            assert not mk.emit or (C["eV"] == KV1_EV + TT * EVTT and C["x1"] == TT)

            mk.gate()

            mk.emit = (upto >= 3)
            @mk.on("gpsimd")
            def _(e, en):
                mk.wait(e, en, "eV")
                mk.sig(e.dma_start(out=ctx_bin.ap(),
                                   in_=ctx_sb[:, :, :].rearrange("p a b -> p (a b)")), "qG", 16)
                mk.wait(e, en, "qG")
                mk.sig(e.collective_compute(
                    "AllReduce", ALU.add, replica_groups=RG_PAIR,
                    ins=[ctx_bin.ap().opt()], outs=[ctx_bout.ap().opt()]), "cc")
                mk.wait(e, en, "cc")
                mk.sig(e.dma_start(out=ctx_sb[:, :, :].rearrange("p a b -> p (a b)"),
                                   in_=ctx_bout.ap()), "qG", 16)

            @mk.on("vector")
            def _(e, en):
                mk.wait(e, en, "qG")
                mk.sig(e.tensor_copy(out=ctx_bf[:, :, :], in_=ctx_sb[:, :, 0:64]), "eV")
            mk.gate()

        mk.emit = (upto >= 4)
        with nc.psum_tensor("num_ps", [128, 2, 512], F32) as num_ps, \
             nc.psum_tensor("den_ps", [128, 2, 512], F32) as den_ps, \
             nc.psum_tensor("wo_ps", [128, 2, 512], F32) as wo_ps, \
             nc.psum_tensor("st_ps", [128, 2, 512], F32) as st_ps:

            for t4 in range(T4):
                t4T0, t4V0 = C["eT"], C["eV"]

                @mk.on("sync")
                def _(e, en, t4=t4, t4V0=t4V0):
                    if t4 >= 1:
                        mk.wait(e, en, "eV", t4V0)
                    for oc in range(OC):
                        mk.sig(e.dma_start(out=xres[:, oc, :],
                                           in_=xT.ap()[oc * 128:(oc + 1) * 128,
                                                       t4 * 512:(t4 + 1) * 512]), "qS", 16)
                XRES_QS = C["qS"]

                @mk.on("vector")
                def _(e, en, t4=t4, t4T0=t4T0):
                    for hp in range(8):
                        rr = hp % 2
                        mk.wait(e, en, "eT", t4T0 + (hp - 1) * 2 if hp >= 2 else t4T0)
                        mk.sig(e.tensor_scalar_mul(out=mtile[:, rr, :], in0=blockmask[:],
                                                   scalar1=ctx_sb[:, hp, 64:65]), "eV")
                        mk.wait(e, en, "eT", t4T0 + (hp + 1) * 2)
                        mk.sig(e.reciprocal(out=dinvs[:, rr, :],
                                            in_=den_ps[:, rr, :]), "eV")
                        mk.sig(e.tensor_mul(out=att_o[:, hp, :],
                                            in0=num_ps[:, rr, :], in1=dinvs[:, rr, :]), "eV")

                @mk.on("tensor")
                def _(e, en, t4=t4, t4V0=t4V0):
                    for hp in range(8):
                        rr = hp % 2
                        mk.wait(e, en, "eV", t4V0 + hp * 3 + 1)
                        for sub in range(2):
                            mm = e.matmul(num_ps[sub * 64:(sub + 1) * 64, rr, :],
                                          ctx_bf[sub * 64:(sub + 1) * 64, hp, :],
                                          qbf[sub * 64:(sub + 1) * 64, hp,
                                              t4 * 512:(t4 + 1) * 512],
                                          start=True, stop=True,
                                          tile_position=(64 * sub, 64 * sub))
                        mk.sig(mm, "eT")
                        e.matmul(den_ps[:, rr, :], mtile[:, rr, :],
                                 qbf[:, hp, t4 * 512:(t4 + 1) * 512],
                                 start=True, stop=False)
                        mm = e.matmul(den_ps[:, rr, :], ones_r1[:, 0:128],
                                      eps_r512[:], start=False, stop=True)
                        mk.sig(mm, "eT")
                ATT_EV = t4V0 + 24

                woT0, woV0, woS0 = C["eT"], C["eV"], C["eS"]

                @mk.on("tensor")
                def _(e, en, t4=t4, woV0=woV0, woS0=woS0, ATT_EV=ATT_EV):
                    mk.wait(e, en, "eV", ATT_EV)
                    for oc in range(OC):
                        rr = oc % 2
                        if oc >= 2:
                            mk.wait(e, en, "eV", woV0 + (oc - 1) * 2)
                        for hp in range(8):
                            mm = e.matmul(wo_ps[:, rr, :],
                                          wo_sb[:, hp, oc * 128:(oc + 1) * 128],
                                          att_o[:, hp, :],
                                          start=(hp == 0), stop=(hp == 7))
                        mk.sig(mm, "eT")
                    for oc in range(OC):
                        mk.wait(e, en, "eV", woV0 + oc * 2 + 2)
                        mm = e.matmul(st_ps[:, 0, :], ones_blk[:], ybf[:, oc, :],
                                      start=(oc == 0), stop=(oc == OC - 1))
                    mk.sig(mm, "eT")
                    for oc in range(OC):
                        mk.wait(e, en, "eS", woS0 + oc + 1)
                        mm = e.matmul(st_ps[:, 1, :], ones_blk[:], y2bf[:, oc, :],
                                      start=(oc == 0), stop=(oc == OC - 1))
                    mk.sig(mm, "eT")

                @mk.on("vector")
                def _(e, en, t4=t4, woT0=woT0, XRES_QS=XRES_QS):
                    mk.wait(e, en, "qS", XRES_QS)
                    for oc in range(OC):
                        rr = oc % 2
                        mk.wait(e, en, "eT", woT0 + oc + 1)
                        mk.sig(e.scalar_tensor_tensor(
                            out=stream[:, oc, t4 * 512:(t4 + 1) * 512],
                            in0=wo_ps[:, rr, :], scalar=bo0_sb[:, oc:oc + 1],
                            in1=xres[:, oc, :],
                            op0=ALU.add, op1=ALU.add), "eV")
                        mk.sig(e.tensor_copy(out=ybf[:, oc, :],
                                             in_=stream[:, oc, t4 * 512:(t4 + 1) * 512]), "eV")

                @mk.on("scalar")
                def _(e, en, t4=t4, woV0=woV0):
                    for oc in range(OC):
                        mk.wait(e, en, "eV", woV0 + oc * 2 + 2)
                        mk.sig(e.activation(out=y2bf[:, oc, :], in_=ybf[:, oc, :],
                                            func=AF.Square), "eS")

                lnT = C["eT"]

                @mk.on("vector")
                def _(e, en, t4=t4, lnT=lnT):
                    mk.wait(e, en, "eT", lnT)
                    c1 = mk.sig(e.tensor_copy(out=scr[:, 0, :], in_=st_ps[:, 0, :]), "eV")
                    mk.wait(e, en, "eV", c1)
                    c2 = mk.sig(e.scalar_tensor_tensor(out=scr[:, 1, :], in0=scr[:, 0, :],
                                                       scalar=1.0 / (D * D), in1=scr[:, 0, :],
                                                       op0=ALU.mult, op1=ALU.mult), "eV")
                    mk.wait(e, en, "eV", c2)
                    mk.sig(e.scalar_tensor_tensor(out=scr[:, 2, :], in0=st_ps[:, 1, :],
                                                  scalar=1.0 / D, in1=scr[:, 1, :],
                                                  op0=ALU.mult, op1=ALU.subtract), "eV")
                VE_EV = C["eV"]

                @mk.on("scalar")
                def _(e, en, t4=t4, VE_EV=VE_EV):
                    mk.wait(e, en, "eV", VE_EV)
                    mk.sig(e.activation(out=r_bc[:, 0, :], in_=scr[:, 2, :], func=AF.Sqrt,
                                        bias=epsln_col[:, 0:1]), "eS")
                SQ_ES = C["eS"]

                @mk.on("vector")
                def _(e, en, t4=t4, SQ_ES=SQ_ES):
                    mk.wait(e, en, "eS", SQ_ES)
                    c1 = mk.sig(e.reciprocal(out=r_bc[:, 1, :], in_=r_bc[:, 0, :]), "eV")
                    mk.wait(e, en, "eV", c1)
                    c2 = mk.sig(e.scalar_tensor_tensor(out=scr[:, 3, :], in0=scr[:, 0, :],
                                                       scalar=1.0 / D, in1=r_bc[:, 1, :],
                                                       op0=ALU.mult, op1=ALU.mult), "eV")
                    mk.wait(e, en, "eV", c2)
                    for oc in range(OC):
                        sl = stream[:, oc, t4 * 512:(t4 + 1) * 512]
                        c3 = mk.sig(e.tensor_mul(out=scr[:, 1, :], in0=sl,
                                                 in1=r_bc[:, 1, :]), "eV")
                        mk.wait(e, en, "eV", c3)
                        c4 = mk.sig(e.tensor_sub(out=sl, in0=scr[:, 1, :],
                                                 in1=scr[:, 3, :]), "eV")
                        mk.wait(e, en, "eV", c4)
                        mk.sig(e.reduce_max(out=maxc_sb[:, t4 * OC + oc:t4 * OC + oc + 1],
                                            in_=sl, axis=AX.X,
                                            apply_absolute_value=True), "eV")
            mk.gate()

        mk.emit = (upto >= 5)

        @mk.on("vector")
        def _(e, en):
            c1 = mk.sig(e.reduce_max(out=maxc_sb[:, 32:33], in_=maxc_sb[:, 0:32],
                                     axis=AX.X), "eV")
            mk.wait(e, en, "eV", c1)
        emit_armax(0, maxc_sb[:, 32:33], maxc_sb[:, 33:34])

        mk.emit = (upto >= 6)

        with nc.psum_tensor("bc_ps", [128, 1], F32) as bc_ps:
            @mk.on("vector")
            def _(e, en):
                mk.wait(e, en, "qG")
                c = mk.sig(e.tensor_mul(out=row16[:, 56:57], in0=row16[:, 0:1], in1=sig_cx[:]), "eV")
                mk.wait(e, en, "eV", c)
                c = mk.sig(e.tensor_scalar_mul(out=row16[:, 57:58], in0=row16[:, 56:57],
                                               scalar1=1.0 / 127.0), "eV")
                mk.wait(e, en, "eV", c)
                c = mk.sig(e.reciprocal(out=row16[:, 58:59], in_=row16[:, 57:58]), "eV")
                mk.wait(e, en, "eV", c)
            emit_bcast(row16[:, 57:58], 0, bc_ps[:, 0:1])
            emit_bcast(row16[:, 58:59], 1, bc_ps[:, 0:1])

            def prep_weight(raw, dst, ws, nchunk, cols, halves=1):
                for ic in range(nchunk):
                    r = ic % 2

                    @mk.on("sync")
                    def _(e, en, ic=ic, r=r):
                        mk.wait(e, en, "qC")
                        for hf in range(halves):
                            mk.sig(e.dma_start(
                                out=wrawA[:, r, hf * cols:(hf + 1) * cols],
                                in_=raw.ap()[ic * 128:(ic + 1) * 128,
                                             hf * cols:(hf + 1) * cols]), "qS", 16)

                    @mk.on("vector")
                    def _(e, en, ic=ic, r=r):
                        mk.wait(e, en, "qS")
                        c = mk.sig(e.reduce_max(out=ws[:, ic:ic + 1],
                                                in_=wrawA[:, r, 0:halves * cols],
                                                axis=AX.X, apply_absolute_value=True), "eV")
                        mk.wait(e, en, "eV", c)
                        c = mk.sig(e.reciprocal(out=scal[:, 2:3], in_=ws[:, ic:ic + 1]), "eV")
                        mk.wait(e, en, "eV", c)
                        c = mk.sig(e.tensor_scalar_mul(out=scal[:, 3:4], in0=scal[:, 2:3],
                                                       scalar1=127.0), "eV")
                        mk.wait(e, en, "eV", c)
                        c = mk.sig(e.tensor_scalar_mul(out=wrawA[:, r, 0:halves * cols],
                                                       in0=wrawA[:, r, 0:halves * cols],
                                                       scalar1=scal[:, 3:4]), "eV")
                        mk.wait(e, en, "eV", c)

                    @mk.on("scalar")
                    def _(e, en, ic=ic, r=r):
                        mk.wait(e, en, "qS")
                        mk.sig(e.activation(out=p2a[:, 0, 0:1],
                                            in_=wrawA[:, r, 0:1],
                                            func=AF.Sign), "eS")

                    @mk.on("vector")
                    def _(e, en, ic=ic, r=r):
                        mk.wait(e, en, "eS")
                        c = mk.sig(e.scalar_tensor_tensor(
                            out=p2b[:, 0:halves * cols].bitcast(I32) if False else p2b[:, 0:halves * cols],
                            in0=p2a[:, 0, 0:1].broadcast_to((128, halves * cols)), scalar=0.0,
                            in1=wrawA[:, r, 0:halves * cols],
                            op0=ALU.mult, op1=ALU.add), "eV")
                        mk.wait(e, en, "eV", c)
                        c = mk.sig(e.tensor_copy(out=w8st[:, r, 0:halves * cols],
                                                 in_=p2b[:, 0:halves * cols]), "eV")
                        mk.wait(e, en, "eV", c)

                    @mk.on("scalar")
                    def _(e, en, ic=ic, r=r):
                        mk.wait(e, en, "eV")
                        for hf in range(halves):
                            mk.sig(e.dma_start(
                                out=dst.ap()[ic * 128:(ic + 1) * 128,
                                             hf * cols:(hf + 1) * cols],
                                in_=w8st[:, r, hf * cols:(hf + 1) * cols]), "qC", 16)

            prep_weight(wg0, w8g_d, wsg_sb, IC0, D, 1)
            prep_weight(wu0, w8u_d, wsu_sb, IC0, D, 1)
            mk.gate()

            for oc in range(OC):
                @mk.on("sync")
                def _(e, en, oc=oc):
                    mk.wait(e, en, "qC")
                    mk.wait(e, en, "eV")
                    mk.sig(e.dma_start(out=wrawD[:, :],
                                       in_=wd0.ap()[oc * 128:(oc + 1) * 128, :]), "qS", 16)

                @mk.on("vector")
                def _(e, en, oc=oc):
                    mk.wait(e, en, "qS")
                    c = mk.sig(e.reduce_max(out=wsd_sb[:, oc:oc + 1], in_=wrawD[:, :],
                                            axis=AX.X, apply_absolute_value=True), "eV")
                    mk.wait(e, en, "eV", c)
                    c = mk.sig(e.reciprocal(out=scal[:, 2:3], in_=wsd_sb[:, oc:oc + 1]), "eV")
                    mk.wait(e, en, "eV", c)
                    c = mk.sig(e.tensor_scalar_mul(out=scal[:, 3:4], in0=scal[:, 2:3],
                                                   scalar1=127.0), "eV")
                    mk.wait(e, en, "eV", c)
                    c = mk.sig(e.tensor_scalar_mul(out=wrawD[:, :], in0=wrawD[:, :],
                                                   scalar1=scal[:, 3:4]), "eV")
                    mk.wait(e, en, "eV", c)

                @mk.on("scalar")
                def _(e, en, oc=oc):
                    mk.wait(e, en, "eV")
                    mk.sig(e.activation(out=p2af[:, 0:1], in_=wrawD[:, 0:1],
                                        func=AF.Sign), "eS")

                @mk.on("vector")
                def _(e, en, oc=oc):
                    mk.wait(e, en, "eS")
                    c = mk.sig(e.scalar_tensor_tensor(out=p2i2[:, :], in0=p2af[:, 0:1].broadcast_to((128, I0)),
                                                      scalar=0.0, in1=wrawD[:, :],
                                                      op0=ALU.mult, op1=ALU.add), "eV")
                    mk.wait(e, en, "eV", c)
                    c = mk.sig(e.tensor_copy(out=p2bh4[:, 0:I0], in_=p2i2[:, :]), "eV")
                    mk.wait(e, en, "eV", c)

                @mk.on("scalar")
                def _(e, en, oc=oc):
                    mk.wait(e, en, "eV")
                    mk.sig(e.dma_start(out=w8d_d.ap()[oc * 128:(oc + 1) * 128, :],
                                       in_=p2bh4[:, 0:I0]), "qC", 16)
            mk.gate()

            XQ_V0 = C["eV"]
            XQ_S0 = C["eS"]

            @mk.on("scalar")
            def _(e, en):
                for oc in range(OC):
                    if oc >= 1:
                        mk.wait(e, en, "eV", XQ_V0 + (oc - 1) * 3 + 2)
                    mk.sig(e.activation(out=p2a[:, 0, 0:1], in_=stream[:, oc, 0:1],
                                        func=AF.Sign), "eS")

            @mk.on("vector")
            def _(e, en):
                for oc in range(OC):
                    c = mk.sig(e.tensor_scalar_mul(out=p2a[:, 1, :], in0=stream[:, oc, :],
                                                   scalar1=scal[:, 1:2]), "eV")
                    mk.wait(e, en, "eS", XQ_S0 + oc + 1)
                    mk.wait(e, en, "eV", c)
                    c = mk.sig(e.scalar_tensor_tensor(out=p2b[:, :], in0=p2a[:, 0, 0:1].broadcast_to((128, T)),
                                                      scalar=0.0, in1=p2a[:, 1, :],
                                                      op0=ALU.mult, op1=ALU.add), "eV")
                    mk.wait(e, en, "eV", c)
                    c = mk.sig(e.tensor_scalar(out=xq[:, oc, :], in0=p2b[:, :],
                                               scalar1=-128.0, scalar2=127.0,
                                               op0=ALU.max, op1=ALU.min), "eV")
                    mk.wait(e, en, "eV", c)
            mk.gate()

        @mk.on("vector")
        def _(e, en):
            mk.sig(e.tensor_scalar(out=csg_sb[:], in0=wsg_sb[:],
                                   scalar1=scal[:, 0:1], scalar2=1.0 / 127.0,
                                   op0=ALU.mult, op1=ALU.mult), "eV")
            mk.sig(e.tensor_scalar(out=csu_sb[:], in0=wsu_sb[:],
                                   scalar1=scal[:, 0:1], scalar2=1.0 / 127.0,
                                   op0=ALU.mult, op1=ALU.mult), "eV")
        mk.gate()

        ICGS = [(0, 4), (4, 4), (8, 4), (12, 4), (16, 4), (20, 2)]
        with nc.psum_tensor("g_ps", [128, 2, 512], F32) as g_ps, \
             nc.psum_tensor("u_ps", [128, 2, 512], F32) as u_ps:
            guT0, guV0, guS0 = C["eT"], C["eV"], C["eS"]
            guQS0, guQC0 = C["qS"], C["qC"]
            tiles_before = [0]
            for _, gn in ICGS:
                tiles_before.append(tiles_before[-1] + gn * T4)

            @mk.on("sync")
            def _(e, en):
                for gi, (ic0, gn) in enumerate(ICGS):
                    if gi >= 1:
                        mk.wait(e, en, "eT", guT0 + tiles_before[gi] * 2)
                    for j, wsrc in enumerate([w8g_d, w8u_d]):
                        for fc in range(OC):
                            mk.sig(e.dma_start(
                                out=wTgu[:, j, fc, 0:gn * 128],
                                in_=wsrc.ap()[ic0 * 128:(ic0 + gn) * 128,
                                              fc * 128:(fc + 1) * 128],
                                transpose=True), "qS", 16)

            @mk.on("tensor")
            def _(e, en):
                k = 0
                for gi, (ic0, gn) in enumerate(ICGS):
                    mk.wait(e, en, "qS", guQS0 + (gi + 1) * 16 * 16)
                    for ii in range(gn):
                        ic = ic0 + ii
                        for t4 in range(T4):
                            r = k % 2
                            if k >= 2:
                                mk.wait(e, en, "eS", guS0 + (k - 1) * 2)
                            for fc in range(OC):
                                mm = e.matmul(g_ps[:, r, :],
                                              wTgu[:, 0, fc, ii * 128:(ii + 1) * 128],
                                              xq[:, fc, t4 * 512:(t4 + 1) * 512],
                                              start=(fc == 0), stop=(fc == OC - 1))
                            mk.sig(mm, "eT")
                            for fc in range(OC):
                                mm = e.matmul(u_ps[:, r, :],
                                              wTgu[:, 1, fc, ii * 128:(ii + 1) * 128],
                                              xq[:, fc, t4 * 512:(t4 + 1) * 512],
                                              start=(fc == 0), stop=(fc == OC - 1))
                            mk.sig(mm, "eT")
                            k += 1

            @mk.on("scalar")
            def _(e, en):
                k = 0
                for gi, (ic0, gn) in enumerate(ICGS):
                    for ii in range(gn):
                        ic = ic0 + ii
                        for t4 in range(T4):
                            r = k % 2
                            mk.wait(e, en, "eT", guT0 + k * 2 + 1)
                            if k >= 1:
                                mk.wait(e, en, "eV", guV0 + k * 3)
                            mk.sig(e.activation(out=p2a[:, 0, r * 512:r * 512 + 512],
                                                in_=g_ps[:, r, :], func=AF.Silu,
                                                scale=csg_sb[:, ic:ic + 1]), "eS")
                            mk.wait(e, en, "eT", guT0 + k * 2 + 2)
                            mk.sig(e.activation(out=p2a[:, 1, r * 512:r * 512 + 512],
                                                in_=u_ps[:, r, :], func=AF.Identity,
                                                scale=csu_sb[:, ic:ic + 1]), "eS")
                            mk.wait(e, en, "eV", guV0 + k * 3 + 1)
                            mk.sig(e.dma_start(
                                out=hpre_d.ap()[ic, :, t4 * 512:(t4 + 1) * 512],
                                in_=p2a[:, 0, 1024 + r * 512:1024 + r * 512 + 512]), "qC", 16)
                            k += 1

            @mk.on("vector")
            def _(e, en):
                k = 0
                for gi, (ic0, gn) in enumerate(ICGS):
                    for ii in range(gn):
                        ic = ic0 + ii
                        for t4 in range(T4):
                            r = k % 2
                            mk.wait(e, en, "eS", guS0 + (k + 1) * 2)
                            if k >= 2:
                                mk.wait(e, en, "qC", guQC0 + (k - 1) * 16)
                            c = mk.sig(e.tensor_mul(out=p2a[:, 0, 1024 + r * 512:1536 + r * 512],
                                                    in0=p2a[:, 0, r * 512:r * 512 + 512],
                                                    in1=p2a[:, 1, r * 512:r * 512 + 512]), "eV")
                            mk.wait(e, en, "eV", c)
                            c = mk.sig(e.reduce_max(out=scal[:, 6:7],
                                                    in_=p2a[:, 0, 1024 + r * 512:1536 + r * 512],
                                                    axis=AX.X, apply_absolute_value=True), "eV")
                            mk.wait(e, en, "eV", c)
                            if t4 == 0:
                                c = mk.sig(e.tensor_copy(out=maxc_sb[:, ic:ic + 1],
                                                         in_=scal[:, 6:7]), "eV")
                            else:
                                c = mk.sig(e.tensor_max(out=maxc_sb[:, ic:ic + 1],
                                                        in0=maxc_sb[:, ic:ic + 1],
                                                        in1=scal[:, 6:7]), "eV")
                            mk.wait(e, en, "eV", c)
                            k += 1
            mk.gate()

        @mk.on("vector")
        def _(e, en):
            c = mk.sig(e.reduce_max(out=maxc_sb[:, 32:33], in_=maxc_sb[:, 0:IC0],
                                    axis=AX.X), "eV")
            mk.wait(e, en, "eV", c)
        emit_armax(1, maxc_sb[:, 32:33], maxc_sb[:, 33:34])

        with nc.psum_tensor("bc_ps2", [128, 1], F32) as bc_ps2:
            @mk.on("vector")
            def _(e, en):
                mk.wait(e, en, "qG")
                c = mk.sig(e.tensor_mul(out=row16[:, 56:57], in0=row16[:, 8:9], in1=sig_cd[:]), "eV")
                mk.wait(e, en, "eV", c)
                c = mk.sig(e.tensor_scalar_mul(out=row16[:, 57:58], in0=row16[:, 56:57],
                                               scalar1=1.0 / 127.0), "eV")
                mk.wait(e, en, "eV", c)
                c = mk.sig(e.reciprocal(out=row16[:, 58:59], in_=row16[:, 57:58]), "eV")
                mk.wait(e, en, "eV", c)
            emit_bcast(row16[:, 57:58], 4, bc_ps2[:, 0:1])
            emit_bcast(row16[:, 58:59], 5, bc_ps2[:, 0:1])

            @mk.on("vector")
            def _(e, en):
                mk.sig(e.tensor_scalar(out=csd_sb[:], in0=wsd_sb[:],
                                       scalar1=scal[:, 4:5], scalar2=1.0 / 127.0,
                                       op0=ALU.mult, op1=ALU.mult), "eV")
            mk.gate()

        with nc.psum_tensor("d_ps", [128, 4, 512], F32) as d_ps, \
             nc.psum_tensor("st2_ps", [128, 2, 512], F32) as st2_ps:
            for t4 in range(T4):
                hqQ0, hqS0, hqV0 = C["qC"], C["eS"], C["eV"]

                @mk.on("scalar")
                def _(e, en, t4=t4, hqV0=hqV0):
                    for ic in range(IC0):
                        r = ic % 2
                        if ic >= 2:
                            mk.wait(e, en, "eV", hqV0 + (ic - 1) * 3)
                        mk.sig(e.dma_start(out=p2a[:, 1, 1024 + r * 512:1536 + r * 512],
                                           in_=hpre_d.ap()[ic, :, t4 * 512:(t4 + 1) * 512]), "qC", 16)
                        mk.wait(e, en, "qC")
                        mk.sig(e.activation(out=p2a[:, 0, r * 512:r * 512 + 1],
                                            in_=p2a[:, 1, 1024 + r * 512:1025 + r * 512],
                                            func=AF.Sign), "eS")

                @mk.on("vector")
                def _(e, en, t4=t4, hqS0=hqS0):
                    for ic in range(IC0):
                        r = ic % 2
                        mk.wait(e, en, "eS", hqS0 + ic + 1)
                        c = mk.sig(e.tensor_scalar_mul(out=p2a[:, 1, r * 512:r * 512 + 512],
                                                       in0=p2a[:, 1, 1024 + r * 512:1536 + r * 512],
                                                       scalar1=scal[:, 5:6]), "eV")
                        mk.wait(e, en, "eV", c)
                        c = mk.sig(e.scalar_tensor_tensor(out=p2b[:, r * 512:r * 512 + 512],
                                                          in0=p2a[:, 0, r * 512:r * 512 + 1].broadcast_to((128, 512)),
                                                          scalar=0.0,
                                                          in1=p2a[:, 1, r * 512:r * 512 + 512],
                                                          op0=ALU.mult, op1=ALU.add), "eV")
                        mk.wait(e, en, "eV", c)
                        c = mk.sig(e.tensor_scalar(out=hintC[:, ic, :],
                                                   in0=p2b[:, r * 512:r * 512 + 512],
                                                   scalar1=-128.0, scalar2=127.0,
                                                   op0=ALU.max, op1=ALU.min), "eV")
                        mk.wait(e, en, "eV", c)
                HQ_EV = C["eV"]

                dT0, dV0, dS0, dQS0 = C["eT"], C["eV"], C["eS"], C["qS"]

                @mk.on("sync")
                def _(e, en, t4=t4, dT0=dT0):
                    for ocg in range(2):
                        for ic in range(IC0):
                            r = ic % 2
                            li = ocg * IC0 + ic
                            if li >= 2:
                                mk.wait(e, en, "eT", dT0 + (li - 1))
                            else:
                                mk.wait(e, en, "eT", dT0)
                            mk.sig(e.dma_start(out=wTd[:, r, :],
                                               in_=w8d_d.ap()[ocg * 512:(ocg + 1) * 512,
                                                              ic * 128:(ic + 1) * 128],
                                               transpose=True), "qS", 16)

                @mk.on("tensor")
                def _(e, en, t4=t4, dQS0=dQS0, HQ_EV=HQ_EV, dV0=dV0):
                    mk.wait(e, en, "eV", HQ_EV)
                    for ocg in range(2):
                        if ocg == 1:
                            mk.wait(e, en, "eV", dV0 + 8)
                        for ic in range(IC0):
                            r = ic % 2
                            li = ocg * IC0 + ic
                            mk.wait(e, en, "qS", dQS0 + (li + 1) * 16)
                            for j in range(4):
                                mm = e.matmul(d_ps[:, j, :],
                                              wTd[:, r, j * 128:(j + 1) * 128],
                                              hintC[:, ic, :],
                                              start=(ic == 0), stop=(ic == IC0 - 1))
                            mk.sig(mm, "eT")

                @mk.on("vector")
                def _(e, en, t4=t4, dT0=dT0):
                    for ocg in range(2):
                        mk.wait(e, en, "eT", dT0 + (ocg + 1) * IC0)
                        for j in range(4):
                            oc = ocg * 4 + j
                            sl = stream[:, oc, t4 * 512:(t4 + 1) * 512]
                            c = mk.sig(e.scalar_tensor_tensor(
                                out=sl, in0=d_ps[:, j, :], scalar=csd_sb[:, oc:oc + 1],
                                in1=sl, op0=ALU.mult, op1=ALU.add), "eV")
                            mk.wait(e, en, "eV", c)
                            mk.sig(e.tensor_copy(out=ybf[:, oc, :], in_=sl), "eV")
                DW_EV = C["eV"]
                DW_S0 = C["eS"]

                @mk.on("scalar")
                def _(e, en, t4=t4, dV0=dV0):
                    for oc in range(OC):
                        mk.wait(e, en, "eV", dV0 + oc * 2 + 2)
                        mk.sig(e.activation(out=y2bf[:, oc, :], in_=ybf[:, oc, :],
                                            func=AF.Square), "eS")

                @mk.on("tensor")
                def _(e, en, t4=t4, dV0=dV0, DW_S0=DW_S0):
                    for oc in range(OC):
                        mk.wait(e, en, "eV", dV0 + oc * 2 + 2)
                        mm = e.matmul(st2_ps[:, 0, :], ones_blk[:], ybf[:, oc, :],
                                      start=(oc == 0), stop=(oc == OC - 1))
                    mk.sig(mm, "eT")
                    for oc in range(OC):
                        mk.wait(e, en, "eS", DW_S0 + oc + 1)
                        mm = e.matmul(st2_ps[:, 1, :], ones_blk[:], y2bf[:, oc, :],
                                      start=(oc == 0), stop=(oc == OC - 1))
                    mk.sig(mm, "eT")

                ln2T = C["eT"]

                @mk.on("vector")
                def _(e, en, t4=t4, ln2T=ln2T):
                    mk.wait(e, en, "eT", ln2T)
                    c1 = mk.sig(e.tensor_copy(out=scr[:, 0, :], in_=st2_ps[:, 0, :]), "eV")
                    mk.wait(e, en, "eV", c1)
                    c2 = mk.sig(e.scalar_tensor_tensor(out=scr[:, 1, :], in0=scr[:, 0, :],
                                                       scalar=1.0 / (D * D), in1=scr[:, 0, :],
                                                       op0=ALU.mult, op1=ALU.mult), "eV")
                    mk.wait(e, en, "eV", c2)
                    mk.sig(e.scalar_tensor_tensor(out=scr[:, 2, :], in0=st2_ps[:, 1, :],
                                                  scalar=1.0 / D, in1=scr[:, 1, :],
                                                  op0=ALU.mult, op1=ALU.subtract), "eV")
                LV = C["eV"]

                @mk.on("scalar")
                def _(e, en, t4=t4, LV=LV):
                    mk.wait(e, en, "eV", LV)
                    mk.sig(e.activation(out=r_bc[:, 0, :], in_=scr[:, 2, :], func=AF.Sqrt,
                                        bias=epsln_col[:, 0:1]), "eS")
                LS = C["eS"]

                @mk.on("vector")
                def _(e, en, t4=t4, LS=LS):
                    mk.wait(e, en, "eS", LS)
                    c1 = mk.sig(e.reciprocal(out=r_bc[:, 1, :], in_=r_bc[:, 0, :]), "eV")
                    mk.wait(e, en, "eV", c1)
                    c2 = mk.sig(e.scalar_tensor_tensor(out=scr[:, 3, :], in0=scr[:, 0, :],
                                                       scalar=1.0 / D, in1=r_bc[:, 1, :],
                                                       op0=ALU.mult, op1=ALU.mult), "eV")
                    mk.wait(e, en, "eV", c2)
                    for oc in range(OC):
                        sl = stream[:, oc, t4 * 512:(t4 + 1) * 512]
                        c3 = mk.sig(e.tensor_mul(out=scr[:, 1, :], in0=sl,
                                                 in1=r_bc[:, 1, :]), "eV")
                        mk.wait(e, en, "eV", c3)
                        c4 = mk.sig(e.tensor_sub(out=sl, in0=scr[:, 1, :],
                                                 in1=scr[:, 3, :]), "eV")
                        mk.wait(e, en, "eV", c4)
            mk.gate()


        mk.emit = (upto >= 7)

        @mk.on("vector")
        def _(e, en):
            for oc in range(OC):
                mk.sig(e.tensor_copy(out=xbf[:, oc, :], in_=stream[:, oc, :]), "eV")
        P3C_EV = C["eV"]

        @mk.on("scalar")
        def _(e, en):
            for oc in range(OC):
                mk.sig(e.dma_start(out=xspill.ap()[oc * 128:(oc + 1) * 128, :],
                                   in_=stream[:, oc, :]), "qC", 16)
        P3SP_QC = C["qC"]

        @mk.on("gpsimd")
        def _(e, en):
            mk.wait(e, en, "qC", P3SP_QC)
            for j, w in enumerate([wqt1, wkt1]):
                for fc in range(OC):
                    mk.sig(e.dma_start(out=w2[:, j, fc, :],
                                       in_=w.ap()[fc * 128:(fc + 1) * 128, :]), "qG", 16)
            for fc in range(OC):
                mk.sig(e.dma_start(out=wo_sb[:, fc, :],
                                   in_=wut1.ap()[fc * 128:(fc + 1) * 128, :]), "qG", 16)
        P3W_QG = C["qG"]

        with nc.psum_tensor("p3q_ps", [128, 2, 512], F32) as p3q_ps:
            NQT3 = OC * T4
            q3T0, q3S0 = C["eT"], C["eS"]

            @mk.on("tensor")
            def _(e, en):
                mk.wait(e, en, "qG", P3W_QG)
                mk.wait(e, en, "eV", P3C_EV)
                for j in range(2):
                    for i in range(NQT3):
                        oc, t4 = divmod(i, T4)
                        gi = j * NQT3 + i
                        r = gi % 2
                        if gi >= 2:
                            mk.wait(e, en, "eS", q3S0 + gi - 1)
                        for fc in range(OC):
                            mm = e.matmul(p3q_ps[:, r, :],
                                          w2[:, j, fc, oc * 128:(oc + 1) * 128],
                                          xbf[:, fc, t4 * 512:(t4 + 1) * 512],
                                          start=(fc == 0), stop=(fc == OC - 1))
                        mk.sig(mm, "eT")

            @mk.on("scalar")
            def _(e, en):
                for j in range(2):
                    for i in range(NQT3):
                        oc, t4 = divmod(i, T4)
                        gi = j * NQT3 + i
                        r = gi % 2
                        mk.wait(e, en, "eT", q3T0 + gi + 1)
                        dst = qbf if j == 0 else p3k
                        mk.sig(e.activation(out=dst[:, oc, t4 * 512:(t4 + 1) * 512],
                                            in_=p3q_ps[:, r, :], func=AF.Identity), "eS")
            mk.gate()

            @mk.on("gpsimd")
            def _(e, en):
                for oc in range(OC):
                    mk.sig(e.dma_start(out=kag_in.ap()[oc * 128:(oc + 1) * 128, :],
                                       in_=p3k[:, oc, :]), "qG", 16)
                for fc in range(OC):
                    mk.sig(e.dma_start(out=w2[:, 0, fc, :],
                                       in_=wvt1.ap()[fc * 128:(fc + 1) * 128, :]), "qG", 16)
            P3KB_QG = C["qG"]
            mk.gate()

        with nc.psum_tensor("p3v_ps", [128, 2, D], F32) as p3v_ps:
            v3T0, v3S0, v3QG0 = C["eT"], C["eS"], C["qG"]

            @mk.on("tensor")
            def _(e, en):
                for tt in range(TT):
                    r = tt % 2
                    if tt >= 2:
                        mk.wait(e, en, "eS", v3S0 + tt - 1)
                    for fc in range(OC):
                        for o2 in range(2):
                            mm = e.matmul(p3v_ps[:, r, o2 * 512:(o2 + 1) * 512],
                                          xbf[:, fc, tt * 128:(tt + 1) * 128],
                                          w2[:, 0, fc, o2 * 512:(o2 + 1) * 512],
                                          start=(fc == 0), stop=(fc == OC - 1))
                    mk.sig(mm, "eT")

            @mk.on("scalar")
            def _(e, en):
                for tt in range(TT):
                    r = tt % 2
                    mk.wait(e, en, "eT", v3T0 + tt + 1)
                    if tt >= 2:
                        mk.wait(e, en, "qG", v3QG0 + (tt - 1) * 16)
                    mk.sig(e.activation(out=vstg[:, r, :], in_=p3v_ps[:, r, :],
                                        func=AF.Identity), "eS")

            @mk.on("gpsimd")
            def _(e, en):
                for tt in range(TT):
                    r = tt % 2
                    mk.wait(e, en, "eS", v3S0 + tt + 1)
                    mk.sig(e.dma_start(
                        out=vag_in.ap().rearrange("h t d -> t h d")[tt * 128:(tt + 1) * 128],
                        in_=vstg[:, r, :].rearrange("p (h d) -> p h d", d=64)), "qG", 16)
            mk.gate()

        @mk.on("gpsimd")
        def _(e, en):
            mk.sig(e.collective_compute(
                "AllGather", ALU.bypass, replica_groups=RG_PAIR,
                ins=[kag_in.ap().opt()], outs=[kag_out.ap().opt()]), "cc")
            mk.sig(e.collective_compute(
                "AllGather", ALU.bypass, replica_groups=RG_PAIR,
                ins=[vag_in.ap().opt()], outs=[vag_out.ap().opt()]), "cc")
            mk.wait(e, en, "cc")
        mk.gate()

        with nc.psum_tensor("sT_ps", [128, 2, 1024], F32) as sT_ps, \
             nc.psum_tensor("av_ps", [128, 4, 512], F32) as av_ps:

            @mk.on("vector")
            def _(e, en):
                mk.sig(e.memset(vh[:, :, 64:65], 1.0), "eV")
            mk.gate()

            for h in range(NH):
                p0 = (h % 2) * 64
                ch = h // 2
                hT0, hS0, hV0, hQS0 = C["eT"], C["eS"], C["eV"], C["qS"]

                @mk.on("sync")
                def _(e, en, h=h, p0=p0, hT0=hT0):
                    for half in range(2):
                        if h + half >= 1:
                            pass
                        if h >= 1:
                            mk.wait(e, en, "eT", hT0)
                        mk.sig(e.dma_start(
                            out=kh[p0:p0 + 64, half, :],
                            in_=kag_out.ap()[half * D + h * 64:half * D + (h + 1) * 64, :]), "qS", 16)
                        mk.sig(e.dma_start(
                            out=vh[:, half * 16:(half + 1) * 16, 0:64],
                            in_=vag_out.ap()[half, h, :, :].rearrange("(g p) d -> p g d", p=128)), "qS", 16)

                NI = 64

                @mk.on("tensor")
                def _(e, en, h=h, p0=p0, ch=ch, hT0=hT0, hS0=hS0, hV0=hV0, hQS0=hQS0):
                    if h >= 1:
                        mk.wait(e, en, "eV", hV0)
                    mk.wait(e, en, "qS", hQS0 + 64)
                    for i in range(NI):
                        kt, qh = divmod(i, 2)
                        half = kt // 16
                        ktl = kt % 16
                        r = i % 2
                        if i >= 2:
                            mk.wait(e, en, "eS", hS0 + i - 1)
                        for half512 in range(2):
                            mm = e.matmul(
                                sT_ps[:, r, half512 * 512:(half512 + 1) * 512],
                                kh[p0:p0 + 64, half, ktl * 128:(ktl + 1) * 128],
                                qbf[p0:p0 + 64, ch,
                                    qh * 1024 + half512 * 512:qh * 1024 + (half512 + 1) * 512],
                                start=True, stop=True)
                        mk.sig(mm, "eT")
                        if i >= 1:
                            self_i = i - 1
                            kt2, qh2 = divmod(self_i, 2)
                            r2 = self_i % 2
                            mk.wait(e, en, "eS", hS0 + self_i + 1)
                            for j2 in range(2):
                                mm = e.matmul(av_ps[0:65, qh2 * 2 + j2, :],
                                              vh[:, kt2, :],
                                              pt[:, r2, j2 * 512:(j2 + 1) * 512],
                                              start=(kt2 == 0), stop=(kt2 == 31))
                            mk.sig(mm, "eT")
                    i = NI - 1
                    kt2, qh2 = divmod(i, 2)
                    r2 = i % 2
                    mk.wait(e, en, "eS", hS0 + i + 1)
                    for j2 in range(2):
                        mm = e.matmul(av_ps[0:65, qh2 * 2 + j2, :],
                                      vh[:, kt2, :],
                                      pt[:, r2, j2 * 512:(j2 + 1) * 512],
                                      start=(kt2 == 0), stop=(kt2 == 31))
                    mk.sig(mm, "eT")
                    AV_DONE = C["eT"]
                    mk.wait(e, en, "eV", hV0 + 1)
                    for j in range(4):
                        mm = e.matmul(sT_ps[0:64, j // 2, (j % 2) * 512:(j % 2) * 512 + 512],
                                      ones_r1f[:, 0:64],
                                      scr[0:1, j, :], start=True, stop=True)
                    mk.sig(mm, "eT")

                @mk.on("scalar")
                def _(e, en, h=h, hT0=hT0, hV0=hV0):
                    for i in range(NI):
                        r = i % 2
                        tsig = 1 if i == 0 else 2 * i
                        mk.wait(e, en, "eT", hT0 + tsig)
                        if i >= 2:
                            mk.wait(e, en, "eT", hT0 + 2 * (i - 2) + 3)
                        mk.sig(e.activation(out=pt[:, r, :], in_=sT_ps[:, r, :],
                                            func=AF.Exp, scale=1.0 / 32.0), "eS")

                @mk.on("vector")
                def _(e, en, h=h, p0=p0, ch=ch, hT0=hT0):
                    mk.wait(e, en, "eT", hT0 + 2 * (NI - 2) + 3 + 1)
                    c = mk.sig(e.reciprocal(out=scr[0:1, :, :], in_=av_ps[64:65, :, :]), "eV")
                    mk.wait(e, en, "eV", c)
                    mk.wait(e, en, "eT", hT0 + 2 * (NI - 2) + 3 + 1 + 1)
                    for j in range(4):
                        c = mk.sig(e.tensor_copy(out=scr[0:64, j, :],
                                                 in_=sT_ps[0:64, j // 2,
                                                           (j % 2) * 512:(j % 2) * 512 + 512]), "eV")
                        mk.wait(e, en, "eV", c)
                        if h % 2 == 0:
                            dst = p3att[0:64, ch, j * 512:(j + 1) * 512]
                        else:
                            dst = strip[:, j * 512:(j + 1) * 512]
                        c = mk.sig(e.tensor_mul(out=dst, in0=av_ps[0:64, j, :],
                                                in1=scr[0:64, j, :]), "eV")
                        mk.wait(e, en, "eV", c)

                if h % 2 == 1:
                    @mk.on("gpsimd")
                    def _(e, en, h=h, ch=ch):
                        mk.wait(e, en, "eV")
                        mk.sig(e.dma_start(out=p3att[64:128, ch, :], in_=strip[:, :]), "qG", 16)
                        mk.wait(e, en, "qG")
            mk.gate()

        with nc.psum_tensor("wu_ps", [128, 2, 512], F32) as wu_ps, \
             nc.psum_tensor("st3_ps", [128, 2, 512], F32) as st3_ps:
            prev_spill_qc = 0
            for t4 in range(T4):
                t4T0, t4V0, t4S0 = C["eT"], C["eV"], C["eS"]

                @mk.on("sync")
                def _(e, en, t4=t4, t4V0=t4V0, prev_spill_qc=prev_spill_qc):
                    if t4 >= 1:
                        mk.wait(e, en, "eV", t4V0)
                        mk.wait(e, en, "qC", prev_spill_qc)
                    for oc in range(OC):
                        mk.sig(e.dma_start(out=xres[:, oc, :],
                                           in_=xspill.ap()[oc * 128:(oc + 1) * 128,
                                                           t4 * 512:(t4 + 1) * 512]), "qS", 16)
                XR_QS = C["qS"]

                @mk.on("tensor")
                def _(e, en, t4=t4, t4V0=t4V0):
                    for oc in range(OC):
                        rr = oc % 2
                        if oc >= 2:
                            mk.wait(e, en, "eV", t4V0 + (oc - 1) * 2)
                        for hp in range(8):
                            mm = e.matmul(wu_ps[:, rr, :],
                                          wo_sb[:, hp, oc * 128:(oc + 1) * 128],
                                          p3att[:, hp, t4 * 512:(t4 + 1) * 512],
                                          start=(hp == 0), stop=(hp == 7))
                        mk.sig(mm, "eT")
                    for oc in range(OC):
                        mk.wait(e, en, "eV", t4V0 + oc * 2 + 2)
                        mm = e.matmul(st3_ps[:, 0, :], ones_blk[:], ybf[:, oc, :],
                                      start=(oc == 0), stop=(oc == OC - 1))
                    mk.sig(mm, "eT")
                    for oc in range(OC):
                        mk.wait(e, en, "eS", t4S0 + oc + 1)
                        mm = e.matmul(st3_ps[:, 1, :], ones_blk[:], y2bf[:, oc, :],
                                      start=(oc == 0), stop=(oc == OC - 1))
                    mk.sig(mm, "eT")

                @mk.on("vector")
                def _(e, en, t4=t4, t4T0=t4T0, XR_QS=XR_QS):
                    mk.wait(e, en, "qS", XR_QS)
                    for oc in range(OC):
                        rr = oc % 2
                        mk.wait(e, en, "eT", t4T0 + oc + 1)
                        mk.sig(e.scalar_tensor_tensor(
                            out=xres[:, oc, :],
                            in0=wu_ps[:, rr, :], scalar=bu1_sb[:, oc:oc + 1],
                            in1=xres[:, oc, :], op0=ALU.add, op1=ALU.add), "eV")
                        mk.sig(e.tensor_copy(out=ybf[:, oc, :],
                                             in_=xres[:, oc, :]), "eV")

                @mk.on("scalar")
                def _(e, en, t4=t4, t4V0=t4V0):
                    for oc in range(OC):
                        mk.wait(e, en, "eV", t4V0 + oc * 2 + 2)
                        mk.sig(e.activation(out=y2bf[:, oc, :], in_=ybf[:, oc, :],
                                            func=AF.Square), "eS")

                ln3T = C["eT"]

                @mk.on("vector")
                def _(e, en, t4=t4, ln3T=ln3T):
                    mk.wait(e, en, "eT", ln3T)
                    c1 = mk.sig(e.tensor_copy(out=scr[:, 0, :], in_=st3_ps[:, 0, :]), "eV")
                    mk.wait(e, en, "eV", c1)
                    c2 = mk.sig(e.scalar_tensor_tensor(out=scr[:, 1, :], in0=scr[:, 0, :],
                                                       scalar=1.0 / (D * D), in1=scr[:, 0, :],
                                                       op0=ALU.mult, op1=ALU.mult), "eV")
                    mk.wait(e, en, "eV", c2)
                    mk.sig(e.scalar_tensor_tensor(out=scr[:, 2, :], in0=st3_ps[:, 1, :],
                                                  scalar=1.0 / D, in1=scr[:, 1, :],
                                                  op0=ALU.mult, op1=ALU.subtract), "eV")
                LV = C["eV"]

                @mk.on("scalar")
                def _(e, en, t4=t4, LV=LV):
                    mk.wait(e, en, "eV", LV)
                    mk.sig(e.activation(out=r_bc[:, 0, :], in_=scr[:, 2, :], func=AF.Sqrt,
                                        bias=epsln_col[:, 0:1]), "eS")
                LS = C["eS"]

                @mk.on("vector")
                def _(e, en, t4=t4, LS=LS):
                    mk.wait(e, en, "eS", LS)
                    c1 = mk.sig(e.reciprocal(out=r_bc[:, 1, :], in_=r_bc[:, 0, :]), "eV")
                    mk.wait(e, en, "eV", c1)
                    c2 = mk.sig(e.scalar_tensor_tensor(out=scr[:, 3, :], in0=scr[:, 0, :],
                                                       scalar=1.0 / D, in1=r_bc[:, 1, :],
                                                       op0=ALU.mult, op1=ALU.mult), "eV")
                    mk.wait(e, en, "eV", c2)
                    for oc in range(OC):
                        sl = xres[:, oc, :]
                        c3 = mk.sig(e.tensor_mul(out=scr[:, 1, :], in0=sl,
                                                 in1=r_bc[:, 1, :]), "eV")
                        mk.wait(e, en, "eV", c3)
                        c4 = mk.sig(e.tensor_sub(out=sl, in0=scr[:, 1, :],
                                                 in1=scr[:, 3, :]), "eV")
                        mk.wait(e, en, "eV", c4)
                        mk.sig(e.reduce_max(out=maxc_sb[:, t4 * OC + oc:t4 * OC + oc + 1],
                                            in_=sl, axis=AX.X,
                                            apply_absolute_value=True), "eV")
                XN_EV = C["eV"]

                @mk.on("scalar")
                def _(e, en, t4=t4, XN_EV=XN_EV):
                    mk.wait(e, en, "eV", XN_EV)
                    for oc in range(OC):
                        mk.sig(e.dma_start(out=xspill.ap()[oc * 128:(oc + 1) * 128,
                                                           t4 * 512:(t4 + 1) * 512],
                                           in_=xres[:, oc, :]), "qC", 16)
                prev_spill_qc = C["qC"]
            mk.gate()

        @mk.on("vector")
        def _(e, en):
            c1 = mk.sig(e.reduce_max(out=maxc_sb[:, 32:33], in_=maxc_sb[:, 0:32],
                                     axis=AX.X), "eV")
            mk.wait(e, en, "eV", c1)
        emit_armax(2, maxc_sb[:, 32:33], maxc_sb[:, 33:34])

        @mk.on("gpsimd")
        def _(e, en):
            for oc in range(OC):
                mk.sig(e.dma_start(out=stream[:, oc, :],
                                   in_=xspill.ap()[oc * 128:(oc + 1) * 128, :]), "qG", 16)
        mk.gate()


        mk.emit = (upto >= 8)

        def prep_lpbq(raw, dst, nchunk, N, colparts=1):
            NB = N // 32
            for ici in range(nchunk * colparts):
                ic, cp = divmod(ici, colparts)
                @mk.on("sync")
                def _(e, en, ic=ic, cp=cp, N=N):
                    mk.wait(e, en, "qC")
                    mk.wait(e, en, "eV")
                    mk.sig(e.dma_start(out=p2a[:, 0, 0:N],
                                       in_=raw.ap()[ic * 128:(ic + 1) * 128,
                                                    cp * N:(cp + 1) * N]), "qS", 16)

                @mk.on("vector")
                def _(e, en, ic=ic, NB=NB, N=N):
                    mk.wait(e, en, "qS")
                    wv = p2a[:, 0, 0:N].rearrange("p (b e) -> p b e", e=32)
                    c = mk.sig(e.reduce_max(out=scr[:, 0, 0:NB], in_=wv, axis=AX.X,
                                            apply_absolute_value=True), "eV")
                    mk.wait(e, en, "eV", c)
                    c = mk.sig(e.tensor_scalar(out=scr[:, 1, 0:NB], in0=scr[:, 0, 0:NB],
                                               scalar1=1.0 / 7.0, scalar2=1e-12,
                                               op0=ALU.mult, op1=ALU.max), "eV")
                    mk.wait(e, en, "eV", c)
                    c = mk.sig(e.reciprocal(out=scr[:, 2, 0:NB], in_=scr[:, 1, 0:NB]), "eV")
                    mk.wait(e, en, "eV", c)
                    c = mk.sig(e.tensor_tensor(
                        out=wv, in0=wv,
                        in1=scr[:, 2, 0:NB].rearrange("p (b o) -> p b o", o=1).broadcast_to((128, NB, 32)),
                        op=ALU.mult), "eV")
                    mk.wait(e, en, "eV", c)

                @mk.on("scalar")
                def _(e, en, ic=ic, N=N):
                    mk.wait(e, en, "eV")
                    mk.sig(e.activation(out=p2a[:, 1, 0:1], in_=p2a[:, 0, 0:1],
                                        func=AF.Sign), "eS")

                @mk.on("vector")
                def _(e, en, ic=ic, NB=NB, N=N):
                    mk.wait(e, en, "eS")
                    src_w = p2a[:, 0, 0:N]
                    src_s = p2a[:, 1, 0:1].broadcast_to((128, N))
                    c = mk.sig(e.scalar_tensor_tensor(out=p4i[:, 0:N], in0=src_s,
                                                      scalar=0.0, in1=src_w,
                                                      op0=ALU.mult, op1=ALU.add), "eV")
                    mk.wait(e, en, "eV", c)
                    c = mk.sig(e.tensor_tensor(
                        out=p2bh4[:, 0:N],
                        in0=p4i[:, 0:N].rearrange("p (b e) -> p b e", e=32),
                        in1=scr[:, 1, 0:NB].rearrange("p (b o) -> p b o", o=1).broadcast_to((128, NB, 32)),
                        op=ALU.mult), "eV")
                    mk.wait(e, en, "eV", c)

                @mk.on("scalar")
                def _(e, en, ic=ic, cp=cp, N=N):
                    mk.wait(e, en, "eV")
                    mk.sig(e.dma_start(out=dst.ap()[ic * 128:(ic + 1) * 128,
                                                    cp * N:(cp + 1) * N],
                                       in_=p2bh4[:, 0:N]), "qC", 16)

        prep_lpbq(wg1, wq1g, IC1, D)
        prep_lpbq(wup1, wq1u, IC1, D)
        prep_lpbq(wd1, wq1d, OC, I1 // 2, colparts=2)
        mk.gate()

        with nc.psum_tensor("bc4_ps", [128, 1], F32) as bc4_ps:
            @mk.on("vector")
            def _(e, en):
                mk.wait(e, en, "qG")
                c = mk.sig(e.tensor_scalar(out=row16[:, 56:57], in0=row16[:, 16:17],
                                           scalar1=1.0 / 32767.0, scalar2=1e-12,
                                           op0=ALU.mult, op1=ALU.max), "eV")
                mk.wait(e, en, "eV", c)
                c = mk.sig(e.reciprocal(out=row16[:, 57:58], in_=row16[:, 56:57]), "eV")
                mk.wait(e, en, "eV", c)
            emit_bcast(row16[:, 56:57], 2, bc4_ps[:, 0:1])
            emit_bcast(row16[:, 57:58], 3, bc4_ps[:, 0:1])

            XQ4_V0 = C["eV"]
            XQ4_S0 = C["eS"]

            @mk.on("scalar")
            def _(e, en):
                for oc in range(OC):
                    if oc >= 1:
                        mk.wait(e, en, "eV", XQ4_V0 + (oc - 1) * 3 + 2)
                    mk.sig(e.activation(out=p2a[:, 0, 0:1], in_=stream[:, oc, 0:1],
                                        func=AF.Sign), "eS")

            @mk.on("vector")
            def _(e, en):
                for oc in range(OC):
                    c = mk.sig(e.tensor_scalar_mul(out=p2a[:, 1, :], in0=stream[:, oc, :],
                                                   scalar1=scal[:, 3:4]), "eV")
                    mk.wait(e, en, "eS", XQ4_S0 + oc + 1)
                    mk.wait(e, en, "eV", c)
                    c = mk.sig(e.scalar_tensor_tensor(out=p2b[:, :], in0=p2a[:, 0, 0:1].broadcast_to((128, T)),
                                                      scalar=0.0, in1=p2a[:, 1, :],
                                                      op0=ALU.mult, op1=ALU.add), "eV")
                    mk.wait(e, en, "eV", c)
                    c = mk.sig(e.tensor_scalar_mul(out=xq[:, oc, :], in0=p2b[:, :],
                                                   scalar1=scal[:, 2:3]), "eV")
                    mk.wait(e, en, "eV", c)
            mk.gate()

        ICGS4 = [(0, 4), (4, 4), (8, 4), (12, 4), (16, 4), (20, 4)]
        with nc.psum_tensor("g4_ps", [128, 2, 512], F32) as g4_ps, \
             nc.psum_tensor("u4_ps", [128, 2, 512], F32) as u4_ps:
            gu4T0, gu4V0, gu4S0 = C["eT"], C["eV"], C["eS"]
            gu4QS0, gu4QC0 = C["qS"], C["qC"]
            tiles_before4 = [0]
            for _, gn in ICGS4:
                tiles_before4.append(tiles_before4[-1] + gn * T4)

            @mk.on("sync")
            def _(e, en):
                for gi, (ic0, gn) in enumerate(ICGS4):
                    if gi >= 1:
                        mk.wait(e, en, "eT", gu4T0 + tiles_before4[gi] * 2)
                    for j, wsrc in enumerate([wq1g, wq1u]):
                        for fc in range(OC):
                            mk.sig(e.dma_start(
                                out=wTgu[:, j, fc, 0:gn * 128],
                                in_=wsrc.ap()[ic0 * 128:(ic0 + gn) * 128,
                                              fc * 128:(fc + 1) * 128],
                                transpose=True), "qS", 16)

            @mk.on("tensor")
            def _(e, en):
                k = 0
                for gi, (ic0, gn) in enumerate(ICGS4):
                    mk.wait(e, en, "qS", gu4QS0 + (gi + 1) * 16 * 16)
                    for ii in range(gn):
                        for t4 in range(T4):
                            r = k % 2
                            if k >= 2:
                                mk.wait(e, en, "eS", gu4S0 + (k - 1) * 2)
                            for fc in range(OC):
                                mm = e.matmul(g4_ps[:, r, :],
                                              wTgu[:, 0, fc, ii * 128:(ii + 1) * 128],
                                              xq[:, fc, t4 * 512:(t4 + 1) * 512],
                                              start=(fc == 0), stop=(fc == OC - 1))
                            mk.sig(mm, "eT")
                            for fc in range(OC):
                                mm = e.matmul(u4_ps[:, r, :],
                                              wTgu[:, 1, fc, ii * 128:(ii + 1) * 128],
                                              xq[:, fc, t4 * 512:(t4 + 1) * 512],
                                              start=(fc == 0), stop=(fc == OC - 1))
                            mk.sig(mm, "eT")
                            k += 1

            @mk.on("scalar")
            def _(e, en):
                k = 0
                for gi, (ic0, gn) in enumerate(ICGS4):
                    for ii in range(gn):
                        ic = ic0 + ii
                        for t4 in range(T4):
                            r = k % 2
                            mk.wait(e, en, "eT", gu4T0 + k * 2 + 1)
                            if k >= 2:
                                mk.wait(e, en, "qC", gu4QC0 + (k - 1) * 32)
                                mk.wait(e, en, "eV", gu4V0 + (k - 1) * 4)
                            mk.sig(e.activation(out=p4st[:, 0, r * 512:r * 512 + 512],
                                                in_=g4_ps[:, r, :], func=AF.Identity), "eS")
                            mk.wait(e, en, "eT", gu4T0 + k * 2 + 2)
                            mk.sig(e.activation(out=p4st[:, 1, r * 512:r * 512 + 512],
                                                in_=u4_ps[:, r, :], func=AF.Identity), "eS")
                            mk.sig(e.dma_start(
                                out=graw_d.ap()[ic, :, t4 * 512:(t4 + 1) * 512],
                                in_=p4st[:, 0, r * 512:r * 512 + 512]), "qC", 16)
                            mk.sig(e.dma_start(
                                out=uraw_d.ap()[ic, :, t4 * 512:(t4 + 1) * 512],
                                in_=p4st[:, 1, r * 512:r * 512 + 512]), "qC", 16)
                            k += 1

            @mk.on("vector")
            def _(e, en):
                k = 0
                for gi, (ic0, gn) in enumerate(ICGS4):
                    for ii in range(gn):
                        for t4 in range(T4):
                            r = k % 2
                            mk.wait(e, en, "eS", gu4S0 + k * 2 + 2)
                            c = mk.sig(e.reduce_max(out=scal[:, 6:7],
                                                    in_=p4st[:, 0, r * 512:r * 512 + 512],
                                                    axis=AX.X, apply_absolute_value=True), "eV")
                            mk.wait(e, en, "eV", c)
                            if k == 0:
                                c = mk.sig(e.tensor_copy(out=maxc_sb[:, 34:35],
                                                         in_=scal[:, 6:7]), "eV")
                            else:
                                c = mk.sig(e.tensor_max(out=maxc_sb[:, 34:35],
                                                        in0=maxc_sb[:, 34:35],
                                                        in1=scal[:, 6:7]), "eV")
                            mk.wait(e, en, "eV", c)
                            c = mk.sig(e.reduce_max(out=scal[:, 6:7],
                                                    in_=p4st[:, 1, r * 512:r * 512 + 512],
                                                    axis=AX.X, apply_absolute_value=True), "eV")
                            mk.wait(e, en, "eV", c)
                            if k == 0:
                                c = mk.sig(e.tensor_copy(out=maxc_sb[:, 35:36],
                                                         in_=scal[:, 6:7]), "eV")
                            else:
                                c = mk.sig(e.tensor_max(out=maxc_sb[:, 35:36],
                                                        in0=maxc_sb[:, 35:36],
                                                        in1=scal[:, 6:7]), "eV")
                            mk.wait(e, en, "eV", c)
                            k += 1
            mk.gate()

        emit_armax(3, maxc_sb[:, 35:36], maxc_sb[:, 33:34])
        emit_armax(4, maxc_sb[:, 34:35], maxc_sb[:, 33:34])
        with nc.psum_tensor("bc5_ps", [128, 1], F32) as bc5_ps:
            @mk.on("vector")
            def _(e, en):
                mk.wait(e, en, "qG")
                for j, col in [(0, 24), (1, 32)]:
                    c = mk.sig(e.tensor_scalar(out=row16[:, 56 + j:57 + j],
                                               in0=row16[:, col:col + 1],
                                               scalar1=1.0 / 32767.0, scalar2=1e-12,
                                               op0=ALU.mult, op1=ALU.max), "eV")
                    mk.wait(e, en, "eV", c)
                    c = mk.sig(e.reciprocal(out=row16[:, 58 + j:59 + j],
                                            in_=row16[:, 56 + j:57 + j]), "eV")
                    mk.wait(e, en, "eV", c)
            emit_bcast(row16[:, 56:57], 0, bc5_ps[:, 0:1])
            emit_bcast(row16[:, 58:59], 1, bc5_ps[:, 0:1])
            emit_bcast(row16[:, 57:58], 4, bc5_ps[:, 0:1])
            emit_bcast(row16[:, 59:60], 5, bc5_ps[:, 0:1])

            B_V0, B_S0 = C["eV"], C["eS"]

            @mk.on("scalar")
            def _(e, en):
                for ic in range(IC1):
                    if ic >= 1:
                        mk.wait(e, en, "eV", B_V0 + (ic - 1) * 7 + 7)
                    mk.sig(e.dma_start(out=p4st[:, 0, :],
                                       in_=graw_d.ap()[ic, :, :]), "qC", 16)
                    mk.wait(e, en, "qC")
                    mk.sig(e.activation(out=p2a[:, 0, 0:1], in_=p4st[:, 0, 0:1],
                                        func=AF.Sign), "eS")
                    mk.wait(e, en, "eV", B_V0 + ic * 7 + 2)
                    mk.sig(e.activation(out=p2a[:, 0, :], in_=p2b[:, :],
                                        func=AF.Sigmoid, scale=scal[:, 4:5]), "eS")
                    mk.wait(e, en, "eV", B_V0 + ic * 7 + 5)
                    mk.sig(e.dma_start(out=gs_d.ap()[ic, :, :],
                                       in_=p2bh4[:, 0:T]), "qC", 16)

            @mk.on("vector")
            def _(e, en):
                for ic in range(IC1):
                    mk.wait(e, en, "eS", B_S0 + ic * 2 + 1)
                    c = mk.sig(e.tensor_scalar_mul(out=p2a[:, 1, :], in0=p4st[:, 0, :],
                                                   scalar1=scal[:, 5:6]), "eV")
                    mk.wait(e, en, "eV", c)
                    c = mk.sig(e.scalar_tensor_tensor(out=p2b[:, :], in0=p2a[:, 0, 0:1].broadcast_to((128, T)),
                                                      scalar=0.0, in1=p2a[:, 1, :],
                                                      op0=ALU.mult, op1=ALU.add), "eV")
                    mk.wait(e, en, "eV", c)
                    mk.wait(e, en, "eS", B_S0 + ic * 2 + 2)
                    c = mk.sig(e.tensor_scalar(out=p4i[:, 0:T], in0=p2a[:, 0, :],
                                               scalar1=65536.0, scalar2=0.0,
                                               op0=ALU.mult, op1=ALU.add), "eV")
                    mk.wait(e, en, "eV", c)
                    c = mk.sig(e.tensor_scalar(out=p2a[:, 1, :], in0=p4i[:, 0:T],
                                               scalar1=65535.0, scalar2=2.0 ** -16,
                                               op0=ALU.min, op1=ALU.mult), "eV")
                    mk.wait(e, en, "eV", c)
                    c = mk.sig(e.scalar_tensor_tensor(out=p2bh4[:, 0:T], in0=p2b[:, :],
                                                      scalar=scal[:, 4:5], in1=p2a[:, 1, :],
                                                      op0=ALU.mult, op1=ALU.mult), "eV")
                    mk.wait(e, en, "eV", c)
                    c = mk.sig(e.reduce_max(out=scal[:, 6:7], in_=p2bh4[:, 0:T],
                                            axis=AX.X, apply_absolute_value=True), "eV")
                    mk.wait(e, en, "eV", c)
                    if ic == 0:
                        c = mk.sig(e.tensor_copy(out=maxc_sb[:, 34:35], in_=scal[:, 6:7]), "eV")
                    else:
                        c = mk.sig(e.tensor_max(out=maxc_sb[:, 34:35],
                                                in0=maxc_sb[:, 34:35], in1=scal[:, 6:7]), "eV")
                    e.wait_ge(mk.sems["eV"], c)
            mk.gate()

        emit_armax(5, maxc_sb[:, 34:35], maxc_sb[:, 33:34])
        with nc.psum_tensor("bc6_ps", [128, 1], F32) as bc6_ps:
            @mk.on("vector")
            def _(e, en):
                mk.wait(e, en, "qG")
                c = mk.sig(e.tensor_scalar(out=row16[:, 56:57], in0=row16[:, 40:41],
                                           scalar1=1.0 / 32767.0, scalar2=1e-12,
                                           op0=ALU.mult, op1=ALU.max), "eV")
                mk.wait(e, en, "eV", c)
                c = mk.sig(e.reciprocal(out=row16[:, 57:58], in_=row16[:, 56:57]), "eV")
                mk.wait(e, en, "eV", c)
            emit_bcast(row16[:, 57:58], 2, bc6_ps[:, 0:1])
            emit_bcast(row16[:, 56:57], 7, bc6_ps[:, 0:1])

            @mk.on("vector")
            def _(e, en):
                c = mk.sig(e.tensor_mul(out=scal[:, 4:5], in0=scal[:, 0:1],
                                        in1=scal[:, 7:8]), "eV")
                mk.wait(e, en, "eV", c)
            mk.gate()

            C_V0, C_S0 = C["eV"], C["eS"]

            @mk.on("scalar")
            def _(e, en):
                for ic in range(IC1):
                    if ic >= 1:
                        mk.wait(e, en, "eV", C_V0 + (ic - 1) * 7 + 7)
                    mk.sig(e.dma_start(out=p4st[:, 0, :], in_=gs_d.ap()[ic, :, :]), "qC", 16)
                    mk.wait(e, en, "qC")
                    mk.sig(e.activation(out=p2a[:, 0, 0:1], in_=p4st[:, 0, 0:1],
                                        func=AF.Sign), "eS")
                    mk.wait(e, en, "eV", C_V0 + ic * 7 + 2)
                    mk.sig(e.dma_start(out=p4st[:, 1, :], in_=uraw_d.ap()[ic, :, :]), "qC", 16)
                    mk.wait(e, en, "qC")
                    mk.sig(e.activation(out=p2a[:, 0, 0:1], in_=p4st[:, 1, 0:1],
                                        func=AF.Sign), "eS")
                    mk.wait(e, en, "eV", C_V0 + ic * 7 + 5)
                    mk.sig(e.dma_start(out=opre_d.ap()[ic, :, :], in_=p2bh4[:, 0:T]), "qC", 16)

            @mk.on("vector")
            def _(e, en):
                for ic in range(IC1):
                    mk.wait(e, en, "eS", C_S0 + ic * 2 + 1)
                    c = mk.sig(e.tensor_scalar_mul(out=p2a[:, 1, :], in0=p4st[:, 0, :],
                                                   scalar1=scal[:, 2:3]), "eV")
                    mk.wait(e, en, "eV", c)
                    c = mk.sig(e.scalar_tensor_tensor(out=p2b[:, :], in0=p2a[:, 0, 0:1].broadcast_to((128, T)),
                                                      scalar=0.0, in1=p2a[:, 1, :],
                                                      op0=ALU.mult, op1=ALU.add), "eV")
                    mk.wait(e, en, "eV", c)
                    mk.wait(e, en, "eS", C_S0 + ic * 2 + 2)
                    c = mk.sig(e.tensor_scalar_mul(out=p2a[:, 1, :], in0=p4st[:, 1, :],
                                                   scalar1=scal[:, 1:2]), "eV")
                    mk.wait(e, en, "eV", c)
                    c = mk.sig(e.scalar_tensor_tensor(out=p4i[:, 0:T], in0=p2a[:, 0, 0:1].broadcast_to((128, T)),
                                                      scalar=0.0, in1=p2a[:, 1, :],
                                                      op0=ALU.mult, op1=ALU.add), "eV")
                    mk.wait(e, en, "eV", c)
                    c = mk.sig(e.scalar_tensor_tensor(out=p2bh4[:, 0:T], in0=p2b[:, :],
                                                      scalar=scal[:, 4:5], in1=p4i[:, 0:T],
                                                      op0=ALU.mult, op1=ALU.mult), "eV")
                    mk.wait(e, en, "eV", c)
                    c = mk.sig(e.reduce_max(out=scal[:, 6:7], in_=p2bh4[:, 0:T],
                                            axis=AX.X, apply_absolute_value=True), "eV")
                    mk.wait(e, en, "eV", c)
                    if ic == 0:
                        c = mk.sig(e.tensor_copy(out=maxc_sb[:, 34:35], in_=scal[:, 6:7]), "eV")
                    else:
                        c = mk.sig(e.tensor_max(out=maxc_sb[:, 34:35],
                                                in0=maxc_sb[:, 34:35], in1=scal[:, 6:7]), "eV")
                    e.wait_ge(mk.sems["eV"], c)
            mk.gate()

        emit_armax(6, maxc_sb[:, 34:35], maxc_sb[:, 33:34])
        with nc.psum_tensor("bc7_ps", [128, 1], F32) as bc7_ps:
            @mk.on("vector")
            def _(e, en):
                mk.wait(e, en, "qG")
                c = mk.sig(e.tensor_scalar(out=row16[:, 56:57], in0=row16[:, 48:49],
                                           scalar1=1.0 / 32767.0, scalar2=1e-12,
                                           op0=ALU.mult, op1=ALU.max), "eV")
                mk.wait(e, en, "eV", c)
                c = mk.sig(e.reciprocal(out=row16[:, 57:58], in_=row16[:, 56:57]), "eV")
                mk.wait(e, en, "eV", c)
            emit_bcast(row16[:, 56:57], 0, bc7_ps[:, 0:1])
            emit_bcast(row16[:, 57:58], 1, bc7_ps[:, 0:1])

        with nc.psum_tensor("d4_ps", [128, 4, 512], F32) as d4_ps, \
             nc.psum_tensor("st4_ps", [128, 2, 512], F32) as st4_ps:
            for t4 in range(T4):
                oq4S0, oq4V0 = C["eS"], C["eV"]

                @mk.on("scalar")
                def _(e, en, t4=t4, oq4V0=oq4V0):
                    for ic in range(IC1):
                        r = ic % 2
                        if ic >= 2:
                            mk.wait(e, en, "eV", oq4V0 + (ic - 1) * 3)
                        mk.sig(e.dma_start(out=w8st[:, r, 0:512],
                                           in_=opre_d.ap()[ic, :, t4 * 512:(t4 + 1) * 512]), "qC", 16)
                        mk.wait(e, en, "qC")
                        mk.sig(e.activation(out=p2a[:, 0, r * 512:r * 512 + 1],
                                            in_=w8st[:, r, 0:1], func=AF.Sign), "eS")

                @mk.on("vector")
                def _(e, en, t4=t4, oq4S0=oq4S0):
                    for ic in range(IC1):
                        r = ic % 2
                        mk.wait(e, en, "eS", oq4S0 + ic + 1)
                        c = mk.sig(e.tensor_scalar_mul(out=p2a[:, 1, r * 512:r * 512 + 512],
                                                       in0=w8st[:, r, 0:512],
                                                       scalar1=scal[:, 1:2]), "eV")
                        mk.wait(e, en, "eV", c)
                        c = mk.sig(e.scalar_tensor_tensor(out=p2b[:, r * 512:r * 512 + 512],
                                                          in0=p2a[:, 0, r * 512:r * 512 + 1].broadcast_to((128, 512)),
                                                          scalar=0.0,
                                                          in1=p2a[:, 1, r * 512:r * 512 + 512],
                                                          op0=ALU.mult, op1=ALU.add), "eV")
                        mk.wait(e, en, "eV", c)
                        c = mk.sig(e.tensor_scalar_mul(out=p4h[:, ic, :],
                                                       in0=p2b[:, r * 512:r * 512 + 512],
                                                       scalar1=scal[:, 0:1]), "eV")
                        mk.wait(e, en, "eV", c)
                HQ4_EV = C["eV"]

                d4T0, d4V0, d4QS0 = C["eT"], C["eV"], C["qS"]

                @mk.on("sync")
                def _(e, en, t4=t4, d4T0=d4T0):
                    for ocg in range(2):
                        for ic in range(IC1):
                            r = ic % 2
                            li = ocg * IC1 + ic
                            if li >= 2:
                                mk.wait(e, en, "eT", d4T0 + (li - 1))
                            else:
                                mk.wait(e, en, "eT", d4T0)
                            mk.sig(e.dma_start(out=wTd[:, r, :],
                                               in_=wq1d.ap()[ocg * 512:(ocg + 1) * 512,
                                                             ic * 128:(ic + 1) * 128],
                                               transpose=True), "qS", 16)

                @mk.on("tensor")
                def _(e, en, t4=t4, d4QS0=d4QS0, HQ4_EV=HQ4_EV, d4V0=d4V0):
                    mk.wait(e, en, "eV", HQ4_EV)
                    for ocg in range(2):
                        if ocg == 1:
                            mk.wait(e, en, "eV", d4V0 + 8)
                        for ic in range(IC1):
                            r = ic % 2
                            li = ocg * IC1 + ic
                            mk.wait(e, en, "qS", d4QS0 + (li + 1) * 16)
                            for j in range(4):
                                mm = e.matmul(d4_ps[:, j, :],
                                              wTd[:, r, j * 128:(j + 1) * 128],
                                              p4h[:, ic, :],
                                              start=(ic == 0), stop=(ic == IC1 - 1))
                            mk.sig(mm, "eT")

                @mk.on("vector")
                def _(e, en, t4=t4, d4T0=d4T0):
                    for ocg in range(2):
                        mk.wait(e, en, "eT", d4T0 + (ocg + 1) * IC1)
                        for j in range(4):
                            oc = ocg * 4 + j
                            sl = stream[:, oc, t4 * 512:(t4 + 1) * 512]
                            c = mk.sig(e.tensor_add(out=sl, in0=d4_ps[:, j, :],
                                                    in1=sl), "eV")
                            mk.wait(e, en, "eV", c)
                            mk.sig(e.tensor_copy(out=ybf[:, oc, :], in_=sl), "eV")
                DW4_EV = C["eV"]
                DW4_S0 = C["eS"]

                @mk.on("scalar")
                def _(e, en, t4=t4, d4V0=d4V0):
                    for oc in range(OC):
                        mk.wait(e, en, "eV", d4V0 + oc * 2 + 2)
                        mk.sig(e.activation(out=y2bf[:, oc, :], in_=ybf[:, oc, :],
                                            func=AF.Square), "eS")

                @mk.on("tensor")
                def _(e, en, t4=t4, d4V0=d4V0, DW4_S0=DW4_S0):
                    for oc in range(OC):
                        mk.wait(e, en, "eV", d4V0 + oc * 2 + 2)
                        mm = e.matmul(st4_ps[:, 0, :], ones_blk[:], ybf[:, oc, :],
                                      start=(oc == 0), stop=(oc == OC - 1))
                    mk.sig(mm, "eT")
                    for oc in range(OC):
                        mk.wait(e, en, "eS", DW4_S0 + oc + 1)
                        mm = e.matmul(st4_ps[:, 1, :], ones_blk[:], y2bf[:, oc, :],
                                      start=(oc == 0), stop=(oc == OC - 1))
                    mk.sig(mm, "eT")

                ln4T = C["eT"]

                @mk.on("vector")
                def _(e, en, t4=t4, ln4T=ln4T):
                    mk.wait(e, en, "eT", ln4T)
                    c1 = mk.sig(e.tensor_copy(out=scr[:, 0, :], in_=st4_ps[:, 0, :]), "eV")
                    mk.wait(e, en, "eV", c1)
                    c2 = mk.sig(e.scalar_tensor_tensor(out=scr[:, 1, :], in0=scr[:, 0, :],
                                                       scalar=1.0 / (D * D), in1=scr[:, 0, :],
                                                       op0=ALU.mult, op1=ALU.mult), "eV")
                    mk.wait(e, en, "eV", c2)
                    mk.sig(e.scalar_tensor_tensor(out=scr[:, 2, :], in0=st4_ps[:, 1, :],
                                                  scalar=1.0 / D, in1=scr[:, 1, :],
                                                  op0=ALU.mult, op1=ALU.subtract), "eV")
                LV4 = C["eV"]

                @mk.on("scalar")
                def _(e, en, t4=t4, LV4=LV4):
                    mk.wait(e, en, "eV", LV4)
                    mk.sig(e.activation(out=r_bc[:, 0, :], in_=scr[:, 2, :], func=AF.Sqrt,
                                        bias=epsln_col[:, 0:1]), "eS")
                LS4 = C["eS"]

                @mk.on("vector")
                def _(e, en, t4=t4, LS4=LS4):
                    mk.wait(e, en, "eS", LS4)
                    c1 = mk.sig(e.reciprocal(out=r_bc[:, 1, :], in_=r_bc[:, 0, :]), "eV")
                    mk.wait(e, en, "eV", c1)
                    c2 = mk.sig(e.scalar_tensor_tensor(out=scr[:, 3, :], in0=scr[:, 0, :],
                                                       scalar=1.0 / D, in1=r_bc[:, 1, :],
                                                       op0=ALU.mult, op1=ALU.mult), "eV")
                    mk.wait(e, en, "eV", c2)
                    for oc in range(OC):
                        sl = stream[:, oc, t4 * 512:(t4 + 1) * 512]
                        c3 = mk.sig(e.tensor_mul(out=scr[:, 1, :], in0=sl,
                                                 in1=r_bc[:, 1, :]), "eV")
                        mk.wait(e, en, "eV", c3)
                        c4 = mk.sig(e.tensor_sub(out=sl, in0=scr[:, 1, :],
                                                 in1=scr[:, 3, :]), "eV")
                        mk.wait(e, en, "eV", c4)
            mk.gate()

        mk.emit = True
        @mk.on("scalar")
        def _(e, en):
            for s in SEMS:
                mk.wait(e, en, s)
            for oc in range(OC):
                mk.sig(e.dma_start(out=out.ap()[oc * 128:(oc + 1) * 128, :],
                                   in_=stream[:, oc, :]), "qC", 16)

        @mk.on("scalar")
        def _(e, en):
            mk.wait(e, en, "qC")

    nc.finalize()
    return nc


def kernel(L0_attn_x, L0_attn_Wq, L0_attn_bq, L0_attn_Wk, L0_attn_bk, L0_attn_Wv,
           L0_attn_bv, L0_attn_Wo, L0_attn_bo, L0_ffn_clip_x, L0_ffn_clip_down,
           L0_ffn_w_gate, L0_ffn_w_up, L0_ffn_w_down, L1_attn_Wq, L1_attn_Wk,
           L1_attn_Wv, L1_attn_Wu, L1_attn_bu, L1_ffn_w_gate, L1_ffn_w_up,
           L1_ffn_w_down):
    x = np.ascontiguousarray(np.asarray(L0_attn_x, dtype=np.float32))
    B, N, Dm = x.shape
    tr = lambda w: np.ascontiguousarray(np.asarray(w, dtype=np.float32).T)
    col = lambda b: np.ascontiguousarray(np.asarray(b, dtype=np.float32).reshape(8, 128).T)
    row = lambda b: np.ascontiguousarray(np.asarray(b, dtype=np.float32).reshape(1, -1))
    shared = {
        "wqt0": tr(L0_attn_Wq), "wkt0": tr(L0_attn_Wk), "wvt0": tr(L0_attn_Wv),
        "wot0": tr(L0_attn_Wo),
        "bq0": col(L0_attn_bq), "bk0r": row(L0_attn_bk), "bv0r": row(L0_attn_bv),
        "bo0": col(L0_attn_bo),
        "cx0": np.asarray(L0_ffn_clip_x, dtype=np.float32).reshape(1, 1),
        "cd0": np.asarray(L0_ffn_clip_down, dtype=np.float32).reshape(1, 1),
        "wg0": np.ascontiguousarray(np.asarray(L0_ffn_w_gate, dtype=np.float32)),
        "wu0": np.ascontiguousarray(np.asarray(L0_ffn_w_up, dtype=np.float32)),
        "wd0": np.ascontiguousarray(np.asarray(L0_ffn_w_down, dtype=np.float32)),
        "wqt1": tr(L1_attn_Wq), "wkt1": tr(L1_attn_Wk), "wvt1": tr(L1_attn_Wv),
        "wut1": tr(L1_attn_Wu), "bu1": col(L1_attn_bu),
        "wg1": np.ascontiguousarray(np.asarray(L1_ffn_w_gate, dtype=np.float32)),
        "wup1": np.ascontiguousarray(np.asarray(L1_ffn_w_up, dtype=np.float32)),
        "wd1": np.ascontiguousarray(np.asarray(L1_ffn_w_down, dtype=np.float32)),
    }
    in_maps = []
    for c in range(8):
        b, half = divmod(c, 2)
        m = dict(shared)
        m["xT"] = np.ascontiguousarray(x[b, half * 2048:(half + 1) * 2048, :].T)
        in_maps.append(m)
    nc = _build(_UPTO)
    res = run_bass_kernel_spmd(nc, in_maps, core_ids=list(range(8)))
    outp = np.empty((B, N, Dm), dtype=np.float32)
    for c in range(8):
        b, half = divmod(c, 2)
        outp[b, half * 2048:(half + 1) * 2048, :] = res.results[c]["out"].T
    return outp
